# version 05
# baseline (speedup 1.0000x reference)
import math
import numpy as np
import ml_dtypes
from contextlib import ExitStack
import concourse.bass as bass
import concourse.mybir as mybir
from concourse.bass_utils import run_bass_kernel_spmd

F32 = mybir.dt.float32
BF16 = mybir.dt.bfloat16
I16 = mybir.dt.int16
I32 = mybir.dt.int32
AF = mybir.ActivationFunctionType
ALU = mybir.AluOpType
AXX = mybir.AxisListType.X
bf16 = ml_dtypes.bfloat16

D = 2048
SEQ = 4096
SEG = 512
NSEG = 8
NH = 4
HFF = 8192
NKD = D // 128
EPS_LN = 1e-5
EPS_RMS = 1.1920929e-07
THR = 0.5009765625
C1E8 = float(np.float32(bf16(1e-8)))
INV_SCALE = float(np.float32(1.0) / np.float32(math.sqrt(128.0)))
Z0 = 1.0 / 128.0
NCORES = 8


class Ctr:
    def __init__(self, name):
        self.name = name
        self.h = None
        self.n = 0

    def bump(self, k=1):
        self.n += k
        return (self, self.n)


def _build(debug=0, phases=9, cut=None, noscr=False, noexp=False):
    nc = bass.Bass("TRN2", target_bir_lowering=False)

    x_b = nc.dram_tensor("x_b", [SEQ, D], F32, kind="ExternalInput").ap()
    x_res = nc.dram_tensor("x_res", [1024, D], F32, kind="ExternalInput").ap()
    wq_d = nc.dram_tensor("wq_d", [D, 512], F32, kind="ExternalInput").ap()
    wk_d = nc.dram_tensor("wk_d", [D, 512], F32, kind="ExternalInput").ap()
    wv_d = nc.dram_tensor("wv_d", [D, 512], F32, kind="ExternalInput").ap()
    wo_d = nc.dram_tensor("wo_d", [D, D], F32, kind="ExternalInput").ap()
    betas_d = nc.dram_tensor("betas_d", [1, 512], F32, kind="ExternalInput").ap()
    wgu_d = nc.dram_tensor("wgu_d", [2048, D], F32, kind="ExternalInput").ap()
    wp_d = nc.dram_tensor("wp_d", [D, 1024], F32, kind="ExternalInput").ap()
    mask_d = nc.dram_tensor("mask_d", [SEG, SEG], BF16, kind="ExternalInput").ap()
    idn_d = nc.dram_tensor("idn_d", [128, 128], BF16, kind="ExternalInput").ap()
    idnf_d = nc.dram_tensor("idnf_d", [128, 128], F32, kind="ExternalInput").ap()
    o_rows = nc.dram_tensor("o_rows", [1024, D], F32, kind="ExternalOutput").ap()

    ag_in_gu = nc.dram_tensor("ag_in_gu", [2048, 2048], BF16, kind="Internal")
    ag_out_gu = nc.dram_tensor("ag_out_gu", [16384, 2048], BF16, kind="Internal")
    ag_in_p = nc.dram_tensor("ag_in_p", [1024, 2048], BF16, kind="Internal")
    ag_out_p = nc.dram_tensor("ag_out_p", [8192, 2048], BF16, kind="Internal")

    dbg = {}
    if debug >= 1:
        dbg["d_aggu"] = nc.dram_tensor("d_aggu", [16384, 2048], BF16, kind="ExternalOutput").ap()
        dbg["d_agp"] = nc.dram_tensor("d_agp", [8192, 2048], BF16, kind="ExternalOutput").ap()
    if debug >= 2:
        dbg["d_y"] = nc.dram_tensor("d_y", [1024, D], F32, kind="ExternalOutput").ap()
        dbg["d_mt"] = nc.dram_tensor("d_mt", [2048, 1024], BF16, kind="ExternalOutput").ap()

    EQ = {k: [] for k in ("t", "v", "c", "g", "y")}
    T = Ctr("T"); V = Ctr("V"); C = Ctr("C")
    CC = Ctr("CC"); BAR = Ctr("BAR")
    CTRS = [T, V, C, CC, BAR]

    def emit(eng, fn):
        EQ[eng].append(fn)

    def wait(eng, evt):
        if evt is None:
            return
        ctr, val = evt
        emit(eng, lambda e, c_=ctr, v_=val: e.wait_ge(c_.h, v_))

    def waits(eng, evts):
        for evt in evts:
            wait(eng, evt)

    def barrier():
        tgt = BAR.n + 5
        BAR.n = tgt
        for eng in ("t", "v", "c", "g", "y"):
            emit(eng, lambda e: e.nop().then_inc(BAR.h, 1))
            emit(eng, lambda e, v_=tgt: e.wait_ge(BAR.h, v_))

    class Stream:
        def __init__(self, name, q):
            self.q = q
            self.ctr = Ctr(name)

    def dma(stream, out, in_, prio_waits=()):
        q = stream.q
        waits(q, prio_waits)
        if stream.ctr.n:
            wait(q, (stream.ctr, stream.ctr.n))
        evt = stream.ctr.bump(16)
        emit(q, lambda e, o=out, i=in_, c_=stream.ctr:
             e.dma_start(out=o, in_=i).then_inc(c_.h, 16))
        return evt

    def op(eng, fn, inc=False, prio_waits=()):
        ctr = {"t": T, "v": V, "c": C}[eng]
        waits(eng, prio_waits)
        if eng in ("v", "c"):
            if ctr.n > 0:
                wait(eng, (ctr, ctr.n))
            evt = ctr.bump(1)
            emit(eng, lambda e, f=fn, c_=ctr: f(e).then_inc(c_.h, 1))
            return evt
        if inc:
            evt = ctr.bump(1)
            emit(eng, lambda e, f=fn, c_=ctr: f(e).then_inc(c_.h, 1))
            return evt
        emit(eng, lambda e, f=fn: f(e))
        return None

    def t_event():
        wait("t", (T, T.n))
        evt = T.bump(1)
        emit("t", lambda e: e.nop().then_inc(T.h, 1))
        return evt

    def flush(name):
        with nc.Block(name) as blk:
            if EQ["t"]:
                blk.tensor(lambda t, L=EQ["t"][:]: [fn(t) for fn in L])
            if EQ["v"]:
                blk.vector(lambda v, L=EQ["v"][:]: [fn(v) for fn in L])
            if EQ["c"]:
                blk.scalar(lambda s, L=EQ["c"][:]: [fn(s) for fn in L])
            if EQ["g"]:
                blk.gpsimd(lambda g, L=EQ["g"][:]: [fn(g) for fn in L])
            if EQ["y"]:
                blk.sync(lambda s, L=EQ["y"][:]: [fn(s) for fn in L])
        for k in EQ:
            EQ[k].clear()

    st = ExitStack()
    en = st.enter_context

    ST_C = Stream("sC", "y")
    ST_SLAB = Stream("sS", "y")
    ST_WG = Stream("sWG", "y")
    ST_WP = Stream("sWP", "y")
    ST_XR = Stream("sXR", "y")
    ST_O = Stream("sO", "y")
    ST_X = Stream("sX", "g")
    ST_QL = Stream("sQL", "g")
    ST_W = Stream("sW", "g")
    ST_WP2 = Stream("sWP2", "g")
    STREAMS = [ST_C, ST_SLAB, ST_WG, ST_WP, ST_XR, ST_O, ST_X, ST_QL, ST_W,
               ST_WP2]

    for ctr in CTRS + [x.ctr for x in STREAMS]:
        ctr.h = en(nc.semaphore(ctr.name))

    idn = en(nc.sbuf_tensor("idn", [128, 128], BF16))
    idnf = en(nc.sbuf_tensor("idnf", [128, 128], F32))
    maskT = en(nc.sbuf_tensor("maskT", [128, 4, SEG], BF16))
    beta_t = en(nc.sbuf_tensor("beta_t", [128, NH, 128], F32))
    onesf = en(nc.sbuf_tensor("onesf", [1, 128], F32))
    bsb = en(nc.sbuf_tensor("bsb", [1, 512], F32))
    zsb = en(nc.sbuf_tensor("zsb", [128, 2048], F32))
    sqs = en(nc.sbuf_tensor("sqs", [128, 1024], F32))
    red4 = en(nc.sbuf_tensor("red4", [128, 6], F32))

    ev_idn = dma(ST_C, idn[:], idn_d)
    ev_idnf = dma(ST_C, idnf[:], idnf_d)
    ev_mask = dma(ST_C, maskT.ap(), mask_d.rearrange("(m p) u -> p m u", p=128))
    ev_bsb = dma(ST_C, bsb[:], betas_d)
    op("v", lambda e_: e_.memset(onesf[:], 1.0))
    ev_ones = op("v", lambda e_: e_.nop(), inc=True)

    s0 = ExitStack()
    e0 = s0.enter_context
    qw_c = [e0(nc.sbuf_tensor(f"qwc{i}", [128, 4096], BF16)) for i in range(2)]
    qwf = e0(nc.sbuf_tensor("qwf", [128, 4096], F32))
    m1 = e0(nc.sbuf_tensor("m1", [128, 4096], F32))
    m2 = e0(nc.sbuf_tensor("m2", [128, 4096], F32))
    qsum = e0(nc.sbuf_tensor("qsum", [128, 64], F32))
    qs = e0(nc.sbuf_tensor("qs", [128, 64], BF16))
    qsf = e0(nc.sbuf_tensor("qsf", [128, 64], F32))
    qt = e0(nc.sbuf_tensor("qt", [128, 64], F32))
    qtn = e0(nc.sbuf_tensor("qtn", [128, 64], F32))
    wfin_gu = e0(nc.sbuf_tensor("wfin_gu", [128, 16, 2048], BF16))
    wfin_p = e0(nc.sbuf_tensor("wfin_p", [128, 16, 1024], BF16))
    asm = [e0(nc.sbuf_tensor(f"asm{i}", [128, 2048], BF16)) for i in range(2)]
    ptq = e0(nc.psum_tensor("ptq", [128, 512], BF16))

    qfree = [None, None]

    def quant_rows(src_dram, n_rows, n_cols, wfin_all, kblk):
        nchunk = n_rows // (128 * kblk)
        ngrp = (kblk * n_cols) // 64
        ev_free = qfree
        ev_done = None
        for ch in range(nchunk):
            par = ch % 2
            w8 = [ev_free[par]] if ev_free[par] else []
            qc = qw_c[par].ap().rearrange("p (k d) -> p k d", k=kblk)[:, :, 0:n_cols]
            ev_load = dma(
                ST_QL, qc,
                src_dram[ch * 128 * kblk:(ch + 1) * 128 * kblk, :]
                .rearrange("(k p) d -> p k d", p=128),
                prio_waits=w8)
            flat = qw_c[par].ap()
            ev_fx = op("v", lambda e_, a=flat: e_.tensor_copy(a.bitcast(I16),
                                                              a.bitcast(I16)),
                       prio_waits=[ev_load])
            ev_red = op("v", lambda e_, a=flat, ng=ngrp: e_.tensor_reduce(
                qsum.ap()[:, 0:ng].rearrange("p (g o) -> p g o", o=1),
                a.rearrange("p (g z) -> p g z", z=64)[:, 0:ng, :],
                AXX, ALU.add, apply_absolute_value=True), inc=True)
            op("v", lambda e_: e_.tensor_copy(qsum.ap().bitcast(I32),
                                              qsum.ap().bitcast(I32)),
               prio_waits=[ev_red])
            op("v", lambda e_: e_.tensor_scalar_mul(qs[:], qsum[:], 1.0 / 64.0))
            op("v", lambda e_: e_.tensor_scalar_max(qs[:], qs[:], C1E8))
            op("v", lambda e_: e_.tensor_copy(qs.ap().bitcast(I16),
                                              qs.ap().bitcast(I16)))
            op("v", lambda e_: e_.tensor_copy(qsf[:], qs[:]))
            op("v", lambda e_: e_.tensor_scalar_mul(qt[:], qsf[:], THR))
            op("v", lambda e_: e_.tensor_scalar_mul(qtn[:], qt[:], -1.0))
            ev_cast = op("c", lambda e_, a=flat: e_.activation(qwf.ap(), a, AF.Copy),
                         inc=True,
                         prio_waits=[ev_fx] + ([ev_done] if ev_done else
                                               [x for x in qfree if x]))
            ng = ngrp
            op("v", lambda e_, ng=ng: e_.tensor_tensor(
                m1.ap().rearrange("p (g z) -> p g z", z=64)[:, 0:ng, :],
                qwf.ap().rearrange("p (g z) -> p g z", z=64)[:, 0:ng, :],
                qt.ap()[:, 0:ng].rearrange("p (g o) -> p g o", o=1)
                .broadcast_to([128, ng, 64]),
                ALU.is_gt), prio_waits=[ev_cast])
            op("v", lambda e_, ng=ng: e_.tensor_tensor(
                m2.ap().rearrange("p (g z) -> p g z", z=64)[:, 0:ng, :],
                qwf.ap().rearrange("p (g z) -> p g z", z=64)[:, 0:ng, :],
                qtn.ap()[:, 0:ng].rearrange("p (g o) -> p g o", o=1)
                .broadcast_to([128, ng, 64]),
                ALU.is_lt))
            op("v", lambda e_: e_.tensor_sub(m1[:], m1[:], m2[:]))
            op("v", lambda e_, ng=ng: e_.tensor_tensor(
                m2.ap().rearrange("p (g z) -> p g z", z=64)[:, 0:ng, :],
                m1.ap().rearrange("p (g z) -> p g z", z=64)[:, 0:ng, :],
                qsf.ap()[:, 0:ng].rearrange("p (g o) -> p g o", o=1)
                .broadcast_to([128, ng, 64]),
                ALU.mult))
            ws = wfin_all.ap()[:, ch * kblk:(ch + 1) * kblk, 0:n_cols] \
                .rearrange("p k d -> p (k d)")
            op("v", lambda e_, ws=ws, a=flat: e_.tensor_sub(ws, m2.ap(), a))
            op("v", lambda e_, ws=ws, a=flat: e_.tensor_add(ws, a, ws))
            op("v", lambda e_, ws=ws: e_.tensor_copy(ws.bitcast(I16),
                                                     ws.bitcast(I16)))
            ev_free[par] = op("v", lambda e_: e_.nop(), inc=True)
            ev_done = ev_free[par]
        return ev_done

    def drain_all():
        for s_ in STREAMS:
            if s_.ctr.n:
                emit(s_.q, lambda e_, c_=s_.ctr: e_.wait_ge(c_.h, c_.n))

    def cutout(tag):
        drain_all()
        flush("cut_" + tag)
        s0.close()
        st.close()
        return nc

    if cut == "none":
        return cutout("none")
    ev_qgu = quant_rows(wgu_d, 2048, 2048, wfin_gu, kblk=2)
    if cut == "qgu":
        if debug >= 1:
            d_wfin = nc.dram_tensor("d_wfin", [2048, 2048], BF16,
                                    kind="ExternalOutput").ap()
            dma(ST_C, d_wfin.rearrange("(k p) d -> p k d", p=128),
                wfin_gu.ap(), prio_waits=[ev_qgu])
        return cutout("qgu")
    ev_qp = quant_rows(wp_d, D, 1024, wfin_p, kblk=4)
    if cut == "qp":
        return cutout("qp")

    ev_asm_free = [None, None]
    ev_slab = []
    ev_ptq_free = None

    def slab(src, n_blocks, n_out_rows, colmap, ag_in, ev_dep):
        nonlocal ev_ptq_free
        for kd in range(n_out_rows):
            par = kd % 2
            for Hq in range((n_blocks + 3) // 4):
                lastt = None
                for hj in range(4):
                    H = Hq * 4 + hj
                    if H >= n_blocks:
                        break
                    w8 = []
                    if hj == 0:
                        if Hq == 0:
                            w8 = [ev_dep, ev_idn]
                        if ev_ptq_free:
                            w8.append(ev_ptq_free)
                    lastt = op("t", lambda e_, H=H, kd=kd, hj=hj: e_.transpose(
                        ptq[:, 128 * hj:128 * (hj + 1)],
                        src(H, kd), idn[:]),
                        inc=(hj == 3 or H == n_blocks - 1), prio_waits=w8)
                for hj in range(4):
                    H = Hq * 4 + hj
                    if H >= n_blocks:
                        break
                    cb = colmap(H)
                    w8 = [lastt] if hj == 0 else []
                    if hj == 0 and ev_asm_free[par] and Hq == 0:
                        w8.append(ev_asm_free[par])
                    ev_ptq_free = op("v", lambda e_, hj=hj, cb=cb, par=par:
                                     e_.tensor_copy(asm[par][:, cb:cb + 128],
                                                    ptq[:, 128 * hj:128 * (hj + 1)]),
                                     inc=(hj == 3 or H == n_blocks - 1),
                                     prio_waits=w8)
            ev_ev = ev_ptq_free
            ev_st = dma(ST_SLAB, ag_in.ap()[128 * kd:128 * (kd + 1), :],
                        asm[par][:], prio_waits=[ev_ev])
            ev_asm_free[par] = ev_st
            ev_slab.append(ev_st)

    slab(lambda H, kd: wfin_gu.ap()[:, H, 128 * kd:128 * (kd + 1)],
         16, 16, lambda H: 256 * (H % 8) + 128 * (H // 8), ag_in_gu, ev_qgu)
    if cut == "sgu":
        return cutout("sgu")
    slab(lambda H, kd: wfin_p.ap()[:, H, 128 * kd:128 * (kd + 1)],
         16, 8, lambda H: 128 * H, ag_in_p, ev_qp)
    if cut == "sp":
        return cutout("sp")

    if phases <= 0:
        drain_all()
        flush("phase0cut")
        s0.close()
        st.close()
        return nc

    waits("g", ev_slab)
    ev_ag_gu = CC.bump(1)
    emit("g", lambda e_: e_.collective_compute(
        "AllGather", ALU.bypass, replica_groups=[list(range(NCORES))],
        ins=[ag_in_gu.ap().opt()], outs=[ag_out_gu.ap().opt()]).then_inc(CC.h, 1))
    wait("g", ev_ag_gu)
    ev_ag_p = CC.bump(1)
    emit("g", lambda e_: e_.collective_compute(
        "AllGather", ALU.bypass, replica_groups=[list(range(NCORES))],
        ins=[ag_in_p.ap().opt()], outs=[ag_out_p.ap().opt()]).then_inc(CC.h, 1))

    if debug >= 1:
        evp = None
        for blk in range(128):
            evl = dma(ST_C, asm[0][:], ag_out_gu.ap()[128 * blk:128 * (blk + 1), :],
                      prio_waits=[ev_ag_gu] + ([evp] if evp else []))
            evp = dma(ST_C, dbg["d_aggu"][128 * blk:128 * (blk + 1), :], asm[0][:],
                      prio_waits=[evl])
        for blk in range(64):
            evl = dma(ST_C, asm[0][:], ag_out_p.ap()[128 * blk:128 * (blk + 1), :],
                      prio_waits=[ev_ag_p, evp])
            evp = dma(ST_C, dbg["d_agp"][128 * blk:128 * (blk + 1), :], asm[0][:],
                      prio_waits=[evl])

    barrier()
    if phases <= 1:
        drain_all()
    flush("phase0")
    s0.close()
    if phases <= 1:
        st.close()
        return nc

    sMT = ExitStack()
    mt_sb = sMT.enter_context(nc.sbuf_tensor("mt_sb", [128, NKD, 1024], BF16))

    sA = ExitStack()
    eA = sA.enter_context
    wq_sb = eA(nc.sbuf_tensor("wq_sb", [128, NKD, 512], BF16))
    wk_sb = eA(nc.sbuf_tensor("wk_sb", [128, NKD, 512], BF16))
    wv_sb = eA(nc.sbuf_tensor("wv_sb", [128, NKD, 512], BF16))
    memz = eA(nc.sbuf_tensor("memz", [128, NH, 129], BF16))
    mem_f = eA(nc.sbuf_tensor("mem_f", [128, NH, 128], F32))
    z_f = eA(nc.sbuf_tensor("z_f", [128, NH], F32))
    xT = eA(nc.sbuf_tensor("xT", [128, NKD, 512], BF16))
    xseg = [eA(nc.sbuf_tensor(f"xseg{i}", [128, 4, 2048], BF16)) for i in range(2)]
    NPAR = 3
    qT_sb = [eA(nc.sbuf_tensor(f"qT{i}", [128, 512], BF16)) for i in range(NPAR)]
    kT_sb = [eA(nc.sbuf_tensor(f"kT{i}", [128, 512], BF16)) for i in range(NPAR)]
    sqT = [eA(nc.sbuf_tensor(f"sqT{i}", [128, 512], BF16)) for i in range(NPAR)]
    skT = [eA(nc.sbuf_tensor(f"skT{i}", [128, 512], BF16)) for i in range(NPAR)]
    skn = [eA(nc.sbuf_tensor(f"skn{i}", [128, 4, 128], BF16)) for i in range(NPAR)]
    tmpA = [eA(nc.sbuf_tensor("tmpA0", [128, 512], F32))] * NPAR
    tmpB = [eA(nc.sbuf_tensor("tmpB0", [128, 512], F32))] * NPAR
    etmp = [eA(nc.sbuf_tensor(f"etmp{i}", [128, 512], F32)) for i in range(2)]
    v_sb = [eA(nc.sbuf_tensor(f"v_sb{i}", [128, 4, 4 * 129], BF16))
            for i in range(2)]
    E_sb = [eA(nc.sbuf_tensor(f"E{i}", [128, 4, 512], BF16)) for i in range(NPAR)]
    att_d = [eA(nc.sbuf_tensor("attd0", [128, 128], F32))] * NPAR
    diff = [eA(nc.sbuf_tensor("diff0", [128, 128], F32))] * NPAR
    att_bf = [eA(nc.sbuf_tensor(f"attbf{i}", [128, 4, 128], BF16)) for i in range(NPAR)]
    delta = [eA(nc.sbuf_tensor(f"delta{i}", [128, 4, 128], BF16)) for i in range(NPAR)]
    rcol = [eA(nc.sbuf_tensor("rcol0", [128, 4], F32))] * NPAR
    pq = [eA(nc.psum_tensor(f"pq{i}", [128, 512], F32)) for i in range(2)]
    pk = [eA(nc.psum_tensor(f"pk{i}", [128, 512], F32)) for i in range(2)]
    psc = eA(nc.psum_tensor("psc", [128, 1024], F32))
    pnd = eA(nc.psum_tensor("pnd", [128, 512], F32))
    ptp = eA(nc.psum_tensor("ptpA", [128, 512], BF16))

    ev_wq = dma(ST_W, wq_sb.ap(), wq_d.rearrange("(k p) c -> p k c", p=128))
    ev_wk = dma(ST_W, wk_sb.ap(), wk_d.rearrange("(k p) c -> p k c", p=128))
    ev_wv = dma(ST_W, wv_sb.ap(), wv_d.rearrange("(k p) c -> p k c", p=128))

    op("v", lambda e_: e_.memset(mem_f.ap().rearrange("p h d -> p (h d)"), 0.0))
    op("v", lambda e_: e_.memset(z_f[:], Z0))
    op("v", lambda e_: e_.memset(memz.ap().rearrange("p h d -> p (h d)"), 0.0))
    for hh in range(NH):
        op("v", lambda e_, hh=hh: e_.memset(memz.ap()[:, hh, 128:129], Z0))
    for vpar in range(2):
        op("v", lambda e_, vpar=vpar:
           e_.memset(v_sb[vpar].ap().rearrange("p m c -> p (m c)"), 0.0))
        for m in range(4):
            for hh in range(NH):
                op("v", lambda e_, m=m, hh=hh, vpar=vpar:
                   e_.memset(v_sb[vpar].ap()[:, m, 129 * hh + 128:129 * hh + 129], 1.0))
    ev_vinit = op("v", lambda e_: e_.nop(), inc=True)

    for hh in range(NH):
        evb = op("t", lambda e_, hh=hh: e_.matmul(
            pq[0][:, 0:128], onesf[0:1, :], bsb[0:1, 128 * hh:128 * (hh + 1)],
            start=True, stop=True), inc=True,
            prio_waits=[ev_bsb, ev_ones] if hh == 0 else ())
        evs = op("c", lambda e_, hh=hh: e_.activation(
            beta_t.ap()[:, hh, :], pq[0][:, 0:128], AF.Sigmoid),
            inc=True, prio_waits=[evb])
        wait("t", evs)

    if cut == "beta":
        drain_all()
        flush("cutbeta")
        sA.close()
        sMT.close()
        st.close()
        return nc

    if noscr:
        scr = lambda apx: apx
    else:
        scr = lambda apx: apx.rearrange("p (t r) -> p r t", r=16)

    ev_xld = [None, None]
    ev_xT_users = None
    ev_psc_free = [None, None]
    ev_etmp_rd = [None, None]
    ev_pq_free = [None, None]
    ev_pk_free = [None, None]
    ev_pnd_free = None
    ev_ptp_free = None
    ev_mt_done = []
    ev_state = [None] * NH
    ev_par_free = [None] * NPAR
    H_evq = {}
    H_evq1 = {}
    H_evk1 = {}
    H_evsk = {}
    H_evskn = {}
    H_evE = {}
    H_evatt = {}

    ev_xld[0] = dma(ST_X, xseg[0].ap(),
                    x_b[0:SEG, :].rearrange("(m p) d -> p m d", p=128))

    ev_xseg_rd = [None, None]

    def st_xtp(i, par):
        nonlocal ev_ptp_free
        last_ev = None
        lastt = None
        for kd in range(NKD):
            for m in range(4):
                w8 = []
                if m == 0:
                    if kd == 0:
                        w8 = [ev_xld[par], ev_idn]
                        if ev_xT_users:
                            w8.append(ev_xT_users)
                    if ev_ptp_free:
                        w8.append(ev_ptp_free)
                lastt = op("t", lambda e_, m=m, kd=kd, par=par: e_.transpose(
                    ptp[:, 128 * m:128 * (m + 1)],
                    xseg[par].ap()[:, m, 128 * kd:128 * (kd + 1)], idn[:]),
                    inc=(m == 3), prio_waits=w8)
            last_ev = op("c", lambda e_, kd=kd: e_.activation(
                xT.ap()[:, kd, :], ptp.ap(), AF.Copy),
                inc=True, prio_waits=[lastt])
            ev_ptp_free = last_ev
        ev_xseg_rd[par] = lastt
        return last_ev

    ev_vseg_free = [None, None]

    def st_v(i, ev_xT_ready):
        vpar = i % 2
        ev = None
        for m in range(4):
            half = m % 2
            lastv = None
            for kd in range(NKD):
                w8 = []
                if kd == 0:
                    w8 = [ev_xT_ready, ev_wv, ev_vinit]
                    if ev_psc_free[half]:
                        w8.append(ev_psc_free[half])
                lastv = op("t", lambda e_, m=m, kd=kd, half=half: e_.matmul(
                    psc[:, 512 * half:512 * (half + 1)],
                    xT.ap()[:, kd, 128 * m:128 * (m + 1)],
                    wv_sb.ap()[:, kd, :],
                    start=(kd == 0), stop=(kd == NKD - 1)),
                    inc=(kd == NKD - 1), prio_waits=w8)
            w8e = [lastv]
            if m == 0 and ev_vseg_free[vpar]:
                w8e += ev_vseg_free[vpar]
            ev = op("c", lambda e_, m=m, half=half, vpar=vpar: e_.activation(
                v_sb[vpar].ap()[:, m, :]
                .rearrange("p (h c) -> p h c", c=129)[:, :, 0:128],
                psc.ap()[:, 512 * half:512 * (half + 1)]
                .rearrange("p (h c) -> p h c", c=128), AF.Copy),
                inc=True, prio_waits=w8e)
            ev_psc_free[half] = ev
        return ev

    def st_qk(i, gh, ev_xT_ready):
        hh = gh % 4
        par = gh % 2
        p3 = gh % NPAR
        w8 = [ev_xT_ready, ev_wq]
        if ev_pq_free[par]:
            w8.append(ev_pq_free[par])
        lastq = None
        for kd in range(NKD):
            lastq = op("t", lambda e_, hh=hh, kd=kd, par=par: e_.matmul(
                pq[par][:], wq_sb.ap()[:, kd, 128 * hh:128 * (hh + 1)],
                xT.ap()[:, kd, :],
                start=(kd == 0), stop=(kd == NKD - 1)),
                inc=(kd == NKD - 1), prio_waits=w8 if kd == 0 else ())
        w8 = [ev_wk]
        if ev_pk_free[par]:
            w8.append(ev_pk_free[par])
        lastk = None
        for kd in range(NKD):
            lastk = op("t", lambda e_, hh=hh, kd=kd, par=par: e_.matmul(
                pk[par][:], wk_sb.ap()[:, kd, 128 * hh:128 * (hh + 1)],
                xT.ap()[:, kd, :],
                start=(kd == 0), stop=(kd == NKD - 1)),
                inc=(kd == NKD - 1), prio_waits=w8 if kd == 0 else ())
        tA, tB = tmpA[p3], tmpB[p3]
        wfree = [ev_par_free[p3]] if ev_par_free[p3] else []
        ev_q1 = op("c", lambda e_, p3=p3, par=par: e_.activation(
            qT_sb[p3][:], scr(pq[par].ap()), AF.Copy), inc=True,
            prio_waits=[lastq] + wfree)
        ev_q2 = op("c", lambda e_, par=par: e_.activation(
            tB[:], scr(pq[par].ap()), AF.Relu), inc=True,
            prio_waits=[H_evsk[gh - 1]] if (gh - 1) in H_evsk else ())
        ev_mnq = op("v", lambda e_, par=par: e_.tensor_scalar_min(
            tA[:], scr(pq[par].ap()), 0.0), prio_waits=[lastq, ev_q2] + wfree)
        ev_q3 = op("c", lambda e_: e_.activation(
            tA[:], tA[:], AF.Exp), inc=True, prio_waits=[ev_mnq])
        ev_pq_free[par] = ev_q3
        ev_sq = op("v", lambda e_, p3=p3: e_.tensor_add(
            sqT[p3][:], tA[:], tB[:]), inc=True, prio_waits=[ev_q2, ev_q3])
        H_evq[gh] = ev_sq
        H_evq1[gh] = ev_q1
        ev_k1 = op("c", lambda e_, p3=p3, par=par: e_.activation(
            kT_sb[p3][:], pk[par].ap(), AF.Copy), inc=True, prio_waits=[lastk])
        ev_k2 = op("c", lambda e_, par=par: e_.activation(
            tB[:], pk[par].ap(), AF.Relu), inc=True, prio_waits=[ev_sq])
        ev_mnk = op("v", lambda e_, par=par: e_.tensor_scalar_min(
            tA[:], pk[par].ap(), 0.0), prio_waits=[lastk, ev_k2])
        ev_k3 = op("c", lambda e_: e_.activation(
            tA[:], tA[:], AF.Exp), inc=True, prio_waits=[ev_mnk])
        ev_pk_free[par] = ev_k3
        ev_sk = op("v", lambda e_, p3=p3: e_.tensor_add(
            skT[p3][:], tA[:], tB[:]), inc=True, prio_waits=[ev_k2, ev_k3])
        H_evsk[gh] = ev_sk
        H_evk1[gh] = ev_k1

    def st_sktp_sc(i, gh):
        nonlocal ev_ptp_free
        p3 = gh % NPAR
        wfree = [ev_par_free[p3]] if ev_par_free[p3] else []
        lastt = None
        for mt in range(4):
            w8 = []
            if mt == 0:
                w8 = [H_evsk[gh]]
                if ev_ptp_free:
                    w8.append(ev_ptp_free)
            lastt = op("t", lambda e_, mt=mt, p3=p3: e_.transpose(
                ptp[:, 128 * mt:128 * (mt + 1)],
                skT[p3].ap()[:, 128 * mt:128 * (mt + 1)], idn[:]),
                inc=(mt == 3), prio_waits=w8)
        ev_ptp_free = op("c", lambda e_, p3=p3: e_.activation(
            skn[p3].ap().rearrange("p m c -> p (m c)"), ptp.ap(), AF.Copy),
            inc=True, prio_waits=[lastt] + wfree)
        H_evskn[gh] = ev_ptp_free
        evE = []
        lastsc = None
        for mt in range(4):
            half = mt % 2
            w8 = [H_evq1[gh], H_evk1[gh]] if mt == 0 else []
            if ev_psc_free[half]:
                w8.append(ev_psc_free[half])
            lastsc = op("t", lambda e_, mt=mt, p3=p3, half=half: e_.matmul(
                psc[:, 512 * half:512 * (half + 1)],
                kT_sb[p3][:, 128 * mt:128 * (mt + 1)], qT_sb[p3][:],
                start=True, stop=True), inc=True, prio_waits=w8)
            buf = etmp[half]
            w8b = [lastsc, ev_mask]
            if ev_etmp_rd[half]:
                w8b.append(ev_etmp_rd[half])
            evx = op("c", lambda e_, mt=mt, buf=buf, half=half: e_.activation(
                buf[:], psc.ap()[:, 512 * half:512 * (half + 1)],
                AF.Exp, scale=INV_SCALE), inc=True, prio_waits=w8b)
            ev_psc_free[half] = evx
            w8c = list(wfree) if mt == 0 else []
            evm = op("v", lambda e_, mt=mt, buf=buf, p3=p3: e_.tensor_mul(
                E_sb[p3].ap()[:, mt, :], buf[:], maskT.ap()[:, mt, :]),
                inc=True, prio_waits=[evx] + w8c)
            ev_etmp_rd[half] = evm
            evE.append(evm)
        H_evE[gh] = evE

    def st_nd(i, gh):
        nonlocal ev_pnd_free
        hh = gh % 4
        p3 = gh % NPAR
        rc = rcol[p3]
        ad = att_d[p3]
        df = diff[p3]
        ev_cmb_all = None
        for m in range(4):
            w8 = [H_evE[gh][-1], ev_v_ready[i]]
            if ev_pnd_free:
                w8.append(ev_pnd_free)
            for mt in range(4):
                op("t", lambda e_, m=m, mt=mt, hh=hh, p3=p3, vp=i % 2: e_.matmul(
                    pnd[:, 0:129],
                    E_sb[p3].ap()[:, mt, 128 * m:128 * (m + 1)],
                    v_sb[vp].ap()[:, mt, 129 * hh:129 * (hh + 1)],
                    start=(mt == 0), stop=(mt == 3)),
                    prio_waits=w8 if mt == 0 else ())
            w8n = [H_evq[gh]]
            if ev_state[hh]:
                w8n.append(ev_state[hh])
            ev_nd = op("t", lambda e_, m=m, hh=hh, p3=p3: e_.matmul(
                pnd[:, 256:385],
                sqT[p3][:, 128 * m:128 * (m + 1)], memz.ap()[:, hh, :],
                start=True, stop=True), inc=True, prio_waits=w8n)
            ev_r1 = op("v", lambda e_, rc=rc: e_.reciprocal(
                rc[:, 0:1], pnd[:, 128:129]), inc=True, prio_waits=[ev_nd])
            ev_r2 = op("v", lambda e_, rc=rc: e_.reciprocal(
                rc[:, 1:2], pnd[:, 384:385]), inc=True)
            op("v", lambda e_, rc=rc, ad=ad: e_.tensor_scalar_mul(
                ad[:], pnd[:, 0:128], rc[:, 0:1]), prio_waits=[ev_r1, ev_r2])
            op("v", lambda e_, rc=rc, ad=ad, df=df: e_.scalar_tensor_tensor(
                df[:], pnd[:, 256:384], rc[:, 1:2], ad[:],
                ALU.mult, ALU.subtract))
            op("v", lambda e_, hh=hh, df=df: e_.tensor_mul(
                df[:], df[:], beta_t.ap()[:, hh, :]))
            ev_cmb = op("v", lambda e_, m=m, p3=p3, ad=ad, df=df: e_.tensor_add(
                att_bf[p3].ap()[:, m, :], ad[:], df[:]), inc=True)
            ev_cmb_all = ev_cmb
            ev_ak = op("t", lambda e_, m=m, hh=hh, p3=p3: e_.matmul(
                pnd[:, 256:385],
                skT[p3][:, 128 * m:128 * (m + 1)], memz.ap()[:, hh, :],
                start=True, stop=True), inc=True,
                prio_waits=[H_evsk[gh], ev_cmb])
            ev_r3 = op("v", lambda e_, rc=rc: e_.reciprocal(
                rc[:, 2:3], pnd[:, 384:385]), inc=True, prio_waits=[ev_ak])
            ev_dl = op("v", lambda e_, m=m, hh=hh, p3=p3, rc=rc, vp=i % 2:
                       e_.scalar_tensor_tensor(
                delta[p3].ap()[:, m, :], pnd[:, 256:384], rc[:, 2:3],
                v_sb[vp].ap()[:, m, 129 * hh:129 * hh + 128],
                ALU.mult, ALU.subtract), inc=True, prio_waits=[ev_r3])
            ev_pnd_free = ev_dl
        H_evatt[gh] = ev_cmb_all
        lastdm = None
        for mt in range(4):
            w8 = [H_evskn[gh], ev_pnd_free] if mt == 0 else []
            lastdm = op("t", lambda e_, mt=mt, p3=p3: e_.matmul(
                pnd[:, 0:128], skn[p3].ap()[:, mt, :],
                delta[p3].ap()[:, mt, :],
                start=(mt == 0), stop=(mt == 3)),
                inc=(mt == 3), prio_waits=w8)
        op("v", lambda e_, hh=hh: e_.tensor_sub(
            mem_f.ap()[:, hh, :], mem_f.ap()[:, hh, :], pnd[:, 0:128]),
            prio_waits=[lastdm])
        ev_mcp = op("v", lambda e_, hh=hh: e_.tensor_copy(
            memz.ap()[:, hh, 0:128], mem_f.ap()[:, hh, :]), inc=True)
        ev_pnd_free = ev_mcp
        ev_zr = op("v", lambda e_, p3=p3, rc=rc: e_.tensor_reduce(
            rc.ap()[:, 3:4], skT[p3].ap(), AXX, ALU.add),
            inc=True, prio_waits=[H_evsk[gh]])
        op("v", lambda e_, hh=hh, rc=rc: e_.tensor_add(
            z_f.ap()[:, hh:hh + 1], z_f.ap()[:, hh:hh + 1], rc[:, 3:4]),
            prio_waits=[ev_zr])
        ev_state[hh] = op("v", lambda e_, hh=hh: e_.tensor_copy(
            memz.ap()[:, hh, 128:129], z_f.ap()[:, hh:hh + 1]), inc=True)
        return lastdm

    def st_atttp(i, gh, ev_dmdone):
        nonlocal ev_ptp_free
        hh = gh % 4
        p3 = gh % NPAR
        lastat = None
        for m in range(4):
            w8 = []
            if m == 0:
                w8 = [H_evatt[gh]]
                if ev_ptp_free:
                    w8.append(ev_ptp_free)
            lastat = op("t", lambda e_, m=m, p3=p3: e_.transpose(
                ptp[:, 128 * m:128 * (m + 1)], att_bf[p3].ap()[:, m, :],
                idn[:]), inc=(m == 3), prio_waits=w8)
        ev_par_free[p3] = lastat
        ev_mtc = op("c", lambda e_, hh=hh, i=i: e_.activation(
            mt_sb.ap()[:, :, 128 * i + 32 * hh:128 * i + 32 * hh + 32]
            .rearrange("p (m r) t -> p m r t", m=4),
            ptp.ap().rearrange("p (m r t) -> p m r t", m=4, r=4), AF.Copy),
            inc=True, prio_waits=[lastat])
        ev_ptp_free = ev_mtc
        return ev_mtc

    ev_v_ready = {}
    for i in range(NSEG):
        par = i % 2
        if i + 1 < NSEG:
            ev_xld[1 - par] = dma(
                ST_X, xseg[1 - par].ap(),
                x_b[SEG * (i + 1):SEG * (i + 2), :].rearrange("(m p) d -> p m d", p=128),
                prio_waits=[ev_xseg_rd[1 - par]] if ev_xseg_rd[1 - par] else ())
        if i == 0:
            ev_xT_ready = st_xtp(0, 0)
            ev_v_ready[0] = st_v(0, ev_xT_ready)
        g0 = 4 * i
        st_qk(i, g0 + 0, ev_xT_ready)
        st_qk(i, g0 + 1, ev_xT_ready)
        st_sktp_sc(i, g0 + 0)
        st_qk(i, g0 + 2, ev_xT_ready)
        dm0 = st_nd(i, g0 + 0)
        st_atttp(i, g0 + 0, dm0)
        st_sktp_sc(i, g0 + 1)
        st_qk(i, g0 + 3, ev_xT_ready)
        dm1 = st_nd(i, g0 + 1)
        st_atttp(i, g0 + 1, dm1)
        st_sktp_sc(i, g0 + 2)
        dm2 = st_nd(i, g0 + 2)
        st_atttp(i, g0 + 2, dm2)
        st_sktp_sc(i, g0 + 3)
        if i + 1 < NSEG:
            ev_xT_users = t_event()
            ev_xT_ready = st_xtp(i + 1, 1 - par)
            ev_v_ready[i + 1] = st_v(i + 1, ev_xT_ready)
        dm3 = st_nd(i, g0 + 3)
        ev_vseg_free[i % 2] = [dm3, ev_state[3]]
        ev_mt3 = st_atttp(i, g0 + 3, dm3)
        ev_mt_done.append(ev_mt3)
        if cut == f"seg{i}":
            drain_all()
            flush("cutseg")
            sA.close()
            sMT.close()
            st.close()
            return nc

    if debug >= 2:
        evp = None
        for r in range(NKD):
            evp = dma(ST_C, dbg["d_mt"][128 * r:128 * (r + 1), :],
                      mt_sb.ap()[:, r, :],
                      prio_waits=[ev_mt_done[-1]] + ([evp] if evp else []))
    barrier()
    if phases <= 2:
        drain_all()
    flush("phaseA")
    sA.close()
    if phases <= 2:
        sMT.close()
        st.close()
        return nc

    y_sb = en(nc.sbuf_tensor("y_sb", [128, NSEG, 2048], F32, side="right"))
    sA2 = ExitStack()
    e2 = sA2.enter_context
    wo_sb = e2(nc.sbuf_tensor("wo_sb", [128, NKD, 2048], BF16))
    xres_b = [e2(nc.sbuf_tensor(f"xres{i}", [128, 2048], F32)) for i in range(2)]
    pwo = e2(nc.psum_tensor("pwo", [128, 2048], F32))

    ev_wo = None
    for part in range(4):
        ev_wo = dma(ST_W, wo_sb.ap()[:, 4 * part:4 * (part + 1), :],
                    wo_d[512 * part:512 * (part + 1), :]
                    .rearrange("(k p) c -> p k c", p=128))

    ev_xres = [None, None]
    ev_xres_free = [None, None]
    ev_pwo_free = None
    ev_zsb_free = None

    ev_xres[0] = dma(ST_XR, xres_b[0][:], x_res[0:128, :])
    for i in range(NSEG):
        par = i % 2
        if i + 1 < NSEG:
            ev_xres[1 - par] = dma(
                ST_XR, xres_b[1 - par][:], x_res[128 * (i + 1):128 * (i + 2), :],
                prio_waits=[ev_xres_free[1 - par]] if ev_xres_free[1 - par] else ())
        lastw = None
        for rch in range(NKD):
            for nb in range(4):
                w8 = []
                if rch == 0 and nb == 0:
                    w8 = [ev_mt_done[i], ev_wo]
                    if ev_pwo_free:
                        w8.append(ev_pwo_free)
                lastw = op("t", lambda e_, i=i, rch=rch, nb=nb: e_.matmul(
                    pwo[:, 512 * nb:512 * (nb + 1)],
                    mt_sb.ap()[:, rch, 128 * i:128 * (i + 1)],
                    wo_sb.ap()[:, rch, 512 * nb:512 * (nb + 1)],
                    start=(rch == 0), stop=(rch == NKD - 1)),
                    inc=(rch == NKD - 1 and nb == 3), prio_waits=w8)
        w8 = [lastw, ev_xres[par]]
        if ev_zsb_free:
            w8.append(ev_zsb_free)
        op("v", lambda e_, par=par: e_.tensor_add(zsb[:], pwo.ap(), xres_b[par][:]),
           prio_waits=w8)
        ev_z = op("v", lambda e_: e_.nop(), inc=True)
        ev_pwo_free = ev_z
        ev_xres_free[par] = ev_z
        ev_sum = op("v", lambda e_: e_.tensor_reduce(
            red4.ap()[:, 0:1], zsb.ap(), AXX, ALU.add), inc=True, prio_waits=[ev_z])
        op("c", lambda e_: e_.activation(
            sqs[:], zsb[:, 0:1024], AF.Square, accum_out=red4.ap()[:, 1:2]),
            prio_waits=[ev_z])
        ev_sq2 = op("c", lambda e_: e_.activation(
            sqs[:], zsb[:, 1024:2048], AF.Square, accum_out=red4.ap()[:, 2:3]),
            inc=True)
        op("v", lambda e_: e_.tensor_scalar_mul(red4[:, 0:1], red4[:, 0:1],
                                                -1.0 / 2048.0), prio_waits=[ev_sum])
        op("v", lambda e_: e_.tensor_add(red4[:, 1:2], red4[:, 1:2],
                                         red4[:, 2:3]), prio_waits=[ev_sq2])
        op("v", lambda e_: e_.tensor_scalar_mul(red4[:, 1:2], red4[:, 1:2],
                                                1.0 / 2048.0))
        op("v", lambda e_: e_.tensor_tensor(red4[:, 2:3], red4[:, 0:1],
                                            red4[:, 0:1], ALU.mult))
        op("v", lambda e_: e_.tensor_sub(red4[:, 1:2], red4[:, 1:2], red4[:, 2:3]))
        op("v", lambda e_: e_.tensor_scalar_add(red4[:, 1:2], red4[:, 1:2], EPS_LN))
        ev_rc = op("v", lambda e_: e_.reciprocal(red4[:, 2:3], red4[:, 1:2]),
                   inc=True)
        ev_st2 = op("c", lambda e_: e_.activation(red4.ap()[:, 3:4],
                                                  red4.ap()[:, 2:3], AF.Sqrt),
                    inc=True, prio_waits=[ev_rc])
        ev_y = op("v", lambda e_, i=i: e_.tensor_scalar(
            y_sb.ap()[:, i, :], zsb[:], red4.ap()[:, 0:1], red4.ap()[:, 3:4],
            ALU.add, ALU.mult), inc=True, prio_waits=[ev_st2])
        ev_zsb_free = ev_y

    if debug >= 2:
        evp = None
        for i in range(NSEG):
            evp = dma(ST_C, dbg["d_y"][128 * i:128 * (i + 1), :],
                      y_sb.ap()[:, i, :],
                      prio_waits=[ev_zsb_free] + ([evp] if evp else []))

    barrier()
    if phases <= 3:
        drain_all()
    flush("phaseA2")
    sA2.close()
    sMT.close()
    if phases <= 3:
        st.close()
        return nc

    sB = ExitStack()
    eB = sB.enter_context
    yTh = eB(nc.sbuf_tensor("yTh", [128, NKD, 512], BF16))
    ybc = eB(nc.sbuf_tensor("ybc", [128, 2048], BF16))
    slu = [eB(nc.sbuf_tensor(f"slu{i}", [128, 512], BF16)) for i in range(2)]
    h_sb = eB(nc.sbuf_tensor("h_sb", [128, 64, 512], BF16))
    wg_buf = [eB(nc.sbuf_tensor(f"wg{i}", [128, NKD, 256], BF16)) for i in range(2)]
    wp_buf = [eB(nc.sbuf_tensor(f"wpb{i}", [128, 2, 2048], BF16)) for i in range(2)]
    inv4 = eB(nc.sbuf_tensor("inv4", [128, 8], F32))
    g1s = eB(nc.sbuf_tensor("g1s", [128, 128], F32))

    ev_ybc_free = None
    ev_ptB_free = None
    ev_wg_free = [None, None]
    ev_wp_free = [None, None]
    ev_pg_free = [None, None, None]
    ev_slu_rd = [None, None]
    ev_yT = None
    ev_h_done = None
    ev_inv = None

    def phaseB_batch(bb):
        nonlocal ev_ybc_free, ev_ptB_free, ev_yT, ev_h_done, ev_inv, ev_zsb_free
        with (nc.psum_tensor(f"ptB{bb}", [128, 512], BF16) as ptB,
              nc.psum_tensor(f"pg0{bb}", [128, 512], F32) as pg0,
              nc.psum_tensor(f"pg1{bb}", [128, 512], F32) as pg1,
              nc.psum_tensor(f"pg2{bb}", [128, 512], F32) as pg2,
              nc.psum_tensor(f"pu0{bb}", [128, 512], F32) as pu0,
              nc.psum_tensor(f"pu1{bb}", [128, 512], F32) as pu1,
              nc.psum_tensor(f"pu2{bb}", [128, 512], F32) as pu2,
              nc.psum_tensor(f"pG{bb}", [128, 128], F32) as pG):
            pg = [pg0, pg1, pg2]
            pu = [pu0, pu1, pu2]
            for m in range(4):
                i = 4 * bb + m
                w8 = [ev_ybc_free] if ev_ybc_free else []
                ev_cast = op("c", lambda e_, i=i: e_.activation(
                    ybc[:], y_sb.ap()[:, i, :], AF.Copy), inc=True, prio_waits=w8)
                for kq in range(4):
                    lastt = None
                    for kj in range(4):
                        kd = 4 * kq + kj
                        w8t = []
                        if kj == 0:
                            w8t = [ev_cast]
                            if ev_ptB_free:
                                w8t.append(ev_ptB_free)
                        lastt = op("t", lambda e_, kd=kd, kj=kj, ptB=ptB:
                                   e_.transpose(
                                       ptB[:, 128 * kj:128 * (kj + 1)],
                                       ybc.ap()[:, 128 * kd:128 * (kd + 1)],
                                       idn[:]),
                                   inc=(kj == 3), prio_waits=w8t)
                    ev_ptB_free = op("v", lambda e_, kq=kq, m=m, ptB=ptB:
                                     e_.tensor_copy(
                        yTh.ap()[:, 4 * kq:4 * (kq + 1), 128 * m:128 * (m + 1)],
                        ptB.ap().rearrange("p (k c) -> p k c", k=4)),
                        inc=True, prio_waits=[lastt])
                ev_ybc_free = ev_ptB_free
            ev_yT = ev_ptB_free

            ev_wg = [None, None]
            w8 = [ev_ag_gu] if bb == 0 else []
            if ev_wg_free[0]:
                w8.append(ev_wg_free[0])
            ev_wg[0] = dma(ST_WG, wg_buf[0].ap(),
                           ag_out_gu.ap()[0:2048, 0:256]
                           .rearrange("(k p) c -> p k c", p=128),
                           prio_waits=w8)
            for jp in range(64):
                par = jp % 3
                wb = jp % 2
                if jp + 1 < 64:
                    r = (jp + 1) // 8
                    q = (jp + 1) % 8
                    ev_wg[1 - wb] = dma(
                        ST_WG, wg_buf[1 - wb].ap(),
                        ag_out_gu.ap()[2048 * r:2048 * (r + 1),
                                       256 * q:256 * (q + 1)]
                        .rearrange("(k p) c -> p k c", p=128),
                        prio_waits=[ev_wg_free[1 - wb]]
                        if ev_wg_free[1 - wb] else ())
                lastg = None
                for kd in range(NKD):
                    w8 = []
                    if kd == 0:
                        w8 = [ev_wg[wb], ev_yT]
                        if ev_pg_free[par]:
                            w8.append(ev_pg_free[par])
                    op("t", lambda e_, kd=kd, par=par, wb=wb: e_.matmul(
                        pg[par][:], wg_buf[wb].ap()[:, kd, 0:128],
                        yTh.ap()[:, kd, :],
                        start=(kd == 0), stop=(kd == NKD - 1)), prio_waits=w8)
                    lastg = op("t", lambda e_, kd=kd, par=par, wb=wb: e_.matmul(
                        pu[par][:], wg_buf[wb].ap()[:, kd, 128:256],
                        yTh.ap()[:, kd, :],
                        start=(kd == 0), stop=(kd == NKD - 1)),
                        inc=(kd == NKD - 1))
                ev_wg_free[wb] = lastg
                ev_sl = op("c", lambda e_, par=par, wb=wb: e_.activation(
                    slu[wb][:], pg[par][:], AF.Silu), inc=True,
                    prio_waits=[lastg] +
                    ([ev_pg_free[par]] if ev_pg_free[par] else []) +
                    ([ev_slu_rd[wb]] if ev_slu_rd[wb] else []))
                ev_h = op("v", lambda e_, jp=jp, par=par, wb=wb: e_.tensor_tensor(
                    h_sb.ap()[:, jp, :], slu[wb][:], pu[par][:], ALU.mult),
                    inc=True, prio_waits=[ev_sl])
                ev_pg_free[par] = ev_h
                ev_slu_rd[wb] = ev_h
                ev_h_done = ev_h
            wait("t", ev_pg_free[0])
            wait("t", ev_pg_free[1])
            wait("t", ev_pg_free[2])

            ev_gr = None
            for m in range(4):
                lastG = None
                for k in range(64):
                    w8 = [ev_h_done] if k == 0 else []
                    lastG = op("t", lambda e_, m=m, k=k: e_.matmul(
                        pG[:], h_sb.ap()[:, k, 128 * m:128 * (m + 1)],
                        h_sb.ap()[:, k, 128 * m:128 * (m + 1)],
                        start=(k == 0), stop=(k == 63)),
                        inc=(k == 63), prio_waits=w8)
                op("v", lambda e_: e_.tensor_tensor(g1s[:], pG.ap(), idnf[:],
                                                    ALU.mult),
                   prio_waits=[lastG, ev_idnf])
                ev_gr = op("v", lambda e_, m=m, bb=bb: e_.tensor_reduce(
                    inv4.ap()[:, 4 * bb + m:4 * bb + m + 1], g1s.ap(), AXX,
                    ALU.add), inc=True)
                wait("t", ev_gr)
                op("v", lambda e_, m=m, bb=bb: e_.tensor_scalar(
                    inv4[:, 4 * bb + m:4 * bb + m + 1],
                    inv4[:, 4 * bb + m:4 * bb + m + 1],
                    1.0 / 8192.0, EPS_RMS, ALU.mult, ALU.add),
                    prio_waits=[ev_gr])
                ev_rcp = op("v", lambda e_, m=m, bb=bb: e_.reciprocal(
                    inv4[:, 4 * bb + m:4 * bb + m + 1],
                    inv4[:, 4 * bb + m:4 * bb + m + 1]), inc=True)
                op("c", lambda e_, m=m, bb=bb: e_.activation(
                    inv4.ap()[:, 4 * bb + m:4 * bb + m + 1],
                    inv4.ap()[:, 4 * bb + m:4 * bb + m + 1], AF.Sqrt),
                    prio_waits=[ev_rcp])
            ev_inv = op("c", lambda e_: e_.nop(), inc=True)
            ev_b1v = op("v", lambda e_: e_.nop(), inc=True)
        return ev_b1v

    def phaseB_proj(bb, ev_b1v):
        nonlocal ev_zsb_free, ev_ptB_free
        with (nc.psum_tensor(f"po0{bb}", [128, 2048], F32) as po0,
              nc.psum_tensor(f"po1{bb}", [128, 2048], F32) as po1):
            po = [po0, po1]
            for mp in range(2):
                def wp_load(kk, par, extra=()):
                    e1 = dma(ST_WP,
                             wp_buf[par].ap()[:, 0, :],
                             ag_out_p.ap()[256 * kk:256 * kk + 128, :]
                             .rearrange("(k p) c -> p (k c)", p=128),
                             prio_waits=list(extra))
                    e2 = dma(ST_WP2,
                             wp_buf[par].ap()[:, 1, :],
                             ag_out_p.ap()[256 * kk + 128:256 * (kk + 1), :]
                             .rearrange("(k p) c -> p (k c)", p=128),
                             prio_waits=list(extra))
                    return (e1, e2)

                ev_wp = [None, None]
                w8 = [ev_ag_p] if (bb == 0 and mp == 0) else []
                if ev_wp_free[0]:
                    w8.append(ev_wp_free[0])
                ev_wp[0] = wp_load(0, 0, extra=w8)
                lastp = None
                for kk in range(32):
                    par = kk % 2
                    if kk + 1 < 32:
                        ev_wp[1 - par] = wp_load(
                            kk + 1, 1 - par,
                            extra=[ev_wp_free[1 - par]]
                            if ev_wp_free[1 - par] else ())
                    for k2 in range(2):
                        k = 2 * kk + k2
                        for mm in range(2):
                            m = 2 * mp + mm
                            for nb in range(4):
                                w8 = []
                                if k2 == 0 and mm == 0 and nb == 0:
                                    w8 = [ev_wp[par][0], ev_wp[par][1],
                                          ev_h_done]
                                    if k == 0:
                                        w8 += [ev_b1v, ev_inv]
                                evmm = op("t", lambda e_, k=k, k2=k2, m=m,
                                          nb=nb, mm=mm, par=par: e_.matmul(
                                    po[mm][:, 512 * nb:512 * (nb + 1)],
                                    h_sb.ap()[:, k, 128 * m:128 * (m + 1)],
                                    wp_buf[par].ap()[:, k2,
                                                     512 * nb:512 * (nb + 1)],
                                    start=(k == 0), stop=(k == 63)),
                                    inc=(k2 == 1 and mm == 1 and nb == 3),
                                    prio_waits=w8)
                                if evmm is not None:
                                    lastp = evmm
                    ev_wp_free[par] = lastp
                for mm in range(2):
                    m = 2 * mp + mm
                    i = 4 * bb + m
                    op("v", lambda e_, mm=mm, m=m, i=i, bb=bb:
                       e_.scalar_tensor_tensor(
                           zsb[:], po[mm].ap(),
                           inv4.ap()[:, 4 * bb + m:4 * bb + m + 1],
                           y_sb.ap()[:, i, :], ALU.mult, ALU.add),
                       prio_waits=[lastp, ev_inv] +
                       ([ev_zsb_free] if ev_zsb_free else []))
                    ev_z = op("v", lambda e_: e_.nop(), inc=True)
                    ev_sum = op("v", lambda e_: e_.tensor_reduce(
                        red4.ap()[:, 0:1], zsb.ap(), AXX, ALU.add), inc=True)
                    op("c", lambda e_: e_.activation(
                        sqs[:], zsb[:, 0:1024], AF.Square,
                        accum_out=red4.ap()[:, 1:2]), prio_waits=[ev_z])
                    ev_sq2 = op("c", lambda e_: e_.activation(
                        sqs[:], zsb[:, 1024:2048], AF.Square,
                        accum_out=red4.ap()[:, 2:3]), inc=True)
                    op("v", lambda e_: e_.tensor_scalar_mul(
                        red4[:, 0:1], red4[:, 0:1], -1.0 / 2048.0),
                        prio_waits=[ev_sum])
                    op("v", lambda e_: e_.tensor_add(
                        red4[:, 1:2], red4[:, 1:2], red4[:, 2:3]),
                        prio_waits=[ev_sq2])
                    op("v", lambda e_: e_.tensor_scalar_mul(
                        red4[:, 1:2], red4[:, 1:2], 1.0 / 2048.0))
                    op("v", lambda e_: e_.tensor_tensor(
                        red4[:, 2:3], red4[:, 0:1], red4[:, 0:1], ALU.mult))
                    op("v", lambda e_: e_.tensor_sub(
                        red4[:, 1:2], red4[:, 1:2], red4[:, 2:3]))
                    op("v", lambda e_: e_.tensor_scalar_add(
                        red4[:, 1:2], red4[:, 1:2], EPS_LN))
                    ev_rc = op("v", lambda e_: e_.reciprocal(
                        red4[:, 2:3], red4[:, 1:2]), inc=True)
                    ev_st2 = op("c", lambda e_: e_.activation(
                        red4.ap()[:, 3:4], red4.ap()[:, 2:3], AF.Sqrt),
                        inc=True, prio_waits=[ev_rc])
                    ev_o = op("v", lambda e_: e_.tensor_scalar(
                        zsb[:], zsb[:], red4.ap()[:, 0:1], red4.ap()[:, 3:4],
                        ALU.add, ALU.mult), inc=True, prio_waits=[ev_st2])
                    ev_zsb_free = ev_o
                    ev_store = dma(ST_O, o_rows[128 * i:128 * (i + 1), :],
                                   zsb[:], prio_waits=[ev_o])
                    wait("v", ev_store)
                wait("t", ev_zsb_free)
        ev_ptB_free = ev_zsb_free

    for bb in range(2):
        ev_b1v = phaseB_batch(bb)
        phaseB_proj(bb, ev_b1v)

    drain_all()
    flush("phaseB")
    sB.close()
    st.close()
    return nc


def _host_inputs(L0_attn_x, L0_attn_Wq, L0_attn_Wk, L0_attn_Wv, L0_attn_Wo,
                 L0_attn_betas, L0_ffn_w_gate_up, L0_ffn_w_proj):
    x = np.asarray(L0_attn_x)
    Wq = np.asarray(L0_attn_Wq); Wk = np.asarray(L0_attn_Wk)
    Wv = np.asarray(L0_attn_Wv); Wo = np.asarray(L0_attn_Wo)
    betas = np.asarray(L0_attn_betas).reshape(16, 128)
    Wgu = np.asarray(L0_ffn_w_gate_up); Wp = np.asarray(L0_ffn_w_proj)

    u = np.arange(SEG)
    s_q = 16 * (u % 32) + u // 32
    t = np.arange(SEG)[:, None]
    maskT = (t <= s_q[None, :]).astype(bf16)
    idn = np.eye(128, dtype=np.float32)

    in_maps = []
    for c in range(NCORES):
        b, g = c // 4, c % 4
        xb = np.ascontiguousarray(x[b])
        xres = np.ascontiguousarray(
            xb.reshape(NSEG, 4, 128, D)[:, g].reshape(1024, D))
        wgu_sl = np.ascontiguousarray(
            np.concatenate([Wgu[1024 * c:1024 * (c + 1)],
                            Wgu[HFF + 1024 * c:HFF + 1024 * (c + 1)]], 0))
        wp_sl = np.ascontiguousarray(Wp[:, 1024 * c:1024 * (c + 1)])
        in_maps.append(dict(
            x_b=xb,
            x_res=xres,
            wq_d=np.ascontiguousarray(Wq[:, 512 * g:512 * (g + 1)]),
            wk_d=np.ascontiguousarray(Wk[:, 512 * g:512 * (g + 1)]),
            wv_d=np.ascontiguousarray(Wv[:, 512 * g:512 * (g + 1)]),
            wo_d=Wo,
            betas_d=np.ascontiguousarray(betas[4 * g:4 * (g + 1)].reshape(1, 512)),
            wgu_d=wgu_sl,
            wp_d=wp_sl,
            mask_d=maskT,
            idn_d=idn.astype(bf16),
            idnf_d=idn,
        ))
    return in_maps


def kernel(L0_attn_x, L0_attn_Wq, L0_attn_Wk, L0_attn_Wv, L0_attn_Wo,
           L0_attn_betas, L0_ffn_w_gate_up, L0_ffn_w_proj):
    in_maps = _host_inputs(L0_attn_x, L0_attn_Wq, L0_attn_Wk, L0_attn_Wv,
                           L0_attn_Wo, L0_attn_betas, L0_ffn_w_gate_up,
                           L0_ffn_w_proj)
    nc = _build(debug=0)
    res = run_bass_kernel_spmd(nc, in_maps, core_ids=list(range(NCORES)))
    out = np.empty((2, SEQ, D), dtype=np.float32)
    for c in range(NCORES):
        b, g = c // 4, c % 4
        rows = np.asarray(res.results[c]["o_rows"]).reshape(NSEG, 128, D)
        out.reshape(2, NSEG, 4, 128, D)[b, :, g] = rows
    return out


# version 06
# speedup vs baseline: 1.2226x; 1.2226x over previous
import math
import numpy as np
import ml_dtypes
from contextlib import ExitStack
import concourse.bass as bass
import concourse.mybir as mybir
from concourse.bass_utils import run_bass_kernel_spmd

F32 = mybir.dt.float32
BF16 = mybir.dt.bfloat16
I16 = mybir.dt.int16
I32 = mybir.dt.int32
AF = mybir.ActivationFunctionType
ALU = mybir.AluOpType
AXX = mybir.AxisListType.X
bf16 = ml_dtypes.bfloat16

D = 2048
SEQ = 4096
SEG = 512
NSEG = 8
NH = 4
HFF = 8192
NKD = D // 128
EPS_LN = 1e-5
EPS_RMS = 1.1920929e-07
THR = 0.5009765625
C1E8 = float(np.float32(bf16(1e-8)))
INV_SCALE = float(np.float32(1.0) / np.float32(math.sqrt(128.0)))
Z0 = 1.0 / 128.0
NCORES = 8


class Ctr:
    def __init__(self, name):
        self.name = name
        self.h = None
        self.n = 0

    def bump(self, k=1):
        self.n += k
        return (self, self.n)


def _build(debug=0, phases=9, cut=None, noscr=False, noexp=False):
    nc = bass.Bass("TRN2", target_bir_lowering=False)

    x_b = nc.dram_tensor("x_b", [SEQ, D], F32, kind="ExternalInput").ap()
    x_res = nc.dram_tensor("x_res", [1024, D], F32, kind="ExternalInput").ap()
    wq_d = nc.dram_tensor("wq_d", [D, 512], F32, kind="ExternalInput").ap()
    wk_d = nc.dram_tensor("wk_d", [D, 512], F32, kind="ExternalInput").ap()
    wv_d = nc.dram_tensor("wv_d", [D, 512], F32, kind="ExternalInput").ap()
    wo_d = nc.dram_tensor("wo_d", [D, D], F32, kind="ExternalInput").ap()
    betas_d = nc.dram_tensor("betas_d", [1, 512], F32, kind="ExternalInput").ap()
    wgu_d = nc.dram_tensor("wgu_d", [2048, D], F32, kind="ExternalInput").ap()
    wp_d = nc.dram_tensor("wp_d", [D, 1024], F32, kind="ExternalInput").ap()
    mask_d = nc.dram_tensor("mask_d", [SEG, SEG], BF16, kind="ExternalInput").ap()
    idn_d = nc.dram_tensor("idn_d", [128, 128], BF16, kind="ExternalInput").ap()
    idnf_d = nc.dram_tensor("idnf_d", [128, 128], F32, kind="ExternalInput").ap()
    o_rows = nc.dram_tensor("o_rows", [1024, D], F32, kind="ExternalOutput").ap()

    ag_in_gu = nc.dram_tensor("ag_in_gu", [1024, 4096], BF16, kind="Internal")
    ag_out_gu = nc.dram_tensor("ag_out_gu", [8192, 4096], BF16, kind="Internal")
    ag_in_p = nc.dram_tensor("ag_in_p", [1024, 2048], BF16, kind="Internal")
    ag_out_p = nc.dram_tensor("ag_out_p", [8192, 2048], BF16, kind="Internal")

    dbg = {}
    if debug >= 1:
        dbg["d_aggu"] = nc.dram_tensor("d_aggu", [16384, 2048], BF16, kind="ExternalOutput").ap()
        dbg["d_agp"] = nc.dram_tensor("d_agp", [8192, 2048], BF16, kind="ExternalOutput").ap()
    if debug >= 2:
        dbg["d_y"] = nc.dram_tensor("d_y", [1024, D], F32, kind="ExternalOutput").ap()
        dbg["d_mt"] = nc.dram_tensor("d_mt", [2048, 1024], BF16, kind="ExternalOutput").ap()

    EQ = {k: [] for k in ("t", "v", "c", "g", "y")}
    T = Ctr("T"); V = Ctr("V"); C = Ctr("C")
    CC = Ctr("CC"); CC2 = Ctr("CC2"); BAR = Ctr("BAR")
    CTRS = [T, V, C, CC, CC2, BAR]

    def emit(eng, fn):
        EQ[eng].append(fn)

    def wait(eng, evt):
        if evt is None:
            return
        ctr, val = evt
        emit(eng, lambda e, c_=ctr, v_=val: e.wait_ge(c_.h, v_))

    def waits(eng, evts):
        for evt in evts:
            wait(eng, evt)

    def barrier():
        tgt = BAR.n + 5
        BAR.n = tgt
        for eng in ("t", "v", "c", "g", "y"):
            emit(eng, lambda e: e.nop().then_inc(BAR.h, 1))
            emit(eng, lambda e, v_=tgt: e.wait_ge(BAR.h, v_))

    class Stream:
        def __init__(self, name, q):
            self.q = q
            self.ctr = Ctr(name)

    def dma(stream, out, in_, prio_waits=()):
        q = stream.q
        waits(q, prio_waits)
        if stream.ctr.n:
            wait(q, (stream.ctr, stream.ctr.n))
        evt = stream.ctr.bump(16)
        emit(q, lambda e, o=out, i=in_, c_=stream.ctr:
             e.dma_start(out=o, in_=i).then_inc(c_.h, 16))
        return evt

    def op(eng, fn, inc=False, prio_waits=()):
        ctr = {"t": T, "v": V, "c": C}[eng]
        waits(eng, prio_waits)
        if eng in ("v", "c"):
            if ctr.n > 0:
                wait(eng, (ctr, ctr.n))
            evt = ctr.bump(1)
            emit(eng, lambda e, f=fn, c_=ctr: f(e).then_inc(c_.h, 1))
            return evt
        if inc:
            evt = ctr.bump(1)
            emit(eng, lambda e, f=fn, c_=ctr: f(e).then_inc(c_.h, 1))
            return evt
        emit(eng, lambda e, f=fn: f(e))
        return None

    def t_event():
        wait("t", (T, T.n))
        evt = T.bump(1)
        emit("t", lambda e: e.nop().then_inc(T.h, 1))
        return evt

    def flush(name):
        with nc.Block(name) as blk:
            if EQ["t"]:
                blk.tensor(lambda t, L=EQ["t"][:]: [fn(t) for fn in L])
            if EQ["v"]:
                blk.vector(lambda v, L=EQ["v"][:]: [fn(v) for fn in L])
            if EQ["c"]:
                blk.scalar(lambda s, L=EQ["c"][:]: [fn(s) for fn in L])
            if EQ["g"]:
                blk.gpsimd(lambda g, L=EQ["g"][:]: [fn(g) for fn in L])
            if EQ["y"]:
                blk.sync(lambda s, L=EQ["y"][:]: [fn(s) for fn in L])
        for k in EQ:
            EQ[k].clear()

    st = ExitStack()
    en = st.enter_context

    ST_C = Stream("sC", "y")
    ST_SLAB = Stream("sS", "y")
    ST_WG = Stream("sWG", "y")
    ST_WP = Stream("sWP", "y")
    ST_XR = Stream("sXR", "y")
    ST_O = Stream("sO", "y")
    ST_X = Stream("sX", "g")
    ST_QL = Stream("sQL", "g")
    ST_W = Stream("sW", "g")
    ST_WP2 = Stream("sWP2", "g")
    STREAMS = [ST_C, ST_SLAB, ST_WG, ST_WP, ST_XR, ST_O, ST_X, ST_QL, ST_W,
               ST_WP2]

    for ctr in CTRS + [x.ctr for x in STREAMS]:
        ctr.h = en(nc.semaphore(ctr.name))

    idn = en(nc.sbuf_tensor("idn", [128, 128], BF16))
    idnf = en(nc.sbuf_tensor("idnf", [128, 128], F32))
    maskT = en(nc.sbuf_tensor("maskT", [128, 4, SEG], BF16))
    beta_t = en(nc.sbuf_tensor("beta_t", [128, NH, 128], F32))
    onesf = en(nc.sbuf_tensor("onesf", [1, 128], F32))
    bsb = en(nc.sbuf_tensor("bsb", [1, 512], F32))
    zsb = en(nc.sbuf_tensor("zsb", [128, 2048], F32))
    sqs = en(nc.sbuf_tensor("sqs", [128, 1024], F32))
    red4 = en(nc.sbuf_tensor("red4", [128, 6], F32))

    ev_idn = dma(ST_C, idn[:], idn_d)
    ev_idnf = dma(ST_C, idnf[:], idnf_d)
    ev_mask = dma(ST_C, maskT.ap(), mask_d.rearrange("(m p) u -> p m u", p=128))
    ev_bsb = dma(ST_C, bsb[:], betas_d)
    op("v", lambda e_: e_.memset(onesf[:], 1.0))
    ev_ones = op("v", lambda e_: e_.nop(), inc=True)

    s0 = ExitStack()
    e0 = s0.enter_context
    qw_c = [e0(nc.sbuf_tensor(f"qwc{i}", [128, 4096], BF16)) for i in range(2)]
    qwf = e0(nc.sbuf_tensor("qwf", [128, 4096], F32))
    m1 = e0(nc.sbuf_tensor("m1", [128, 4096], F32))
    m2 = e0(nc.sbuf_tensor("m2", [128, 4096], F32))
    qsum = e0(nc.sbuf_tensor("qsum", [128, 64], F32))
    qs = e0(nc.sbuf_tensor("qs", [128, 64], BF16))
    qsf = e0(nc.sbuf_tensor("qsf", [128, 64], F32))
    qt = e0(nc.sbuf_tensor("qt", [128, 64], F32))
    qtn = e0(nc.sbuf_tensor("qtn", [128, 64], F32))
    wfin_gu = e0(nc.sbuf_tensor("wfin_gu", [128, 16, 2048], BF16))
    wfin_p = e0(nc.sbuf_tensor("wfin_p", [128, 16, 1024], BF16))
    asm = [e0(nc.sbuf_tensor(f"asm{i}", [128, 2048], BF16)) for i in range(2)]
    asm2 = [e0(nc.sbuf_tensor(f"asm2{i}", [128, 4096], BF16)) for i in range(2)]
    ptq = e0(nc.psum_tensor("ptq", [128, 512], BF16))

    qfree = [None, None]

    def quant_rows(src_dram, n_rows, n_cols, wfin_all, kblk):
        nchunk = n_rows // (128 * kblk)
        ngrp = (kblk * n_cols) // 64
        ev_free = qfree
        ev_done = None
        for ch in range(nchunk):
            par = ch % 2
            w8 = [ev_free[par]] if ev_free[par] else []
            qc = qw_c[par].ap().rearrange("p (k d) -> p k d", k=kblk)[:, :, 0:n_cols]
            ev_load = dma(
                ST_QL, qc,
                src_dram[ch * 128 * kblk:(ch + 1) * 128 * kblk, :]
                .rearrange("(k p) d -> p k d", p=128),
                prio_waits=w8)
            flat = qw_c[par].ap()
            ev_fx = op("v", lambda e_, a=flat: e_.tensor_copy(a.bitcast(I16),
                                                              a.bitcast(I16)),
                       prio_waits=[ev_load])
            ev_red = op("v", lambda e_, a=flat, ng=ngrp: e_.tensor_reduce(
                qsum.ap()[:, 0:ng].rearrange("p (g o) -> p g o", o=1),
                a.rearrange("p (g z) -> p g z", z=64)[:, 0:ng, :],
                AXX, ALU.add, apply_absolute_value=True), inc=True)
            op("v", lambda e_: e_.tensor_copy(qsum.ap().bitcast(I32),
                                              qsum.ap().bitcast(I32)),
               prio_waits=[ev_red])
            op("v", lambda e_: e_.tensor_scalar_mul(qs[:], qsum[:], 1.0 / 64.0))
            op("v", lambda e_: e_.tensor_scalar_max(qs[:], qs[:], C1E8))
            op("v", lambda e_: e_.tensor_copy(qs.ap().bitcast(I16),
                                              qs.ap().bitcast(I16)))
            op("v", lambda e_: e_.tensor_copy(qsf[:], qs[:]))
            op("v", lambda e_: e_.tensor_scalar_mul(qt[:], qsf[:], THR))
            op("v", lambda e_: e_.tensor_scalar_mul(qtn[:], qt[:], -1.0))
            ev_cast = op("c", lambda e_, a=flat: e_.activation(qwf.ap(), a, AF.Copy),
                         inc=True,
                         prio_waits=[ev_fx] + ([ev_done] if ev_done else
                                               [x for x in qfree if x]))
            ng = ngrp
            op("v", lambda e_, ng=ng: e_.tensor_tensor(
                m1.ap().rearrange("p (g z) -> p g z", z=64)[:, 0:ng, :],
                qwf.ap().rearrange("p (g z) -> p g z", z=64)[:, 0:ng, :],
                qt.ap()[:, 0:ng].rearrange("p (g o) -> p g o", o=1)
                .broadcast_to([128, ng, 64]),
                ALU.is_gt), prio_waits=[ev_cast])
            op("v", lambda e_, ng=ng: e_.tensor_tensor(
                m2.ap().rearrange("p (g z) -> p g z", z=64)[:, 0:ng, :],
                qwf.ap().rearrange("p (g z) -> p g z", z=64)[:, 0:ng, :],
                qtn.ap()[:, 0:ng].rearrange("p (g o) -> p g o", o=1)
                .broadcast_to([128, ng, 64]),
                ALU.is_lt))
            op("v", lambda e_: e_.tensor_sub(m1[:], m1[:], m2[:]))
            op("v", lambda e_, ng=ng: e_.tensor_tensor(
                m2.ap().rearrange("p (g z) -> p g z", z=64)[:, 0:ng, :],
                m1.ap().rearrange("p (g z) -> p g z", z=64)[:, 0:ng, :],
                qsf.ap()[:, 0:ng].rearrange("p (g o) -> p g o", o=1)
                .broadcast_to([128, ng, 64]),
                ALU.mult))
            ws = wfin_all.ap()[:, ch * kblk:(ch + 1) * kblk, 0:n_cols] \
                .rearrange("p k d -> p (k d)")
            op("v", lambda e_, ws=ws, a=flat: e_.tensor_sub(ws, m2.ap(), a))
            op("v", lambda e_, ws=ws, a=flat: e_.tensor_add(ws, a, ws))
            op("v", lambda e_, ws=ws: e_.tensor_copy(ws.bitcast(I16),
                                                     ws.bitcast(I16)))
            ev_free[par] = op("v", lambda e_: e_.nop(), inc=True)
            ev_done = ev_free[par]
        return ev_done

    def drain_all():
        for s_ in STREAMS:
            if s_.ctr.n:
                emit(s_.q, lambda e_, c_=s_.ctr: e_.wait_ge(c_.h, c_.n))

    def cutout(tag):
        drain_all()
        flush("cut_" + tag)
        s0.close()
        st.close()
        return nc

    if cut == "none":
        return cutout("none")
    ev_qgu = quant_rows(wgu_d, 2048, 2048, wfin_gu, kblk=2)
    if cut == "qgu":
        if debug >= 1:
            d_wfin = nc.dram_tensor("d_wfin", [2048, 2048], BF16,
                                    kind="ExternalOutput").ap()
            dma(ST_C, d_wfin.rearrange("(k p) d -> p k d", p=128),
                wfin_gu.ap(), prio_waits=[ev_qgu])
        return cutout("qgu")
    ev_qp = quant_rows(wp_d, D, 1024, wfin_p, kblk=4)
    if cut == "qp":
        return cutout("qp")

    ev_asm_free = [None, None]
    ev_slab = []
    ev_ptq_free = None

    def slab(src, n_blocks, n_out_rows, colmap, ag_in, ev_dep):
        nonlocal ev_ptq_free
        for kd in range(n_out_rows):
            par = kd % 2
            for Hq in range((n_blocks + 3) // 4):
                lastt = None
                for hj in range(4):
                    H = Hq * 4 + hj
                    if H >= n_blocks:
                        break
                    w8 = []
                    if hj == 0:
                        if Hq == 0:
                            w8 = [ev_dep, ev_idn]
                        if ev_ptq_free:
                            w8.append(ev_ptq_free)
                    lastt = op("t", lambda e_, H=H, kd=kd, hj=hj: e_.transpose(
                        ptq[:, 128 * hj:128 * (hj + 1)],
                        src(H, kd), idn[:]),
                        inc=(hj == 3 or H == n_blocks - 1), prio_waits=w8)
                for hj in range(4):
                    H = Hq * 4 + hj
                    if H >= n_blocks:
                        break
                    cb = colmap(H)
                    w8 = [lastt] if hj == 0 else []
                    if hj == 0 and ev_asm_free[par] and Hq == 0:
                        w8.append(ev_asm_free[par])
                    ev_ptq_free = op("v", lambda e_, hj=hj, cb=cb, par=par:
                                     e_.tensor_copy(asm[par][:, cb:cb + 128],
                                                    ptq[:, 128 * hj:128 * (hj + 1)]),
                                     inc=(hj == 3 or H == n_blocks - 1),
                                     prio_waits=w8)
            ev_ev = ev_ptq_free
            ev_st = dma(ST_SLAB, ag_in.ap()[128 * kd:128 * (kd + 1), :],
                        asm[par][:], prio_waits=[ev_ev])
            ev_asm_free[par] = ev_st
            ev_slab.append(ev_st)

    for q in range(8):
        par = q % 2
        for grp in range(8):
            lastt = None
            for j4 in range(4):
                idx = grp * 4 + j4
                kd = idx // 2
                is_up = idx % 2
                H = q + 8 * is_up
                w8 = []
                if j4 == 0:
                    w8 = [ev_qgu, ev_idn]
                    if ev_ptq_free:
                        w8.append(ev_ptq_free)
                lastt = op("t", lambda e_, H=H, kd=kd, j4=j4: e_.transpose(
                    ptq[:, 128 * j4:128 * (j4 + 1)],
                    wfin_gu.ap()[:, H, 128 * kd:128 * (kd + 1)], idn[:]),
                    inc=(j4 == 3), prio_waits=w8)
            for j4 in range(4):
                idx = grp * 4 + j4
                kd = idx // 2
                is_up = idx % 2
                cb = 256 * kd + 128 * is_up
                w8 = [lastt] if j4 == 0 else []
                if j4 == 0 and grp == 0 and ev_asm_free[par]:
                    w8.append(ev_asm_free[par])
                ev_ptq_free = op("v", lambda e_, j4=j4, cb=cb, par=par:
                                 e_.tensor_copy(asm2[par][:, cb:cb + 128],
                                                ptq[:, 128 * j4:128 * (j4 + 1)]),
                                 inc=(j4 == 3), prio_waits=w8)
        ev_st = dma(ST_SLAB, ag_in_gu.ap()[128 * q:128 * (q + 1), :],
                    asm2[par][:], prio_waits=[ev_ptq_free])
        ev_asm_free[par] = ev_st
        ev_slab.append(ev_st)
    if cut == "sgu":
        return cutout("sgu")
    slab(lambda H, kd: wfin_p.ap()[:, H, 128 * kd:128 * (kd + 1)],
         16, 8, lambda H: 128 * H, ag_in_p, ev_qp)
    if cut == "sp":
        return cutout("sp")

    if phases <= 0:
        drain_all()
        flush("phase0cut")
        s0.close()
        st.close()
        return nc

    waits("g", ev_slab)
    ev_ag_gu = CC.bump(1)
    emit("g", lambda e_: e_.collective_compute(
        "AllGather", ALU.bypass, replica_groups=[list(range(NCORES))],
        ins=[ag_in_gu.ap().opt()], outs=[ag_out_gu.ap().opt()]).then_inc(CC.h, 1))
    ev_ag_p = CC2.bump(1)
    emit("g", lambda e_: e_.collective_compute(
        "AllGather", ALU.bypass, replica_groups=[list(range(NCORES))],
        ins=[ag_in_p.ap().opt()], outs=[ag_out_p.ap().opt()]).then_inc(CC2.h, 1))

    if debug >= 1:
        evp = None
        for blk in range(64):
            evl = dma(ST_C, asm2[0][:], ag_out_gu.ap()[128 * blk:128 * (blk + 1), :],
                      prio_waits=[ev_ag_gu] + ([evp] if evp else []))
            evp = dma(ST_C, dbg["d_aggu"].rearrange("(n p) c -> p n c", p=128)
                      [:, blk:blk + 1, :].rearrange("p n (a b) -> p (n a) b", a=2),
                      asm2[0].ap().rearrange("p (a b) -> p a b", a=2),
                      prio_waits=[evl])
        for blk in range(64):
            evl = dma(ST_C, asm[0][:], ag_out_p.ap()[128 * blk:128 * (blk + 1), :],
                      prio_waits=[ev_ag_p, evp])
            evp = dma(ST_C, dbg["d_agp"][128 * blk:128 * (blk + 1), :], asm[0][:],
                      prio_waits=[evl])

    barrier()
    if phases <= 1:
        drain_all()
    flush("phase0")
    s0.close()
    if phases <= 1:
        st.close()
        return nc

    sMT = ExitStack()
    mt_sb = sMT.enter_context(nc.sbuf_tensor("mt_sb", [128, NKD, 1024], BF16))

    sA = ExitStack()
    eA = sA.enter_context
    wq_sb = eA(nc.sbuf_tensor("wq_sb", [128, NKD, 512], BF16))
    wk_sb = eA(nc.sbuf_tensor("wk_sb", [128, NKD, 512], BF16))
    wv_sb = eA(nc.sbuf_tensor("wv_sb", [128, NKD, 512], BF16))
    memz = eA(nc.sbuf_tensor("memz", [128, NH, 129], BF16))
    mem_f = eA(nc.sbuf_tensor("mem_f", [128, NH, 128], F32))
    z_f = eA(nc.sbuf_tensor("z_f", [128, NH], F32))
    xT = eA(nc.sbuf_tensor("xT", [128, NKD, 512], BF16))
    xseg = [eA(nc.sbuf_tensor(f"xseg{i}", [128, 4, 2048], BF16)) for i in range(2)]
    NPAR = 3
    qT_sb = [eA(nc.sbuf_tensor(f"qT{i}", [128, 512], BF16)) for i in range(NPAR)]
    kT_sb = [eA(nc.sbuf_tensor(f"kT{i}", [128, 512], BF16)) for i in range(NPAR)]
    sqT = [eA(nc.sbuf_tensor(f"sqT{i}", [128, 512], BF16)) for i in range(NPAR)]
    skT = [eA(nc.sbuf_tensor(f"skT{i}", [128, 512], BF16)) for i in range(NPAR)]
    skn = [eA(nc.sbuf_tensor(f"skn{i}", [128, 4, 128], BF16)) for i in range(NPAR)]
    tmpA = [eA(nc.sbuf_tensor("tmpA0", [128, 512], F32))] * NPAR
    tmpB = [eA(nc.sbuf_tensor("tmpB0", [128, 512], F32))] * NPAR
    etmp = [eA(nc.sbuf_tensor(f"etmp{i}", [128, 512], F32)) for i in range(2)]
    v_sb = [eA(nc.sbuf_tensor(f"v_sb{i}", [128, 4, 4 * 129], BF16))
            for i in range(2)]
    E_sb = [eA(nc.sbuf_tensor(f"E{i}", [128, 4, 512], BF16)) for i in range(NPAR)]
    att_d = [eA(nc.sbuf_tensor("attd0", [128, 128], F32))] * NPAR
    diff = [eA(nc.sbuf_tensor("diff0", [128, 128], F32))] * NPAR
    att_bf = [eA(nc.sbuf_tensor(f"attbf{i}", [128, 4, 128], BF16)) for i in range(NPAR)]
    delta = [eA(nc.sbuf_tensor(f"delta{i}", [128, 4, 128], BF16)) for i in range(NPAR)]
    rcol = [eA(nc.sbuf_tensor("rcol0", [128, 4], F32))] * NPAR
    pq = [eA(nc.psum_tensor(f"pq{i}", [128, 512], F32)) for i in range(2)]
    pk = [eA(nc.psum_tensor(f"pk{i}", [128, 512], F32)) for i in range(2)]
    psc = eA(nc.psum_tensor("psc", [128, 1024], F32))
    pnd = eA(nc.psum_tensor("pnd", [128, 512], F32))
    ptp = eA(nc.psum_tensor("ptpA", [128, 512], BF16))

    ev_wq = dma(ST_W, wq_sb.ap(), wq_d.rearrange("(k p) c -> p k c", p=128))
    ev_wk = dma(ST_W, wk_sb.ap(), wk_d.rearrange("(k p) c -> p k c", p=128))
    ev_wv = dma(ST_W, wv_sb.ap(), wv_d.rearrange("(k p) c -> p k c", p=128))

    op("v", lambda e_: e_.memset(mem_f.ap().rearrange("p h d -> p (h d)"), 0.0))
    op("v", lambda e_: e_.memset(z_f[:], Z0))
    op("v", lambda e_: e_.memset(memz.ap().rearrange("p h d -> p (h d)"), 0.0))
    for hh in range(NH):
        op("v", lambda e_, hh=hh: e_.memset(memz.ap()[:, hh, 128:129], Z0))
    for vpar in range(2):
        op("v", lambda e_, vpar=vpar:
           e_.memset(v_sb[vpar].ap().rearrange("p m c -> p (m c)"), 0.0))
        for m in range(4):
            for hh in range(NH):
                op("v", lambda e_, m=m, hh=hh, vpar=vpar:
                   e_.memset(v_sb[vpar].ap()[:, m, 129 * hh + 128:129 * hh + 129], 1.0))
    ev_vinit = op("v", lambda e_: e_.nop(), inc=True)

    for hh in range(NH):
        evb = op("t", lambda e_, hh=hh: e_.matmul(
            pq[0][:, 0:128], onesf[0:1, :], bsb[0:1, 128 * hh:128 * (hh + 1)],
            start=True, stop=True), inc=True,
            prio_waits=[ev_bsb, ev_ones] if hh == 0 else ())
        evs = op("c", lambda e_, hh=hh: e_.activation(
            beta_t.ap()[:, hh, :], pq[0][:, 0:128], AF.Sigmoid),
            inc=True, prio_waits=[evb])
        wait("t", evs)

    if cut == "beta":
        drain_all()
        flush("cutbeta")
        sA.close()
        sMT.close()
        st.close()
        return nc

    if noscr:
        scr = lambda apx: apx
    else:
        scr = lambda apx: apx.rearrange("p (t r) -> p r t", r=16)

    ev_xld = [None, None]
    ev_xT_users = None
    ev_psc_free = [None, None]
    ev_etmp_rd = [None, None]
    ev_pq_free = [None, None]
    ev_pk_free = [None, None]
    ev_pnd_free = None
    ev_ptp_free = None
    ev_mt_done = []
    ev_state = [None] * NH
    ev_par_free = [None] * NPAR
    H_evq = {}
    H_evq1 = {}
    H_evk1 = {}
    H_evsk = {}
    H_evskn = {}
    H_evE = {}
    H_evatt = {}

    ev_xld[0] = dma(ST_X, xseg[0].ap(),
                    x_b[0:SEG, :].rearrange("(m p) d -> p m d", p=128))

    ev_xseg_rd = [None, None]

    def st_xtp(i, par):
        nonlocal ev_ptp_free
        last_ev = None
        lastt = None
        for kd in range(NKD):
            for m in range(4):
                w8 = []
                if m == 0:
                    if kd == 0:
                        w8 = [ev_xld[par], ev_idn]
                        if ev_xT_users:
                            w8.append(ev_xT_users)
                    if ev_ptp_free:
                        w8.append(ev_ptp_free)
                lastt = op("t", lambda e_, m=m, kd=kd, par=par: e_.transpose(
                    ptp[:, 128 * m:128 * (m + 1)],
                    xseg[par].ap()[:, m, 128 * kd:128 * (kd + 1)], idn[:]),
                    inc=(m == 3), prio_waits=w8)
            last_ev = op("c", lambda e_, kd=kd: e_.activation(
                xT.ap()[:, kd, :], ptp.ap(), AF.Copy),
                inc=True, prio_waits=[lastt])
            ev_ptp_free = last_ev
        ev_xseg_rd[par] = lastt
        return last_ev

    ev_vseg_free = [None, None]

    def st_v(i, ev_xT_ready):
        vpar = i % 2
        ev = None
        for m in range(4):
            half = m % 2
            lastv = None
            for kd in range(NKD):
                w8 = []
                if kd == 0:
                    w8 = [ev_xT_ready, ev_wv, ev_vinit]
                    if ev_psc_free[half]:
                        w8.append(ev_psc_free[half])
                lastv = op("t", lambda e_, m=m, kd=kd, half=half: e_.matmul(
                    psc[:, 512 * half:512 * (half + 1)],
                    xT.ap()[:, kd, 128 * m:128 * (m + 1)],
                    wv_sb.ap()[:, kd, :],
                    start=(kd == 0), stop=(kd == NKD - 1)),
                    inc=(kd == NKD - 1), prio_waits=w8)
            w8e = [lastv]
            if m == 0 and ev_vseg_free[vpar]:
                w8e += ev_vseg_free[vpar]
            ev = op("c", lambda e_, m=m, half=half, vpar=vpar: e_.activation(
                v_sb[vpar].ap()[:, m, :]
                .rearrange("p (h c) -> p h c", c=129)[:, :, 0:128],
                psc.ap()[:, 512 * half:512 * (half + 1)]
                .rearrange("p (h c) -> p h c", c=128), AF.Copy),
                inc=True, prio_waits=w8e)
            ev_psc_free[half] = ev
        return ev

    def st_qk(i, gh, ev_xT_ready):
        hh = gh % 4
        par = gh % 2
        p3 = gh % NPAR
        w8 = [ev_xT_ready, ev_wq]
        if ev_pq_free[par]:
            w8.append(ev_pq_free[par])
        lastq = None
        for kd in range(NKD):
            lastq = op("t", lambda e_, hh=hh, kd=kd, par=par: e_.matmul(
                pq[par][:], wq_sb.ap()[:, kd, 128 * hh:128 * (hh + 1)],
                xT.ap()[:, kd, :],
                start=(kd == 0), stop=(kd == NKD - 1)),
                inc=(kd == NKD - 1), prio_waits=w8 if kd == 0 else ())
        w8 = [ev_wk]
        if ev_pk_free[par]:
            w8.append(ev_pk_free[par])
        lastk = None
        for kd in range(NKD):
            lastk = op("t", lambda e_, hh=hh, kd=kd, par=par: e_.matmul(
                pk[par][:], wk_sb.ap()[:, kd, 128 * hh:128 * (hh + 1)],
                xT.ap()[:, kd, :],
                start=(kd == 0), stop=(kd == NKD - 1)),
                inc=(kd == NKD - 1), prio_waits=w8 if kd == 0 else ())
        tA, tB = tmpA[p3], tmpB[p3]
        wfree = [ev_par_free[p3]] if ev_par_free[p3] else []
        ev_q1 = op("c", lambda e_, p3=p3, par=par: e_.activation(
            qT_sb[p3][:], scr(pq[par].ap()), AF.Copy), inc=True,
            prio_waits=[lastq] + wfree)
        ev_q2 = op("c", lambda e_, par=par: e_.activation(
            tB[:], scr(pq[par].ap()), AF.Relu), inc=True,
            prio_waits=[H_evsk[gh - 1]] if (gh - 1) in H_evsk else ())
        ev_mnq = op("v", lambda e_, par=par: e_.tensor_scalar_min(
            tA[:], scr(pq[par].ap()), 0.0), prio_waits=[lastq, ev_q2] + wfree)
        ev_q3 = op("c", lambda e_: e_.activation(
            tA[:], tA[:], AF.Exp), inc=True, prio_waits=[ev_mnq])
        ev_pq_free[par] = ev_q3
        ev_sq = op("v", lambda e_, p3=p3: e_.tensor_add(
            sqT[p3][:], tA[:], tB[:]), inc=True, prio_waits=[ev_q2, ev_q3])
        H_evq[gh] = ev_sq
        H_evq1[gh] = ev_q1
        ev_k1 = op("c", lambda e_, p3=p3, par=par: e_.activation(
            kT_sb[p3][:], pk[par].ap(), AF.Copy), inc=True, prio_waits=[lastk])
        ev_k2 = op("c", lambda e_, par=par: e_.activation(
            tB[:], pk[par].ap(), AF.Relu), inc=True, prio_waits=[ev_sq])
        ev_mnk = op("v", lambda e_, par=par: e_.tensor_scalar_min(
            tA[:], pk[par].ap(), 0.0), prio_waits=[lastk, ev_k2])
        ev_k3 = op("c", lambda e_: e_.activation(
            tA[:], tA[:], AF.Exp), inc=True, prio_waits=[ev_mnk])
        ev_pk_free[par] = ev_k3
        ev_sk = op("v", lambda e_, p3=p3: e_.tensor_add(
            skT[p3][:], tA[:], tB[:]), inc=True, prio_waits=[ev_k2, ev_k3])
        H_evsk[gh] = ev_sk
        H_evk1[gh] = ev_k1

    def st_sktp_sc(i, gh):
        nonlocal ev_ptp_free
        p3 = gh % NPAR
        wfree = [ev_par_free[p3]] if ev_par_free[p3] else []
        lastt = None
        for mt in range(4):
            w8 = []
            if mt == 0:
                w8 = [H_evsk[gh]]
                if ev_ptp_free:
                    w8.append(ev_ptp_free)
            lastt = op("t", lambda e_, mt=mt, p3=p3: e_.transpose(
                ptp[:, 128 * mt:128 * (mt + 1)],
                skT[p3].ap()[:, 128 * mt:128 * (mt + 1)], idn[:]),
                inc=(mt == 3), prio_waits=w8)
        ev_ptp_free = op("c", lambda e_, p3=p3: e_.activation(
            skn[p3].ap().rearrange("p m c -> p (m c)"), ptp.ap(), AF.Copy),
            inc=True, prio_waits=[lastt] + wfree)
        H_evskn[gh] = ev_ptp_free
        evE = []
        lastsc = None
        for mt in range(4):
            half = mt % 2
            w8 = [H_evq1[gh], H_evk1[gh]] if mt == 0 else []
            if ev_psc_free[half]:
                w8.append(ev_psc_free[half])
            lastsc = op("t", lambda e_, mt=mt, p3=p3, half=half: e_.matmul(
                psc[:, 512 * half:512 * (half + 1)],
                kT_sb[p3][:, 128 * mt:128 * (mt + 1)], qT_sb[p3][:],
                start=True, stop=True), inc=True, prio_waits=w8)
            buf = etmp[half]
            w8b = [lastsc, ev_mask]
            if ev_etmp_rd[half]:
                w8b.append(ev_etmp_rd[half])
            evx = op("c", lambda e_, mt=mt, buf=buf, half=half: e_.activation(
                buf[:], psc.ap()[:, 512 * half:512 * (half + 1)],
                AF.Exp, scale=INV_SCALE), inc=True, prio_waits=w8b)
            ev_psc_free[half] = evx
            w8c = list(wfree) if mt == 0 else []
            evm = op("v", lambda e_, mt=mt, buf=buf, p3=p3: e_.tensor_mul(
                E_sb[p3].ap()[:, mt, :], buf[:], maskT.ap()[:, mt, :]),
                inc=True, prio_waits=[evx] + w8c)
            ev_etmp_rd[half] = evm
            evE.append(evm)
        H_evE[gh] = evE

    def st_nd(i, gh):
        nonlocal ev_pnd_free
        hh = gh % 4
        p3 = gh % NPAR
        rc = rcol[p3]
        ad = att_d[p3]
        df = diff[p3]
        ev_cmb_all = None
        for m in range(4):
            w8 = [H_evE[gh][-1], ev_v_ready[i]]
            if ev_pnd_free:
                w8.append(ev_pnd_free)
            for mt in range(4):
                op("t", lambda e_, m=m, mt=mt, hh=hh, p3=p3, vp=i % 2: e_.matmul(
                    pnd[:, 0:129],
                    E_sb[p3].ap()[:, mt, 128 * m:128 * (m + 1)],
                    v_sb[vp].ap()[:, mt, 129 * hh:129 * (hh + 1)],
                    start=(mt == 0), stop=(mt == 3)),
                    prio_waits=w8 if mt == 0 else ())
            w8n = [H_evq[gh]]
            if ev_state[hh]:
                w8n.append(ev_state[hh])
            ev_nd = op("t", lambda e_, m=m, hh=hh, p3=p3: e_.matmul(
                pnd[:, 256:385],
                sqT[p3][:, 128 * m:128 * (m + 1)], memz.ap()[:, hh, :],
                start=True, stop=True), inc=True, prio_waits=w8n)
            ev_r1 = op("v", lambda e_, rc=rc: e_.reciprocal(
                rc[:, 0:1], pnd[:, 128:129]), inc=True, prio_waits=[ev_nd])
            ev_r2 = op("v", lambda e_, rc=rc: e_.reciprocal(
                rc[:, 1:2], pnd[:, 384:385]), inc=True)
            op("v", lambda e_, rc=rc, ad=ad: e_.tensor_scalar_mul(
                ad[:], pnd[:, 0:128], rc[:, 0:1]), prio_waits=[ev_r1, ev_r2])
            op("v", lambda e_, rc=rc, ad=ad, df=df: e_.scalar_tensor_tensor(
                df[:], pnd[:, 256:384], rc[:, 1:2], ad[:],
                ALU.mult, ALU.subtract))
            op("v", lambda e_, hh=hh, df=df: e_.tensor_mul(
                df[:], df[:], beta_t.ap()[:, hh, :]))
            ev_cmb = op("v", lambda e_, m=m, p3=p3, ad=ad, df=df: e_.tensor_add(
                att_bf[p3].ap()[:, m, :], ad[:], df[:]), inc=True)
            ev_cmb_all = ev_cmb
            ev_ak = op("t", lambda e_, m=m, hh=hh, p3=p3: e_.matmul(
                pnd[:, 256:385],
                skT[p3][:, 128 * m:128 * (m + 1)], memz.ap()[:, hh, :],
                start=True, stop=True), inc=True,
                prio_waits=[H_evsk[gh], ev_cmb])
            ev_r3 = op("v", lambda e_, rc=rc: e_.reciprocal(
                rc[:, 2:3], pnd[:, 384:385]), inc=True, prio_waits=[ev_ak])
            ev_dl = op("v", lambda e_, m=m, hh=hh, p3=p3, rc=rc, vp=i % 2:
                       e_.scalar_tensor_tensor(
                delta[p3].ap()[:, m, :], pnd[:, 256:384], rc[:, 2:3],
                v_sb[vp].ap()[:, m, 129 * hh:129 * hh + 128],
                ALU.mult, ALU.subtract), inc=True, prio_waits=[ev_r3])
            ev_pnd_free = ev_dl
        H_evatt[gh] = ev_cmb_all
        lastdm = None
        for mt in range(4):
            w8 = [H_evskn[gh], ev_pnd_free] if mt == 0 else []
            lastdm = op("t", lambda e_, mt=mt, p3=p3: e_.matmul(
                pnd[:, 0:128], skn[p3].ap()[:, mt, :],
                delta[p3].ap()[:, mt, :],
                start=(mt == 0), stop=(mt == 3)),
                inc=(mt == 3), prio_waits=w8)
        op("v", lambda e_, hh=hh: e_.tensor_sub(
            mem_f.ap()[:, hh, :], mem_f.ap()[:, hh, :], pnd[:, 0:128]),
            prio_waits=[lastdm])
        ev_mcp = op("v", lambda e_, hh=hh: e_.tensor_copy(
            memz.ap()[:, hh, 0:128], mem_f.ap()[:, hh, :]), inc=True)
        ev_pnd_free = ev_mcp
        ev_zr = op("v", lambda e_, p3=p3, rc=rc: e_.tensor_reduce(
            rc.ap()[:, 3:4], skT[p3].ap(), AXX, ALU.add),
            inc=True, prio_waits=[H_evsk[gh]])
        op("v", lambda e_, hh=hh, rc=rc: e_.tensor_add(
            z_f.ap()[:, hh:hh + 1], z_f.ap()[:, hh:hh + 1], rc[:, 3:4]),
            prio_waits=[ev_zr])
        ev_state[hh] = op("v", lambda e_, hh=hh: e_.tensor_copy(
            memz.ap()[:, hh, 128:129], z_f.ap()[:, hh:hh + 1]), inc=True)
        return lastdm

    def st_atttp(i, gh, ev_dmdone):
        nonlocal ev_ptp_free
        hh = gh % 4
        p3 = gh % NPAR
        lastat = None
        for m in range(4):
            w8 = []
            if m == 0:
                w8 = [H_evatt[gh]]
                if ev_ptp_free:
                    w8.append(ev_ptp_free)
            lastat = op("t", lambda e_, m=m, p3=p3: e_.transpose(
                ptp[:, 128 * m:128 * (m + 1)], att_bf[p3].ap()[:, m, :],
                idn[:]), inc=(m == 3), prio_waits=w8)
        ev_par_free[p3] = lastat
        ev_mtc = op("c", lambda e_, hh=hh, i=i: e_.activation(
            mt_sb.ap()[:, :, 128 * i + 32 * hh:128 * i + 32 * hh + 32]
            .rearrange("p (m r) t -> p m r t", m=4),
            ptp.ap().rearrange("p (m r t) -> p m r t", m=4, r=4), AF.Copy),
            inc=True, prio_waits=[lastat])
        ev_ptp_free = ev_mtc
        return ev_mtc

    ev_v_ready = {}
    for i in range(NSEG):
        par = i % 2
        if i + 1 < NSEG:
            ev_xld[1 - par] = dma(
                ST_X, xseg[1 - par].ap(),
                x_b[SEG * (i + 1):SEG * (i + 2), :].rearrange("(m p) d -> p m d", p=128),
                prio_waits=[ev_xseg_rd[1 - par]] if ev_xseg_rd[1 - par] else ())
        if i == 0:
            ev_xT_ready = st_xtp(0, 0)
            ev_v_ready[0] = st_v(0, ev_xT_ready)
        g0 = 4 * i
        st_qk(i, g0 + 0, ev_xT_ready)
        st_qk(i, g0 + 1, ev_xT_ready)
        st_sktp_sc(i, g0 + 0)
        st_qk(i, g0 + 2, ev_xT_ready)
        dm0 = st_nd(i, g0 + 0)
        st_atttp(i, g0 + 0, dm0)
        st_sktp_sc(i, g0 + 1)
        st_qk(i, g0 + 3, ev_xT_ready)
        dm1 = st_nd(i, g0 + 1)
        st_atttp(i, g0 + 1, dm1)
        st_sktp_sc(i, g0 + 2)
        dm2 = st_nd(i, g0 + 2)
        st_atttp(i, g0 + 2, dm2)
        st_sktp_sc(i, g0 + 3)
        if i + 1 < NSEG:
            ev_xT_users = t_event()
            ev_xT_ready = st_xtp(i + 1, 1 - par)
            ev_v_ready[i + 1] = st_v(i + 1, ev_xT_ready)
        dm3 = st_nd(i, g0 + 3)
        ev_vseg_free[i % 2] = [dm3, ev_state[3]]
        ev_mt3 = st_atttp(i, g0 + 3, dm3)
        ev_mt_done.append(ev_mt3)
        if cut == f"seg{i}":
            drain_all()
            flush("cutseg")
            sA.close()
            sMT.close()
            st.close()
            return nc

    if debug >= 2:
        evp = None
        for r in range(NKD):
            evp = dma(ST_C, dbg["d_mt"][128 * r:128 * (r + 1), :],
                      mt_sb.ap()[:, r, :],
                      prio_waits=[ev_mt_done[-1]] + ([evp] if evp else []))
    barrier()
    if phases <= 2:
        drain_all()
    flush("phaseA")
    sA.close()
    if phases <= 2:
        sMT.close()
        st.close()
        return nc

    y_sb = en(nc.sbuf_tensor("y_sb", [128, NSEG, 2048], F32, side="right"))
    sA2 = ExitStack()
    e2 = sA2.enter_context
    wo_sb = e2(nc.sbuf_tensor("wo_sb", [128, NKD, 2048], BF16))
    xres_b = [e2(nc.sbuf_tensor(f"xres{i}", [128, 2048], F32)) for i in range(2)]
    pwo = e2(nc.psum_tensor("pwo", [128, 2048], F32))

    ev_wo = None
    for part in range(4):
        ev_wo = dma(ST_W, wo_sb.ap()[:, 4 * part:4 * (part + 1), :],
                    wo_d[512 * part:512 * (part + 1), :]
                    .rearrange("(k p) c -> p k c", p=128))

    ev_xres = [None, None]
    ev_xres_free = [None, None]
    ev_pwo_free = None
    ev_zsb_free = None

    ev_xres[0] = dma(ST_XR, xres_b[0][:], x_res[0:128, :])
    for i in range(NSEG):
        par = i % 2
        if i + 1 < NSEG:
            ev_xres[1 - par] = dma(
                ST_XR, xres_b[1 - par][:], x_res[128 * (i + 1):128 * (i + 2), :],
                prio_waits=[ev_xres_free[1 - par]] if ev_xres_free[1 - par] else ())
        lastw = None
        for rch in range(NKD):
            for nb in range(4):
                w8 = []
                if rch == 0 and nb == 0:
                    w8 = [ev_mt_done[i], ev_wo]
                    if ev_pwo_free:
                        w8.append(ev_pwo_free)
                lastw = op("t", lambda e_, i=i, rch=rch, nb=nb: e_.matmul(
                    pwo[:, 512 * nb:512 * (nb + 1)],
                    mt_sb.ap()[:, rch, 128 * i:128 * (i + 1)],
                    wo_sb.ap()[:, rch, 512 * nb:512 * (nb + 1)],
                    start=(rch == 0), stop=(rch == NKD - 1)),
                    inc=(rch == NKD - 1 and nb == 3), prio_waits=w8)
        w8 = [lastw, ev_xres[par]]
        if ev_zsb_free:
            w8.append(ev_zsb_free)
        op("v", lambda e_, par=par: e_.tensor_add(zsb[:], pwo.ap(), xres_b[par][:]),
           prio_waits=w8)
        ev_z = op("v", lambda e_: e_.nop(), inc=True)
        ev_pwo_free = ev_z
        ev_xres_free[par] = ev_z
        ev_sum = op("v", lambda e_: e_.tensor_reduce(
            red4.ap()[:, 0:1], zsb.ap(), AXX, ALU.add), inc=True, prio_waits=[ev_z])
        op("c", lambda e_: e_.activation(
            sqs[:], zsb[:, 0:1024], AF.Square, accum_out=red4.ap()[:, 1:2]),
            prio_waits=[ev_z])
        ev_sq2 = op("c", lambda e_: e_.activation(
            sqs[:], zsb[:, 1024:2048], AF.Square, accum_out=red4.ap()[:, 2:3]),
            inc=True)
        op("v", lambda e_: e_.tensor_scalar_mul(red4[:, 0:1], red4[:, 0:1],
                                                -1.0 / 2048.0), prio_waits=[ev_sum])
        op("v", lambda e_: e_.tensor_add(red4[:, 1:2], red4[:, 1:2],
                                         red4[:, 2:3]), prio_waits=[ev_sq2])
        op("v", lambda e_: e_.tensor_scalar_mul(red4[:, 1:2], red4[:, 1:2],
                                                1.0 / 2048.0))
        op("v", lambda e_: e_.tensor_tensor(red4[:, 2:3], red4[:, 0:1],
                                            red4[:, 0:1], ALU.mult))
        op("v", lambda e_: e_.tensor_sub(red4[:, 1:2], red4[:, 1:2], red4[:, 2:3]))
        op("v", lambda e_: e_.tensor_scalar_add(red4[:, 1:2], red4[:, 1:2], EPS_LN))
        ev_rc = op("v", lambda e_: e_.reciprocal(red4[:, 2:3], red4[:, 1:2]),
                   inc=True)
        ev_st2 = op("c", lambda e_: e_.activation(red4.ap()[:, 3:4],
                                                  red4.ap()[:, 2:3], AF.Sqrt),
                    inc=True, prio_waits=[ev_rc])
        ev_y = op("v", lambda e_, i=i: e_.tensor_scalar(
            y_sb.ap()[:, i, :], zsb[:], red4.ap()[:, 0:1], red4.ap()[:, 3:4],
            ALU.add, ALU.mult), inc=True, prio_waits=[ev_st2])
        ev_zsb_free = ev_y

    if debug >= 2:
        evp = None
        for i in range(NSEG):
            evp = dma(ST_C, dbg["d_y"][128 * i:128 * (i + 1), :],
                      y_sb.ap()[:, i, :],
                      prio_waits=[ev_zsb_free] + ([evp] if evp else []))

    barrier()
    if phases <= 3:
        drain_all()
    flush("phaseA2")
    sA2.close()
    sMT.close()
    if phases <= 3:
        st.close()
        return nc

    sB = ExitStack()
    eB = sB.enter_context
    yTh = eB(nc.sbuf_tensor("yTh", [128, NKD, 512], BF16))
    ybc = eB(nc.sbuf_tensor("ybc", [128, 2048], BF16))
    slu = [eB(nc.sbuf_tensor(f"slu{i}", [128, 512], BF16)) for i in range(2)]
    h_sb = eB(nc.sbuf_tensor("h_sb", [128, 64, 512], BF16))
    wg_buf = [eB(nc.sbuf_tensor(f"wg{i}", [128, NKD, 256], BF16)) for i in range(2)]
    wp_buf = [eB(nc.sbuf_tensor(f"wpb{i}", [128, 2, 2048], BF16)) for i in range(2)]
    inv4 = eB(nc.sbuf_tensor("inv4", [128, 8], F32))
    g1s = eB(nc.sbuf_tensor("g1s", [128, 128], F32))

    ev_ybc_free = None
    ev_ptB_free = None
    ev_wg_free = [None, None]
    ev_wp_free = [None, None]
    ev_pg_free = [None, None, None]
    ev_slu_rd = [None, None]
    ev_yT = None
    ev_h_done = None
    ev_inv = None

    def phaseB_batch(bb):
        nonlocal ev_ybc_free, ev_ptB_free, ev_yT, ev_h_done, ev_inv, ev_zsb_free
        with (nc.psum_tensor(f"ptB{bb}", [128, 512], BF16) as ptB,
              nc.psum_tensor(f"pg0{bb}", [128, 512], F32) as pg0,
              nc.psum_tensor(f"pg1{bb}", [128, 512], F32) as pg1,
              nc.psum_tensor(f"pg2{bb}", [128, 512], F32) as pg2,
              nc.psum_tensor(f"pu0{bb}", [128, 512], F32) as pu0,
              nc.psum_tensor(f"pu1{bb}", [128, 512], F32) as pu1,
              nc.psum_tensor(f"pu2{bb}", [128, 512], F32) as pu2,
              nc.psum_tensor(f"pG{bb}", [128, 128], F32) as pG):
            pg = [pg0, pg1, pg2]
            pu = [pu0, pu1, pu2]
            for m in range(4):
                i = 4 * bb + m
                w8 = [ev_ybc_free] if ev_ybc_free else []
                ev_cast = op("c", lambda e_, i=i: e_.activation(
                    ybc[:], y_sb.ap()[:, i, :], AF.Copy), inc=True, prio_waits=w8)
                for kq in range(4):
                    lastt = None
                    for kj in range(4):
                        kd = 4 * kq + kj
                        w8t = []
                        if kj == 0:
                            w8t = [ev_cast]
                            if ev_ptB_free:
                                w8t.append(ev_ptB_free)
                        lastt = op("t", lambda e_, kd=kd, kj=kj, ptB=ptB:
                                   e_.transpose(
                                       ptB[:, 128 * kj:128 * (kj + 1)],
                                       ybc.ap()[:, 128 * kd:128 * (kd + 1)],
                                       idn[:]),
                                   inc=(kj == 3), prio_waits=w8t)
                    ev_ptB_free = op("v", lambda e_, kq=kq, m=m, ptB=ptB:
                                     e_.tensor_copy(
                        yTh.ap()[:, 4 * kq:4 * (kq + 1), 128 * m:128 * (m + 1)],
                        ptB.ap().rearrange("p (k c) -> p k c", k=4)),
                        inc=True, prio_waits=[lastt])
                ev_ybc_free = ev_ptB_free
            ev_yT = ev_ptB_free

            ev_wg = [None, None]
            w8 = [ev_ag_gu] if bb == 0 else []
            if ev_wg_free[0]:
                w8.append(ev_wg_free[0])
            ev_wg[0] = dma(ST_WG, wg_buf[0].ap().rearrange("p k c -> p (k c)"),
                           ag_out_gu.ap()[0:128, :], prio_waits=w8)
            for jp in range(64):
                par = jp % 3
                wb = jp % 2
                if jp + 1 < 64:
                    r = (jp + 1) // 8
                    q = (jp + 1) % 8
                    ev_wg[1 - wb] = dma(
                        ST_WG, wg_buf[1 - wb].ap().rearrange("p k c -> p (k c)"),
                        ag_out_gu.ap()[1024 * r + 128 * q:1024 * r + 128 * (q + 1), :],
                        prio_waits=[ev_wg_free[1 - wb]]
                        if ev_wg_free[1 - wb] else ())
                lastg = None
                for kd in range(NKD):
                    w8 = []
                    if kd == 0:
                        w8 = [ev_wg[wb], ev_yT]
                        if ev_pg_free[par]:
                            w8.append(ev_pg_free[par])
                    op("t", lambda e_, kd=kd, par=par, wb=wb: e_.matmul(
                        pg[par][:], wg_buf[wb].ap()[:, kd, 0:128],
                        yTh.ap()[:, kd, :],
                        start=(kd == 0), stop=(kd == NKD - 1)), prio_waits=w8)
                    lastg = op("t", lambda e_, kd=kd, par=par, wb=wb: e_.matmul(
                        pu[par][:], wg_buf[wb].ap()[:, kd, 128:256],
                        yTh.ap()[:, kd, :],
                        start=(kd == 0), stop=(kd == NKD - 1)),
                        inc=(kd == NKD - 1))
                ev_wg_free[wb] = lastg
                ev_sl = op("c", lambda e_, par=par, wb=wb: e_.activation(
                    slu[wb][:], pg[par][:], AF.Silu), inc=True,
                    prio_waits=[lastg] +
                    ([ev_pg_free[par]] if ev_pg_free[par] else []) +
                    ([ev_slu_rd[wb]] if ev_slu_rd[wb] else []))
                ev_h = op("v", lambda e_, jp=jp, par=par, wb=wb: e_.tensor_tensor(
                    h_sb.ap()[:, jp, :], slu[wb][:], pu[par][:], ALU.mult),
                    inc=True, prio_waits=[ev_sl])
                ev_pg_free[par] = ev_h
                ev_slu_rd[wb] = ev_h
                ev_h_done = ev_h
            wait("t", ev_pg_free[0])
            wait("t", ev_pg_free[1])
            wait("t", ev_pg_free[2])

            ev_gr = None
            for m in range(4):
                lastG = None
                for k in range(64):
                    w8 = [ev_h_done] if k == 0 else []
                    lastG = op("t", lambda e_, m=m, k=k: e_.matmul(
                        pG[:], h_sb.ap()[:, k, 128 * m:128 * (m + 1)],
                        h_sb.ap()[:, k, 128 * m:128 * (m + 1)],
                        start=(k == 0), stop=(k == 63)),
                        inc=(k == 63), prio_waits=w8)
                op("v", lambda e_: e_.tensor_tensor(g1s[:], pG.ap(), idnf[:],
                                                    ALU.mult),
                   prio_waits=[lastG, ev_idnf])
                ev_gr = op("v", lambda e_, m=m, bb=bb: e_.tensor_reduce(
                    inv4.ap()[:, 4 * bb + m:4 * bb + m + 1], g1s.ap(), AXX,
                    ALU.add), inc=True)
                wait("t", ev_gr)
                op("v", lambda e_, m=m, bb=bb: e_.tensor_scalar(
                    inv4[:, 4 * bb + m:4 * bb + m + 1],
                    inv4[:, 4 * bb + m:4 * bb + m + 1],
                    1.0 / 8192.0, EPS_RMS, ALU.mult, ALU.add),
                    prio_waits=[ev_gr])
                ev_rcp = op("v", lambda e_, m=m, bb=bb: e_.reciprocal(
                    inv4[:, 4 * bb + m:4 * bb + m + 1],
                    inv4[:, 4 * bb + m:4 * bb + m + 1]), inc=True)
                op("c", lambda e_, m=m, bb=bb: e_.activation(
                    inv4.ap()[:, 4 * bb + m:4 * bb + m + 1],
                    inv4.ap()[:, 4 * bb + m:4 * bb + m + 1], AF.Sqrt),
                    prio_waits=[ev_rcp])
            ev_inv = op("c", lambda e_: e_.nop(), inc=True)
            ev_b1v = op("v", lambda e_: e_.nop(), inc=True)
        return ev_b1v

    def phaseB_proj(bb, ev_b1v):
        nonlocal ev_zsb_free, ev_ptB_free
        with (nc.psum_tensor(f"po0{bb}", [128, 2048], F32) as po0,
              nc.psum_tensor(f"po1{bb}", [128, 2048], F32) as po1):
            po = [po0, po1]
            for mp in range(2):
                def wp_load(kk, par, extra=()):
                    e1 = dma(ST_WP,
                             wp_buf[par].ap()[:, 0, :],
                             ag_out_p.ap()[256 * kk:256 * kk + 128, :]
                             .rearrange("(k p) c -> p (k c)", p=128),
                             prio_waits=list(extra))
                    e2 = dma(ST_WP2,
                             wp_buf[par].ap()[:, 1, :],
                             ag_out_p.ap()[256 * kk + 128:256 * (kk + 1), :]
                             .rearrange("(k p) c -> p (k c)", p=128),
                             prio_waits=list(extra))
                    return (e1, e2)

                ev_wp = [None, None]
                w8 = [ev_ag_p] if (bb == 0 and mp == 0) else []
                if ev_wp_free[0]:
                    w8.append(ev_wp_free[0])
                ev_wp[0] = wp_load(0, 0, extra=w8)
                lastp = None
                for kk in range(32):
                    par = kk % 2
                    if kk + 1 < 32:
                        ev_wp[1 - par] = wp_load(
                            kk + 1, 1 - par,
                            extra=[ev_wp_free[1 - par]]
                            if ev_wp_free[1 - par] else ())
                    for k2 in range(2):
                        k = 2 * kk + k2
                        for mm in range(2):
                            m = 2 * mp + mm
                            for nb in range(4):
                                w8 = []
                                if k2 == 0 and mm == 0 and nb == 0:
                                    w8 = [ev_wp[par][0], ev_wp[par][1],
                                          ev_h_done]
                                    if k == 0:
                                        w8 += [ev_b1v, ev_inv]
                                evmm = op("t", lambda e_, k=k, k2=k2, m=m,
                                          nb=nb, mm=mm, par=par: e_.matmul(
                                    po[mm][:, 512 * nb:512 * (nb + 1)],
                                    h_sb.ap()[:, k, 128 * m:128 * (m + 1)],
                                    wp_buf[par].ap()[:, k2,
                                                     512 * nb:512 * (nb + 1)],
                                    start=(k == 0), stop=(k == 63)),
                                    inc=(k2 == 1 and mm == 1 and nb == 3),
                                    prio_waits=w8)
                                if evmm is not None:
                                    lastp = evmm
                    ev_wp_free[par] = lastp
                for mm in range(2):
                    m = 2 * mp + mm
                    i = 4 * bb + m
                    op("v", lambda e_, mm=mm, m=m, i=i, bb=bb:
                       e_.scalar_tensor_tensor(
                           zsb[:], po[mm].ap(),
                           inv4.ap()[:, 4 * bb + m:4 * bb + m + 1],
                           y_sb.ap()[:, i, :], ALU.mult, ALU.add),
                       prio_waits=[lastp, ev_inv] +
                       ([ev_zsb_free] if ev_zsb_free else []))
                    ev_z = op("v", lambda e_: e_.nop(), inc=True)
                    ev_sum = op("v", lambda e_: e_.tensor_reduce(
                        red4.ap()[:, 0:1], zsb.ap(), AXX, ALU.add), inc=True)
                    op("c", lambda e_: e_.activation(
                        sqs[:], zsb[:, 0:1024], AF.Square,
                        accum_out=red4.ap()[:, 1:2]), prio_waits=[ev_z])
                    ev_sq2 = op("c", lambda e_: e_.activation(
                        sqs[:], zsb[:, 1024:2048], AF.Square,
                        accum_out=red4.ap()[:, 2:3]), inc=True)
                    op("v", lambda e_: e_.tensor_scalar_mul(
                        red4[:, 0:1], red4[:, 0:1], -1.0 / 2048.0),
                        prio_waits=[ev_sum])
                    op("v", lambda e_: e_.tensor_add(
                        red4[:, 1:2], red4[:, 1:2], red4[:, 2:3]),
                        prio_waits=[ev_sq2])
                    op("v", lambda e_: e_.tensor_scalar_mul(
                        red4[:, 1:2], red4[:, 1:2], 1.0 / 2048.0))
                    op("v", lambda e_: e_.tensor_tensor(
                        red4[:, 2:3], red4[:, 0:1], red4[:, 0:1], ALU.mult))
                    op("v", lambda e_: e_.tensor_sub(
                        red4[:, 1:2], red4[:, 1:2], red4[:, 2:3]))
                    op("v", lambda e_: e_.tensor_scalar_add(
                        red4[:, 1:2], red4[:, 1:2], EPS_LN))
                    ev_rc = op("v", lambda e_: e_.reciprocal(
                        red4[:, 2:3], red4[:, 1:2]), inc=True)
                    ev_st2 = op("c", lambda e_: e_.activation(
                        red4.ap()[:, 3:4], red4.ap()[:, 2:3], AF.Sqrt),
                        inc=True, prio_waits=[ev_rc])
                    ev_o = op("v", lambda e_: e_.tensor_scalar(
                        zsb[:], zsb[:], red4.ap()[:, 0:1], red4.ap()[:, 3:4],
                        ALU.add, ALU.mult), inc=True, prio_waits=[ev_st2])
                    ev_zsb_free = ev_o
                    ev_store = dma(ST_O, o_rows[128 * i:128 * (i + 1), :],
                                   zsb[:], prio_waits=[ev_o])
                    wait("v", ev_store)
                wait("t", ev_zsb_free)
        ev_ptB_free = ev_zsb_free

    for bb in range(2):
        ev_b1v = phaseB_batch(bb)
        phaseB_proj(bb, ev_b1v)

    drain_all()
    flush("phaseB")
    sB.close()
    st.close()
    return nc


def _host_inputs(L0_attn_x, L0_attn_Wq, L0_attn_Wk, L0_attn_Wv, L0_attn_Wo,
                 L0_attn_betas, L0_ffn_w_gate_up, L0_ffn_w_proj):
    x = np.asarray(L0_attn_x)
    Wq = np.asarray(L0_attn_Wq); Wk = np.asarray(L0_attn_Wk)
    Wv = np.asarray(L0_attn_Wv); Wo = np.asarray(L0_attn_Wo)
    betas = np.asarray(L0_attn_betas).reshape(16, 128)
    Wgu = np.asarray(L0_ffn_w_gate_up); Wp = np.asarray(L0_ffn_w_proj)

    u = np.arange(SEG)
    s_q = 16 * (u % 32) + u // 32
    t = np.arange(SEG)[:, None]
    maskT = (t <= s_q[None, :]).astype(bf16)
    idn = np.eye(128, dtype=np.float32)

    in_maps = []
    for c in range(NCORES):
        b, g = c // 4, c % 4
        xb = np.ascontiguousarray(x[b])
        xres = np.ascontiguousarray(
            xb.reshape(NSEG, 4, 128, D)[:, g].reshape(1024, D))
        wgu_sl = np.ascontiguousarray(
            np.concatenate([Wgu[1024 * c:1024 * (c + 1)],
                            Wgu[HFF + 1024 * c:HFF + 1024 * (c + 1)]], 0))
        wp_sl = np.ascontiguousarray(Wp[:, 1024 * c:1024 * (c + 1)])
        in_maps.append(dict(
            x_b=xb,
            x_res=xres,
            wq_d=np.ascontiguousarray(Wq[:, 512 * g:512 * (g + 1)]),
            wk_d=np.ascontiguousarray(Wk[:, 512 * g:512 * (g + 1)]),
            wv_d=np.ascontiguousarray(Wv[:, 512 * g:512 * (g + 1)]),
            wo_d=Wo,
            betas_d=np.ascontiguousarray(betas[4 * g:4 * (g + 1)].reshape(1, 512)),
            wgu_d=wgu_sl,
            wp_d=wp_sl,
            mask_d=maskT,
            idn_d=idn.astype(bf16),
            idnf_d=idn,
        ))
    return in_maps


def kernel(L0_attn_x, L0_attn_Wq, L0_attn_Wk, L0_attn_Wv, L0_attn_Wo,
           L0_attn_betas, L0_ffn_w_gate_up, L0_ffn_w_proj):
    in_maps = _host_inputs(L0_attn_x, L0_attn_Wq, L0_attn_Wk, L0_attn_Wv,
                           L0_attn_Wo, L0_attn_betas, L0_ffn_w_gate_up,
                           L0_ffn_w_proj)
    nc = _build(debug=0)
    res = run_bass_kernel_spmd(nc, in_maps, core_ids=list(range(NCORES)))
    out = np.empty((2, SEQ, D), dtype=np.float32)
    for c in range(NCORES):
        b, g = c // 4, c % 4
        rows = np.asarray(res.results[c]["o_rows"]).reshape(NSEG, 128, D)
        out.reshape(2, NSEG, 4, 128, D)[b, :, g] = rows
    return out


# version 07
# speedup vs baseline: 1.2369x; 1.0117x over previous
import math
import numpy as np
import ml_dtypes
from contextlib import ExitStack
import concourse.bass as bass
import concourse.mybir as mybir
from concourse.bass_utils import run_bass_kernel_spmd

F32 = mybir.dt.float32
BF16 = mybir.dt.bfloat16
I16 = mybir.dt.int16
I32 = mybir.dt.int32
AF = mybir.ActivationFunctionType
ALU = mybir.AluOpType
AXX = mybir.AxisListType.X
bf16 = ml_dtypes.bfloat16

D = 2048
SEQ = 4096
SEG = 512
NSEG = 8
NH = 4
HFF = 8192
NKD = D // 128
EPS_LN = 1e-5
EPS_RMS = 1.1920929e-07
THR = 0.5009765625
C1E8 = float(np.float32(bf16(1e-8)))
INV_SCALE = float(np.float32(1.0) / np.float32(math.sqrt(128.0)))
Z0 = 1.0 / 128.0
NCORES = 8


class Ctr:
    def __init__(self, name):
        self.name = name
        self.h = None
        self.n = 0

    def bump(self, k=1):
        self.n += k
        return (self, self.n)


def _build(debug=0, phases=9, cut=None, noscr=False, noexp=False):
    nc = bass.Bass("TRN2", target_bir_lowering=False)

    x_b = nc.dram_tensor("x_b", [SEQ, D], F32, kind="ExternalInput").ap()
    x_res = nc.dram_tensor("x_res", [1024, D], F32, kind="ExternalInput").ap()
    wq_d = nc.dram_tensor("wq_d", [D, 512], F32, kind="ExternalInput").ap()
    wk_d = nc.dram_tensor("wk_d", [D, 512], F32, kind="ExternalInput").ap()
    wv_d = nc.dram_tensor("wv_d", [D, 512], F32, kind="ExternalInput").ap()
    wo_d = nc.dram_tensor("wo_d", [D, D], F32, kind="ExternalInput").ap()
    betas_d = nc.dram_tensor("betas_d", [1, 512], F32, kind="ExternalInput").ap()
    wgu_d = nc.dram_tensor("wgu_d", [2048, D], F32, kind="ExternalInput").ap()
    wp_d = nc.dram_tensor("wp_d", [D, 1024], F32, kind="ExternalInput").ap()
    mask_d = nc.dram_tensor("mask_d", [SEG, SEG], BF16, kind="ExternalInput").ap()
    idn_d = nc.dram_tensor("idn_d", [128, 128], BF16, kind="ExternalInput").ap()
    idnf_d = nc.dram_tensor("idnf_d", [128, 128], F32, kind="ExternalInput").ap()
    o_rows = nc.dram_tensor("o_rows", [1024, D], F32, kind="ExternalOutput").ap()

    ag_in_gu = nc.dram_tensor("ag_in_gu", [1024, 4096], BF16, kind="Internal")
    ag_out_gu = nc.dram_tensor("ag_out_gu", [8192, 4096], BF16, kind="Internal")
    ag_in_p = nc.dram_tensor("ag_in_p", [1024, 2048], BF16, kind="Internal")
    ag_out_p = nc.dram_tensor("ag_out_p", [8192, 2048], BF16, kind="Internal")

    dbg = {}
    if debug >= 1:
        dbg["d_aggu"] = nc.dram_tensor("d_aggu", [16384, 2048], BF16, kind="ExternalOutput").ap()
        dbg["d_agp"] = nc.dram_tensor("d_agp", [8192, 2048], BF16, kind="ExternalOutput").ap()
    if debug >= 2:
        dbg["d_y"] = nc.dram_tensor("d_y", [1024, D], F32, kind="ExternalOutput").ap()
        dbg["d_mt"] = nc.dram_tensor("d_mt", [2048, 1024], BF16, kind="ExternalOutput").ap()

    EQ = {k: [] for k in ("t", "v", "c", "g", "y")}
    T = Ctr("T"); V = Ctr("V"); C = Ctr("C"); GC = Ctr("GC")
    CC = Ctr("CC"); CC2 = Ctr("CC2"); BAR = Ctr("BAR")
    CTRS = [T, V, C, GC, CC, CC2, BAR]

    def emit(eng, fn):
        EQ[eng].append(fn)

    def wait(eng, evt):
        if evt is None:
            return
        ctr, val = evt
        emit(eng, lambda e, c_=ctr, v_=val: e.wait_ge(c_.h, v_))

    def waits(eng, evts):
        for evt in evts:
            wait(eng, evt)

    def barrier():
        tgt = BAR.n + 5
        BAR.n = tgt
        for eng in ("t", "v", "c", "g", "y"):
            emit(eng, lambda e: e.nop().then_inc(BAR.h, 1))
            emit(eng, lambda e, v_=tgt: e.wait_ge(BAR.h, v_))

    class Stream:
        def __init__(self, name, q):
            self.q = q
            self.ctr = Ctr(name)

    def dma(stream, out, in_, prio_waits=()):
        q = stream.q
        waits(q, prio_waits)
        if stream.ctr.n:
            wait(q, (stream.ctr, stream.ctr.n))
        evt = stream.ctr.bump(16)
        emit(q, lambda e, o=out, i=in_, c_=stream.ctr:
             e.dma_start(out=o, in_=i).then_inc(c_.h, 16))
        return evt

    def op(eng, fn, inc=False, prio_waits=()):
        ctr = {"t": T, "v": V, "c": C, "g": GC}[eng]
        waits(eng, prio_waits)
        if eng in ("v", "c", "g"):
            if ctr.n > 0:
                wait(eng, (ctr, ctr.n))
            evt = ctr.bump(1)
            emit(eng, lambda e, f=fn, c_=ctr: f(e).then_inc(c_.h, 1))
            return evt
        if inc:
            evt = ctr.bump(1)
            emit(eng, lambda e, f=fn, c_=ctr: f(e).then_inc(c_.h, 1))
            return evt
        emit(eng, lambda e, f=fn: f(e))
        return None

    def t_event():
        wait("t", (T, T.n))
        evt = T.bump(1)
        emit("t", lambda e: e.nop().then_inc(T.h, 1))
        return evt

    def flush(name):
        with nc.Block(name) as blk:
            if EQ["t"]:
                blk.tensor(lambda t, L=EQ["t"][:]: [fn(t) for fn in L])
            if EQ["v"]:
                blk.vector(lambda v, L=EQ["v"][:]: [fn(v) for fn in L])
            if EQ["c"]:
                blk.scalar(lambda s, L=EQ["c"][:]: [fn(s) for fn in L])
            if EQ["g"]:
                blk.gpsimd(lambda g, L=EQ["g"][:]: [fn(g) for fn in L])
            if EQ["y"]:
                blk.sync(lambda s, L=EQ["y"][:]: [fn(s) for fn in L])
        for k in EQ:
            EQ[k].clear()

    st = ExitStack()
    en = st.enter_context

    ST_C = Stream("sC", "y")
    ST_SLAB = Stream("sS", "y")
    ST_WG = Stream("sWG", "y")
    ST_WP = Stream("sWP", "y")
    ST_XR = Stream("sXR", "y")
    ST_O = Stream("sO", "y")
    ST_X = Stream("sX", "g")
    ST_QL = Stream("sQL", "g")
    ST_W = Stream("sW", "g")
    ST_WP2 = Stream("sWP2", "g")
    STREAMS = [ST_C, ST_SLAB, ST_WG, ST_WP, ST_XR, ST_O, ST_X, ST_QL, ST_W,
               ST_WP2]

    for ctr in CTRS + [x.ctr for x in STREAMS]:
        ctr.h = en(nc.semaphore(ctr.name))

    idn = en(nc.sbuf_tensor("idn", [128, 128], BF16))
    idnf = en(nc.sbuf_tensor("idnf", [128, 128], F32))
    maskT = en(nc.sbuf_tensor("maskT", [128, 4, SEG], BF16))
    beta_t = en(nc.sbuf_tensor("beta_t", [128, NH, 128], F32))
    onesf = en(nc.sbuf_tensor("onesf", [1, 128], F32))
    bsb = en(nc.sbuf_tensor("bsb", [1, 512], F32))
    zsb = en(nc.sbuf_tensor("zsb", [128, 2048], F32))
    sqs = en(nc.sbuf_tensor("sqs", [128, 1024], F32))
    red4 = en(nc.sbuf_tensor("red4", [128, 6], F32))

    ev_idn = dma(ST_C, idn[:], idn_d)
    ev_idnf = dma(ST_C, idnf[:], idnf_d)
    ev_mask = dma(ST_C, maskT.ap(), mask_d.rearrange("(m p) u -> p m u", p=128))
    ev_bsb = dma(ST_C, bsb[:], betas_d)
    op("v", lambda e_: e_.memset(onesf[:], 1.0))
    ev_ones = op("v", lambda e_: e_.nop(), inc=True)

    s0 = ExitStack()
    e0 = s0.enter_context
    qw_c = [e0(nc.sbuf_tensor(f"qwc{i}", [128, 4096], BF16)) for i in range(3)]
    m1b = [e0(nc.sbuf_tensor(f"m1b{i}", [128, 4096], BF16)) for i in range(2)]
    m2b = [e0(nc.sbuf_tensor(f"m2b{i}", [128, 4096], BF16)) for i in range(2)]
    qsum = e0(nc.sbuf_tensor("qsum", [128, 64], F32))
    qs = e0(nc.sbuf_tensor("qs", [128, 64], BF16))
    qsf2 = [e0(nc.sbuf_tensor(f"qsf{i}", [128, 64], F32)) for i in range(2)]
    qt2 = [e0(nc.sbuf_tensor(f"qt{i}", [128, 64], F32)) for i in range(2)]
    qtn2 = [e0(nc.sbuf_tensor(f"qtn{i}", [128, 64], F32)) for i in range(2)]
    wfin_gu = e0(nc.sbuf_tensor("wfin_gu", [128, 16, 2048], BF16))
    wfin_p = e0(nc.sbuf_tensor("wfin_p", [128, 16, 1024], BF16))
    asm = [e0(nc.sbuf_tensor(f"asm{i}", [128, 2048], BF16)) for i in range(2)]
    asm2 = [e0(nc.sbuf_tensor(f"asm2{i}", [128, 4096], BF16)) for i in range(2)]
    ptq = e0(nc.psum_tensor("ptq", [128, 512], BF16))

    qfree = [None, None, None]

    def quant_rows(src_dram, n_rows, n_cols, wfin_all, kblk):
        nchunk = n_rows // (128 * kblk)
        ngrp = (kblk * n_cols) // 64
        ev_free = qfree
        ev_done = None

        def grp(a):
            return a.rearrange("p (g z) -> p g z", z=64)

        def bcast(a):
            return a.rearrange("p (g o) -> p g o", o=1).broadcast_to(
                [128, ngrp, 64])

        def head(ch):
            par = ch % 3
            pm = ch % 2
            w8 = [ev_free[par]] if ev_free[par] else []
            qc = qw_c[par].ap().rearrange("p (k d) -> p k d", k=kblk)[:, :, 0:n_cols]
            ev_load = dma(
                ST_QL, qc,
                src_dram[ch * 128 * kblk:(ch + 1) * 128 * kblk, :]
                .rearrange("(k p) d -> p k d", p=128),
                prio_waits=w8)
            flat = qw_c[par].ap()
            op("v", lambda e_, a=flat: e_.tensor_copy(a.bitcast(I16),
                                                      a.bitcast(I16)),
               prio_waits=[ev_load])
            ev_red = op("v", lambda e_, a=flat: e_.tensor_reduce(
                qsum.ap()[:, 0:ngrp].rearrange("p (g o) -> p g o", o=1),
                grp(a),
                AXX, ALU.add, apply_absolute_value=True), inc=True)
            op("v", lambda e_: e_.tensor_copy(qsum.ap().bitcast(I32),
                                              qsum.ap().bitcast(I32)),
               prio_waits=[ev_red])
            op("v", lambda e_: e_.tensor_scalar_mul(qs[:], qsum[:], 1.0 / 64.0))
            op("v", lambda e_: e_.tensor_scalar_max(qs[:], qs[:], C1E8))
            op("v", lambda e_: e_.tensor_copy(qs.ap().bitcast(I16),
                                              qs.ap().bitcast(I16)))
            op("v", lambda e_, pm=pm: e_.tensor_copy(qsf2[pm][:], qs[:]))
            op("v", lambda e_, pm=pm: e_.tensor_scalar_mul(
                qt2[pm][:], qsf2[pm][:], THR))
            return op("v", lambda e_, pm=pm: e_.tensor_scalar_mul(
                qtn2[pm][:], qt2[pm][:], -1.0), inc=True)

        def gcomp(ch, ev_head):
            par = ch % 3
            pm = ch % 2
            flat = qw_c[par].ap()
            op("v", lambda e_, pm=pm, a=flat: e_.tensor_tensor(
                grp(m1b[pm].ap()), grp(a), bcast(qt2[pm].ap()[:, 0:ngrp]),
                ALU.is_gt), prio_waits=[ev_head])
            return op("v", lambda e_, pm=pm, a=flat: e_.tensor_tensor(
                grp(m2b[pm].ap()), grp(a), bcast(qtn2[pm].ap()[:, 0:ngrp]),
                ALU.is_lt), inc=True)

        def tail(ch, ev_g):
            par = ch % 3
            pm = ch % 2
            flat = qw_c[par].ap()
            op("v", lambda e_, pm=pm: e_.tensor_sub(
                m1b[pm][:], m1b[pm][:], m2b[pm][:]), prio_waits=[ev_g])
            op("v", lambda e_, pm=pm: e_.tensor_tensor(
                grp(m2b[pm].ap()), grp(m1b[pm].ap()),
                bcast(qsf2[pm].ap()[:, 0:ngrp]), ALU.mult))
            ws = wfin_all.ap()[:, ch * kblk:(ch + 1) * kblk, 0:n_cols] \
                .rearrange("p k d -> p (k d)")
            op("v", lambda e_, ws=ws, pm=pm, a=flat: e_.tensor_sub(
                ws, m2b[pm].ap(), a))
            op("v", lambda e_, ws=ws, a=flat: e_.tensor_add(ws, a, ws))
            op("v", lambda e_, ws=ws: e_.tensor_copy(ws.bitcast(I16),
                                                     ws.bitcast(I16)))
            ev_free[ch % 3] = op("v", lambda e_: e_.nop(), inc=True)
            return ev_free[ch % 3]

        ev_h = {0: head(0)}
        if nchunk > 1:
            ev_h[1] = head(1)
        for ch in range(nchunk):
            ev_g = gcomp(ch, ev_h[ch])
            ev_done = tail(ch, ev_g)
            if ch + 2 < nchunk:
                ev_h[ch + 2] = head(ch + 2)
        return ev_done

    def drain_all():
        for s_ in STREAMS:
            if s_.ctr.n:
                emit(s_.q, lambda e_, c_=s_.ctr: e_.wait_ge(c_.h, c_.n))

    def cutout(tag):
        drain_all()
        flush("cut_" + tag)
        s0.close()
        st.close()
        return nc

    if cut == "none":
        return cutout("none")
    ev_qgu = quant_rows(wgu_d, 2048, 2048, wfin_gu, kblk=2)
    if cut == "qgu":
        if debug >= 1:
            d_wfin = nc.dram_tensor("d_wfin", [2048, 2048], BF16,
                                    kind="ExternalOutput").ap()
            dma(ST_C, d_wfin.rearrange("(k p) d -> p k d", p=128),
                wfin_gu.ap(), prio_waits=[ev_qgu])
        return cutout("qgu")
    ev_qp = quant_rows(wp_d, D, 1024, wfin_p, kblk=4)
    if cut == "qp":
        return cutout("qp")

    ev_asm_free = [None, None]
    ev_slab = []
    ev_ptq_free = None

    def slab(src, n_blocks, n_out_rows, colmap, ag_in, ev_dep):
        nonlocal ev_ptq_free
        for kd in range(n_out_rows):
            par = kd % 2
            for Hq in range((n_blocks + 3) // 4):
                lastt = None
                for hj in range(4):
                    H = Hq * 4 + hj
                    if H >= n_blocks:
                        break
                    w8 = []
                    if hj == 0:
                        if Hq == 0:
                            w8 = [ev_dep, ev_idn]
                        if ev_ptq_free:
                            w8.append(ev_ptq_free)
                    lastt = op("t", lambda e_, H=H, kd=kd, hj=hj: e_.transpose(
                        ptq[:, 128 * hj:128 * (hj + 1)],
                        src(H, kd), idn[:]),
                        inc=(hj == 3 or H == n_blocks - 1), prio_waits=w8)
                for hj in range(4):
                    H = Hq * 4 + hj
                    if H >= n_blocks:
                        break
                    cb = colmap(H)
                    w8 = [lastt] if hj == 0 else []
                    if hj == 0 and ev_asm_free[par] and Hq == 0:
                        w8.append(ev_asm_free[par])
                    ev_ptq_free = op("v", lambda e_, hj=hj, cb=cb, par=par:
                                     e_.tensor_copy(asm[par][:, cb:cb + 128],
                                                    ptq[:, 128 * hj:128 * (hj + 1)]),
                                     inc=(hj == 3 or H == n_blocks - 1),
                                     prio_waits=w8)
            ev_ev = ev_ptq_free
            ev_st = dma(ST_SLAB, ag_in.ap()[128 * kd:128 * (kd + 1), :],
                        asm[par][:], prio_waits=[ev_ev])
            ev_asm_free[par] = ev_st
            ev_slab.append(ev_st)

    for q in range(8):
        par = q % 2
        for grp in range(8):
            lastt = None
            for j4 in range(4):
                idx = grp * 4 + j4
                kd = idx // 2
                is_up = idx % 2
                H = q + 8 * is_up
                w8 = []
                if j4 == 0:
                    w8 = [ev_qgu, ev_idn]
                    if ev_ptq_free:
                        w8.append(ev_ptq_free)
                lastt = op("t", lambda e_, H=H, kd=kd, j4=j4: e_.transpose(
                    ptq[:, 128 * j4:128 * (j4 + 1)],
                    wfin_gu.ap()[:, H, 128 * kd:128 * (kd + 1)], idn[:]),
                    inc=(j4 == 3), prio_waits=w8)
            for j4 in range(4):
                idx = grp * 4 + j4
                kd = idx // 2
                is_up = idx % 2
                cb = 256 * kd + 128 * is_up
                w8 = [lastt] if j4 == 0 else []
                if j4 == 0 and grp == 0 and ev_asm_free[par]:
                    w8.append(ev_asm_free[par])
                ev_ptq_free = op("v", lambda e_, j4=j4, cb=cb, par=par:
                                 e_.tensor_copy(asm2[par][:, cb:cb + 128],
                                                ptq[:, 128 * j4:128 * (j4 + 1)]),
                                 inc=(j4 == 3), prio_waits=w8)
        ev_st = dma(ST_SLAB, ag_in_gu.ap()[128 * q:128 * (q + 1), :],
                    asm2[par][:], prio_waits=[ev_ptq_free])
        ev_asm_free[par] = ev_st
        ev_slab.append(ev_st)
    if cut == "sgu":
        return cutout("sgu")
    slab(lambda H, kd: wfin_p.ap()[:, H, 128 * kd:128 * (kd + 1)],
         16, 8, lambda H: 128 * H, ag_in_p, ev_qp)
    if cut == "sp":
        return cutout("sp")

    if phases <= 0:
        drain_all()
        flush("phase0cut")
        s0.close()
        st.close()
        return nc

    waits("g", ev_slab)
    ev_ag_gu = CC.bump(1)
    emit("g", lambda e_: e_.collective_compute(
        "AllGather", ALU.bypass, replica_groups=[list(range(NCORES))],
        ins=[ag_in_gu.ap().opt()], outs=[ag_out_gu.ap().opt()]).then_inc(CC.h, 1))
    ev_ag_p = CC2.bump(1)
    emit("g", lambda e_: e_.collective_compute(
        "AllGather", ALU.bypass, replica_groups=[list(range(NCORES))],
        ins=[ag_in_p.ap().opt()], outs=[ag_out_p.ap().opt()]).then_inc(CC2.h, 1))

    if debug >= 1:
        evp = None
        for blk in range(64):
            evl = dma(ST_C, asm2[0][:], ag_out_gu.ap()[128 * blk:128 * (blk + 1), :],
                      prio_waits=[ev_ag_gu] + ([evp] if evp else []))
            evp = dma(ST_C, dbg["d_aggu"].rearrange("(n p) c -> p n c", p=128)
                      [:, blk:blk + 1, :].rearrange("p n (a b) -> p (n a) b", a=2),
                      asm2[0].ap().rearrange("p (a b) -> p a b", a=2),
                      prio_waits=[evl])
        for blk in range(64):
            evl = dma(ST_C, asm[0][:], ag_out_p.ap()[128 * blk:128 * (blk + 1), :],
                      prio_waits=[ev_ag_p, evp])
            evp = dma(ST_C, dbg["d_agp"][128 * blk:128 * (blk + 1), :], asm[0][:],
                      prio_waits=[evl])

    barrier()
    if phases <= 1:
        drain_all()
    flush("phase0")
    s0.close()
    if phases <= 1:
        st.close()
        return nc

    sMT = ExitStack()
    mt_sb = sMT.enter_context(nc.sbuf_tensor("mt_sb", [128, NKD, 1024], BF16))

    sA = ExitStack()
    eA = sA.enter_context
    wq_sb = eA(nc.sbuf_tensor("wq_sb", [128, NKD, 512], BF16))
    wk_sb = eA(nc.sbuf_tensor("wk_sb", [128, NKD, 512], BF16))
    wv_sb = eA(nc.sbuf_tensor("wv_sb", [128, NKD, 512], BF16))
    memz = eA(nc.sbuf_tensor("memz", [128, NH, 129], BF16))
    mem_f = eA(nc.sbuf_tensor("mem_f", [128, NH, 128], F32))
    z_f = eA(nc.sbuf_tensor("z_f", [128, NH], F32))
    xT = eA(nc.sbuf_tensor("xT", [128, NKD, 512], BF16))
    xseg = [eA(nc.sbuf_tensor(f"xseg{i}", [128, 4, 2048], BF16)) for i in range(2)]
    NPAR = 3
    qT_sb = [eA(nc.sbuf_tensor(f"qT{i}", [128, 512], BF16)) for i in range(NPAR)]
    kT_sb = [eA(nc.sbuf_tensor(f"kT{i}", [128, 512], BF16)) for i in range(NPAR)]
    sqT = [eA(nc.sbuf_tensor(f"sqT{i}", [128, 512], BF16)) for i in range(NPAR)]
    skT = [eA(nc.sbuf_tensor(f"skT{i}", [128, 512], BF16)) for i in range(NPAR)]
    skn = [eA(nc.sbuf_tensor(f"skn{i}", [128, 4, 128], BF16)) for i in range(NPAR)]
    tmpA = [eA(nc.sbuf_tensor("tmpA0", [128, 512], F32))] * NPAR
    tmpB = [eA(nc.sbuf_tensor("tmpB0", [128, 512], F32))] * NPAR
    etmp = [eA(nc.sbuf_tensor(f"etmp{i}", [128, 512], F32)) for i in range(2)]
    v_sb = [eA(nc.sbuf_tensor(f"v_sb{i}", [128, 4, 4 * 129], BF16))
            for i in range(2)]
    E_sb = [eA(nc.sbuf_tensor(f"E{i}", [128, 4, 512], BF16)) for i in range(NPAR)]
    att_d = [eA(nc.sbuf_tensor("attd0", [128, 128], F32))] * NPAR
    diff = [eA(nc.sbuf_tensor("diff0", [128, 128], F32))] * NPAR
    att_bf = [eA(nc.sbuf_tensor(f"attbf{i}", [128, 4, 128], BF16)) for i in range(NPAR)]
    delta = [eA(nc.sbuf_tensor(f"delta{i}", [128, 4, 128], BF16)) for i in range(NPAR)]
    rcol = [eA(nc.sbuf_tensor("rcol0", [128, 4], F32))] * NPAR
    pq = [eA(nc.psum_tensor(f"pq{i}", [128, 512], F32)) for i in range(2)]
    pk = [eA(nc.psum_tensor(f"pk{i}", [128, 512], F32)) for i in range(2)]
    psc = eA(nc.psum_tensor("psc", [128, 1024], F32))
    pnd = eA(nc.psum_tensor("pnd", [128, 512], F32))
    ptp = eA(nc.psum_tensor("ptpA", [128, 512], BF16))

    ev_wq = dma(ST_W, wq_sb.ap(), wq_d.rearrange("(k p) c -> p k c", p=128))
    ev_wk = dma(ST_W, wk_sb.ap(), wk_d.rearrange("(k p) c -> p k c", p=128))
    ev_wv = dma(ST_W, wv_sb.ap(), wv_d.rearrange("(k p) c -> p k c", p=128))

    op("v", lambda e_: e_.memset(mem_f.ap().rearrange("p h d -> p (h d)"), 0.0))
    op("v", lambda e_: e_.memset(z_f[:], Z0))
    op("v", lambda e_: e_.memset(memz.ap().rearrange("p h d -> p (h d)"), 0.0))
    for hh in range(NH):
        op("v", lambda e_, hh=hh: e_.memset(memz.ap()[:, hh, 128:129], Z0))
    for vpar in range(2):
        op("v", lambda e_, vpar=vpar:
           e_.memset(v_sb[vpar].ap().rearrange("p m c -> p (m c)"), 0.0))
        for m in range(4):
            for hh in range(NH):
                op("v", lambda e_, m=m, hh=hh, vpar=vpar:
                   e_.memset(v_sb[vpar].ap()[:, m, 129 * hh + 128:129 * hh + 129], 1.0))
    ev_vinit = op("v", lambda e_: e_.nop(), inc=True)

    for hh in range(NH):
        evb = op("t", lambda e_, hh=hh: e_.matmul(
            pq[0][:, 0:128], onesf[0:1, :], bsb[0:1, 128 * hh:128 * (hh + 1)],
            start=True, stop=True), inc=True,
            prio_waits=[ev_bsb, ev_ones] if hh == 0 else ())
        evs = op("c", lambda e_, hh=hh: e_.activation(
            beta_t.ap()[:, hh, :], pq[0][:, 0:128], AF.Sigmoid),
            inc=True, prio_waits=[evb])
        wait("t", evs)

    if cut == "beta":
        drain_all()
        flush("cutbeta")
        sA.close()
        sMT.close()
        st.close()
        return nc

    if noscr:
        scr = lambda apx: apx
    else:
        scr = lambda apx: apx.rearrange("p (t r) -> p r t", r=16)

    ev_xld = [None, None]
    ev_xT_users = None
    ev_psc_free = [None, None]
    ev_etmp_rd = [None, None]
    ev_pq_free = [None, None]
    ev_pk_free = [None, None]
    ev_pnd_free = None
    ev_ptp_free = None
    ev_mt_done = []
    ev_state = [None] * NH
    ev_par_free = [None] * NPAR
    H_evq = {}
    H_evq1 = {}
    H_evk1 = {}
    H_evsk = {}
    H_evskn = {}
    H_evE = {}
    H_evatt = {}

    ev_xld[0] = dma(ST_X, xseg[0].ap(),
                    x_b[0:SEG, :].rearrange("(m p) d -> p m d", p=128))

    ev_xseg_rd = [None, None]

    def st_xtp(i, par):
        nonlocal ev_ptp_free
        last_ev = None
        lastt = None
        for kd in range(NKD):
            for m in range(4):
                w8 = []
                if m == 0:
                    if kd == 0:
                        w8 = [ev_xld[par], ev_idn]
                        if ev_xT_users:
                            w8.append(ev_xT_users)
                    if ev_ptp_free:
                        w8.append(ev_ptp_free)
                lastt = op("t", lambda e_, m=m, kd=kd, par=par: e_.transpose(
                    ptp[:, 128 * m:128 * (m + 1)],
                    xseg[par].ap()[:, m, 128 * kd:128 * (kd + 1)], idn[:]),
                    inc=(m == 3), prio_waits=w8)
            last_ev = op("c", lambda e_, kd=kd: e_.activation(
                xT.ap()[:, kd, :], ptp.ap(), AF.Copy),
                inc=True, prio_waits=[lastt])
            ev_ptp_free = last_ev
        ev_xseg_rd[par] = lastt
        return last_ev

    ev_vseg_free = [None, None]

    def st_v(i, ev_xT_ready):
        vpar = i % 2
        ev = None
        for m in range(4):
            half = m % 2
            lastv = None
            for kd in range(NKD):
                w8 = []
                if kd == 0:
                    w8 = [ev_xT_ready, ev_wv, ev_vinit]
                    if ev_psc_free[half]:
                        w8.append(ev_psc_free[half])
                lastv = op("t", lambda e_, m=m, kd=kd, half=half: e_.matmul(
                    psc[:, 512 * half:512 * (half + 1)],
                    xT.ap()[:, kd, 128 * m:128 * (m + 1)],
                    wv_sb.ap()[:, kd, :],
                    start=(kd == 0), stop=(kd == NKD - 1)),
                    inc=(kd == NKD - 1), prio_waits=w8)
            w8e = [lastv]
            if m == 0 and ev_vseg_free[vpar]:
                w8e += ev_vseg_free[vpar]
            ev = op("c", lambda e_, m=m, half=half, vpar=vpar: e_.activation(
                v_sb[vpar].ap()[:, m, :]
                .rearrange("p (h c) -> p h c", c=129)[:, :, 0:128],
                psc.ap()[:, 512 * half:512 * (half + 1)]
                .rearrange("p (h c) -> p h c", c=128), AF.Copy),
                inc=True, prio_waits=w8e)
            ev_psc_free[half] = ev
        return ev

    def st_qk(i, gh, ev_xT_ready):
        hh = gh % 4
        par = gh % 2
        p3 = gh % NPAR
        w8 = [ev_xT_ready, ev_wq]
        if ev_pq_free[par]:
            w8.append(ev_pq_free[par])
        lastq = None
        for kd in range(NKD):
            lastq = op("t", lambda e_, hh=hh, kd=kd, par=par: e_.matmul(
                pq[par][:], wq_sb.ap()[:, kd, 128 * hh:128 * (hh + 1)],
                xT.ap()[:, kd, :],
                start=(kd == 0), stop=(kd == NKD - 1)),
                inc=(kd == NKD - 1), prio_waits=w8 if kd == 0 else ())
        w8 = [ev_wk]
        if ev_pk_free[par]:
            w8.append(ev_pk_free[par])
        lastk = None
        for kd in range(NKD):
            lastk = op("t", lambda e_, hh=hh, kd=kd, par=par: e_.matmul(
                pk[par][:], wk_sb.ap()[:, kd, 128 * hh:128 * (hh + 1)],
                xT.ap()[:, kd, :],
                start=(kd == 0), stop=(kd == NKD - 1)),
                inc=(kd == NKD - 1), prio_waits=w8 if kd == 0 else ())
        tA, tB = tmpA[p3], tmpB[p3]
        wfree = [ev_par_free[p3]] if ev_par_free[p3] else []
        ev_q1 = op("c", lambda e_, p3=p3, par=par: e_.activation(
            qT_sb[p3][:], scr(pq[par].ap()), AF.Copy), inc=True,
            prio_waits=[lastq] + wfree)
        ev_q2 = op("c", lambda e_, par=par: e_.activation(
            tB[:], scr(pq[par].ap()), AF.Relu), inc=True,
            prio_waits=[H_evsk[gh - 1]] if (gh - 1) in H_evsk else ())
        ev_mnq = op("v", lambda e_, par=par: e_.tensor_scalar_min(
            tA[:], scr(pq[par].ap()), 0.0), prio_waits=[lastq, ev_q2] + wfree)
        ev_q3 = op("c", lambda e_: e_.activation(
            tA[:], tA[:], AF.Exp), inc=True, prio_waits=[ev_mnq])
        ev_pq_free[par] = ev_q3
        ev_sq = op("v", lambda e_, p3=p3: e_.tensor_add(
            sqT[p3][:], tA[:], tB[:]), inc=True, prio_waits=[ev_q2, ev_q3])
        H_evq[gh] = ev_sq
        H_evq1[gh] = ev_q1
        ev_k1 = op("c", lambda e_, p3=p3, par=par: e_.activation(
            kT_sb[p3][:], pk[par].ap(), AF.Copy), inc=True, prio_waits=[lastk])
        ev_k2 = op("c", lambda e_, par=par: e_.activation(
            tB[:], pk[par].ap(), AF.Relu), inc=True, prio_waits=[ev_sq])
        ev_mnk = op("v", lambda e_, par=par: e_.tensor_scalar_min(
            tA[:], pk[par].ap(), 0.0), prio_waits=[lastk, ev_k2])
        ev_k3 = op("c", lambda e_: e_.activation(
            tA[:], tA[:], AF.Exp), inc=True, prio_waits=[ev_mnk])
        ev_pk_free[par] = ev_k3
        ev_sk = op("v", lambda e_, p3=p3: e_.tensor_add(
            skT[p3][:], tA[:], tB[:]), inc=True, prio_waits=[ev_k2, ev_k3])
        H_evsk[gh] = ev_sk
        H_evk1[gh] = ev_k1

    def st_sktp_sc(i, gh):
        nonlocal ev_ptp_free
        p3 = gh % NPAR
        wfree = [ev_par_free[p3]] if ev_par_free[p3] else []
        lastt = None
        for mt in range(4):
            w8 = []
            if mt == 0:
                w8 = [H_evsk[gh]]
                if ev_ptp_free:
                    w8.append(ev_ptp_free)
            lastt = op("t", lambda e_, mt=mt, p3=p3: e_.transpose(
                ptp[:, 128 * mt:128 * (mt + 1)],
                skT[p3].ap()[:, 128 * mt:128 * (mt + 1)], idn[:]),
                inc=(mt == 3), prio_waits=w8)
        ev_ptp_free = op("c", lambda e_, p3=p3: e_.activation(
            skn[p3].ap().rearrange("p m c -> p (m c)"), ptp.ap(), AF.Copy),
            inc=True, prio_waits=[lastt] + wfree)
        H_evskn[gh] = ev_ptp_free
        evE = []
        lastsc = None
        for mt in range(4):
            half = mt % 2
            w8 = [H_evq1[gh], H_evk1[gh]] if mt == 0 else []
            if ev_psc_free[half]:
                w8.append(ev_psc_free[half])
            lastsc = op("t", lambda e_, mt=mt, p3=p3, half=half: e_.matmul(
                psc[:, 512 * half:512 * (half + 1)],
                kT_sb[p3][:, 128 * mt:128 * (mt + 1)], qT_sb[p3][:],
                start=True, stop=True), inc=True, prio_waits=w8)
            buf = etmp[half]
            w8b = [lastsc, ev_mask]
            if ev_etmp_rd[half]:
                w8b.append(ev_etmp_rd[half])
            evx = op("c", lambda e_, mt=mt, buf=buf, half=half: e_.activation(
                buf[:], psc.ap()[:, 512 * half:512 * (half + 1)],
                AF.Exp, scale=INV_SCALE), inc=True, prio_waits=w8b)
            ev_psc_free[half] = evx
            w8c = list(wfree) if mt == 0 else []
            evm = op("v", lambda e_, mt=mt, buf=buf, p3=p3: e_.tensor_mul(
                E_sb[p3].ap()[:, mt, :], buf[:], maskT.ap()[:, mt, :]),
                inc=True, prio_waits=[evx] + w8c)
            ev_etmp_rd[half] = evm
            evE.append(evm)
        H_evE[gh] = evE

    def st_nd(i, gh):
        nonlocal ev_pnd_free
        hh = gh % 4
        p3 = gh % NPAR
        rc = rcol[p3]
        ad = att_d[p3]
        df = diff[p3]
        ev_cmb_all = None
        for m in range(4):
            w8 = [H_evE[gh][-1], ev_v_ready[i]]
            if ev_pnd_free:
                w8.append(ev_pnd_free)
            for mt in range(4):
                op("t", lambda e_, m=m, mt=mt, hh=hh, p3=p3, vp=i % 2: e_.matmul(
                    pnd[:, 0:129],
                    E_sb[p3].ap()[:, mt, 128 * m:128 * (m + 1)],
                    v_sb[vp].ap()[:, mt, 129 * hh:129 * (hh + 1)],
                    start=(mt == 0), stop=(mt == 3)),
                    prio_waits=w8 if mt == 0 else ())
            w8n = [H_evq[gh]]
            if ev_state[hh]:
                w8n.append(ev_state[hh])
            ev_nd = op("t", lambda e_, m=m, hh=hh, p3=p3: e_.matmul(
                pnd[:, 256:385],
                sqT[p3][:, 128 * m:128 * (m + 1)], memz.ap()[:, hh, :],
                start=True, stop=True), inc=True, prio_waits=w8n)
            ev_r1 = op("v", lambda e_, rc=rc: e_.reciprocal(
                rc[:, 0:1], pnd[:, 128:129]), inc=True, prio_waits=[ev_nd])
            ev_r2 = op("v", lambda e_, rc=rc: e_.reciprocal(
                rc[:, 1:2], pnd[:, 384:385]), inc=True)
            op("v", lambda e_, rc=rc, ad=ad: e_.tensor_scalar_mul(
                ad[:], pnd[:, 0:128], rc[:, 0:1]), prio_waits=[ev_r1, ev_r2])
            op("v", lambda e_, rc=rc, ad=ad, df=df: e_.scalar_tensor_tensor(
                df[:], pnd[:, 256:384], rc[:, 1:2], ad[:],
                ALU.mult, ALU.subtract))
            op("v", lambda e_, hh=hh, df=df: e_.tensor_mul(
                df[:], df[:], beta_t.ap()[:, hh, :]))
            ev_cmb = op("v", lambda e_, m=m, p3=p3, ad=ad, df=df: e_.tensor_add(
                att_bf[p3].ap()[:, m, :], ad[:], df[:]), inc=True)
            ev_cmb_all = ev_cmb
            ev_ak = op("t", lambda e_, m=m, hh=hh, p3=p3: e_.matmul(
                pnd[:, 256:385],
                skT[p3][:, 128 * m:128 * (m + 1)], memz.ap()[:, hh, :],
                start=True, stop=True), inc=True,
                prio_waits=[H_evsk[gh], ev_cmb])
            ev_r3 = op("v", lambda e_, rc=rc: e_.reciprocal(
                rc[:, 2:3], pnd[:, 384:385]), inc=True, prio_waits=[ev_ak])
            ev_dl = op("v", lambda e_, m=m, hh=hh, p3=p3, rc=rc, vp=i % 2:
                       e_.scalar_tensor_tensor(
                delta[p3].ap()[:, m, :], pnd[:, 256:384], rc[:, 2:3],
                v_sb[vp].ap()[:, m, 129 * hh:129 * hh + 128],
                ALU.mult, ALU.subtract), inc=True, prio_waits=[ev_r3])
            ev_pnd_free = ev_dl
        H_evatt[gh] = ev_cmb_all
        lastdm = None
        for mt in range(4):
            w8 = [H_evskn[gh], ev_pnd_free] if mt == 0 else []
            lastdm = op("t", lambda e_, mt=mt, p3=p3: e_.matmul(
                pnd[:, 0:128], skn[p3].ap()[:, mt, :],
                delta[p3].ap()[:, mt, :],
                start=(mt == 0), stop=(mt == 3)),
                inc=(mt == 3), prio_waits=w8)
        op("v", lambda e_, hh=hh: e_.tensor_sub(
            mem_f.ap()[:, hh, :], mem_f.ap()[:, hh, :], pnd[:, 0:128]),
            prio_waits=[lastdm])
        ev_mcp = op("v", lambda e_, hh=hh: e_.tensor_copy(
            memz.ap()[:, hh, 0:128], mem_f.ap()[:, hh, :]), inc=True)
        ev_pnd_free = ev_mcp
        ev_zr = op("v", lambda e_, p3=p3, rc=rc: e_.tensor_reduce(
            rc.ap()[:, 3:4], skT[p3].ap(), AXX, ALU.add),
            inc=True, prio_waits=[H_evsk[gh]])
        op("v", lambda e_, hh=hh, rc=rc: e_.tensor_add(
            z_f.ap()[:, hh:hh + 1], z_f.ap()[:, hh:hh + 1], rc[:, 3:4]),
            prio_waits=[ev_zr])
        ev_state[hh] = op("v", lambda e_, hh=hh: e_.tensor_copy(
            memz.ap()[:, hh, 128:129], z_f.ap()[:, hh:hh + 1]), inc=True)
        return lastdm

    def st_atttp(i, gh, ev_dmdone):
        nonlocal ev_ptp_free
        hh = gh % 4
        p3 = gh % NPAR
        lastat = None
        for m in range(4):
            w8 = []
            if m == 0:
                w8 = [H_evatt[gh]]
                if ev_ptp_free:
                    w8.append(ev_ptp_free)
            lastat = op("t", lambda e_, m=m, p3=p3: e_.transpose(
                ptp[:, 128 * m:128 * (m + 1)], att_bf[p3].ap()[:, m, :],
                idn[:]), inc=(m == 3), prio_waits=w8)
        ev_par_free[p3] = lastat
        ev_mtc = op("c", lambda e_, hh=hh, i=i: e_.activation(
            mt_sb.ap()[:, :, 128 * i + 32 * hh:128 * i + 32 * hh + 32]
            .rearrange("p (m r) t -> p m r t", m=4),
            ptp.ap().rearrange("p (m r t) -> p m r t", m=4, r=4), AF.Copy),
            inc=True, prio_waits=[lastat])
        ev_ptp_free = ev_mtc
        return ev_mtc

    ev_v_ready = {}
    for i in range(NSEG):
        par = i % 2
        if i + 1 < NSEG:
            ev_xld[1 - par] = dma(
                ST_X, xseg[1 - par].ap(),
                x_b[SEG * (i + 1):SEG * (i + 2), :].rearrange("(m p) d -> p m d", p=128),
                prio_waits=[ev_xseg_rd[1 - par]] if ev_xseg_rd[1 - par] else ())
        if i == 0:
            ev_xT_ready = st_xtp(0, 0)
            ev_v_ready[0] = st_v(0, ev_xT_ready)
        g0 = 4 * i
        st_qk(i, g0 + 0, ev_xT_ready)
        st_qk(i, g0 + 1, ev_xT_ready)
        st_sktp_sc(i, g0 + 0)
        st_qk(i, g0 + 2, ev_xT_ready)
        dm0 = st_nd(i, g0 + 0)
        st_atttp(i, g0 + 0, dm0)
        st_sktp_sc(i, g0 + 1)
        st_qk(i, g0 + 3, ev_xT_ready)
        dm1 = st_nd(i, g0 + 1)
        st_atttp(i, g0 + 1, dm1)
        st_sktp_sc(i, g0 + 2)
        dm2 = st_nd(i, g0 + 2)
        st_atttp(i, g0 + 2, dm2)
        st_sktp_sc(i, g0 + 3)
        if i + 1 < NSEG:
            ev_xT_users = t_event()
            ev_xT_ready = st_xtp(i + 1, 1 - par)
            ev_v_ready[i + 1] = st_v(i + 1, ev_xT_ready)
        dm3 = st_nd(i, g0 + 3)
        ev_vseg_free[i % 2] = [dm3, ev_state[3]]
        ev_mt3 = st_atttp(i, g0 + 3, dm3)
        ev_mt_done.append(ev_mt3)
        if cut == f"seg{i}":
            drain_all()
            flush("cutseg")
            sA.close()
            sMT.close()
            st.close()
            return nc

    if debug >= 2:
        evp = None
        for r in range(NKD):
            evp = dma(ST_C, dbg["d_mt"][128 * r:128 * (r + 1), :],
                      mt_sb.ap()[:, r, :],
                      prio_waits=[ev_mt_done[-1]] + ([evp] if evp else []))
    barrier()
    if phases <= 2:
        drain_all()
    flush("phaseA")
    sA.close()
    if phases <= 2:
        sMT.close()
        st.close()
        return nc

    y_sb = en(nc.sbuf_tensor("y_sb", [128, NSEG, 2048], F32, side="right"))
    sA2 = ExitStack()
    e2 = sA2.enter_context
    wo_sb = e2(nc.sbuf_tensor("wo_sb", [128, NKD, 2048], BF16))
    xres_b = [e2(nc.sbuf_tensor(f"xres{i}", [128, 2048], F32)) for i in range(2)]
    pwo = e2(nc.psum_tensor("pwo", [128, 2048], F32))

    ev_wo = None
    for part in range(4):
        ev_wo = dma(ST_W, wo_sb.ap()[:, 4 * part:4 * (part + 1), :],
                    wo_d[512 * part:512 * (part + 1), :]
                    .rearrange("(k p) c -> p k c", p=128))

    ev_xres = [None, None]
    ev_xres_free = [None, None]
    ev_pwo_free = None
    ev_zsb_free = None

    ev_xres[0] = dma(ST_XR, xres_b[0][:], x_res[0:128, :])
    for i in range(NSEG):
        par = i % 2
        if i + 1 < NSEG:
            ev_xres[1 - par] = dma(
                ST_XR, xres_b[1 - par][:], x_res[128 * (i + 1):128 * (i + 2), :],
                prio_waits=[ev_xres_free[1 - par]] if ev_xres_free[1 - par] else ())
        lastw = None
        for rch in range(NKD):
            for nb in range(4):
                w8 = []
                if rch == 0 and nb == 0:
                    w8 = [ev_mt_done[i], ev_wo]
                    if ev_pwo_free:
                        w8.append(ev_pwo_free)
                lastw = op("t", lambda e_, i=i, rch=rch, nb=nb: e_.matmul(
                    pwo[:, 512 * nb:512 * (nb + 1)],
                    mt_sb.ap()[:, rch, 128 * i:128 * (i + 1)],
                    wo_sb.ap()[:, rch, 512 * nb:512 * (nb + 1)],
                    start=(rch == 0), stop=(rch == NKD - 1)),
                    inc=(rch == NKD - 1 and nb == 3), prio_waits=w8)
        w8 = [lastw, ev_xres[par]]
        if ev_zsb_free:
            w8.append(ev_zsb_free)
        op("v", lambda e_, par=par: e_.tensor_add(zsb[:], pwo.ap(), xres_b[par][:]),
           prio_waits=w8)
        ev_z = op("v", lambda e_: e_.nop(), inc=True)
        ev_pwo_free = ev_z
        ev_xres_free[par] = ev_z
        ev_sum = op("v", lambda e_: e_.tensor_reduce(
            red4.ap()[:, 0:1], zsb.ap(), AXX, ALU.add), inc=True, prio_waits=[ev_z])
        op("c", lambda e_: e_.activation(
            sqs[:], zsb[:, 0:1024], AF.Square, accum_out=red4.ap()[:, 1:2]),
            prio_waits=[ev_z])
        ev_sq2 = op("c", lambda e_: e_.activation(
            sqs[:], zsb[:, 1024:2048], AF.Square, accum_out=red4.ap()[:, 2:3]),
            inc=True)
        op("v", lambda e_: e_.tensor_scalar_mul(red4[:, 0:1], red4[:, 0:1],
                                                -1.0 / 2048.0), prio_waits=[ev_sum])
        op("v", lambda e_: e_.tensor_add(red4[:, 1:2], red4[:, 1:2],
                                         red4[:, 2:3]), prio_waits=[ev_sq2])
        op("v", lambda e_: e_.tensor_scalar_mul(red4[:, 1:2], red4[:, 1:2],
                                                1.0 / 2048.0))
        op("v", lambda e_: e_.tensor_tensor(red4[:, 2:3], red4[:, 0:1],
                                            red4[:, 0:1], ALU.mult))
        op("v", lambda e_: e_.tensor_sub(red4[:, 1:2], red4[:, 1:2], red4[:, 2:3]))
        op("v", lambda e_: e_.tensor_scalar_add(red4[:, 1:2], red4[:, 1:2], EPS_LN))
        ev_rc = op("v", lambda e_: e_.reciprocal(red4[:, 2:3], red4[:, 1:2]),
                   inc=True)
        ev_st2 = op("c", lambda e_: e_.activation(red4.ap()[:, 3:4],
                                                  red4.ap()[:, 2:3], AF.Sqrt),
                    inc=True, prio_waits=[ev_rc])
        ev_y = op("v", lambda e_, i=i: e_.tensor_scalar(
            y_sb.ap()[:, i, :], zsb[:], red4.ap()[:, 0:1], red4.ap()[:, 3:4],
            ALU.add, ALU.mult), inc=True, prio_waits=[ev_st2])
        ev_zsb_free = ev_y

    if debug >= 2:
        evp = None
        for i in range(NSEG):
            evp = dma(ST_C, dbg["d_y"][128 * i:128 * (i + 1), :],
                      y_sb.ap()[:, i, :],
                      prio_waits=[ev_zsb_free] + ([evp] if evp else []))

    barrier()
    if phases <= 3:
        drain_all()
    flush("phaseA2")
    sA2.close()
    sMT.close()
    if phases <= 3:
        st.close()
        return nc

    sB = ExitStack()
    eB = sB.enter_context
    yTh = eB(nc.sbuf_tensor("yTh", [128, NKD, 512], BF16))
    ybc = eB(nc.sbuf_tensor("ybc", [128, 2048], BF16))
    slu = [eB(nc.sbuf_tensor(f"slu{i}", [128, 512], BF16)) for i in range(2)]
    h_sb = eB(nc.sbuf_tensor("h_sb", [128, 64, 512], BF16))
    wg_buf = [eB(nc.sbuf_tensor(f"wg{i}", [128, NKD, 256], BF16)) for i in range(2)]
    wp_buf = [eB(nc.sbuf_tensor(f"wpb{i}", [128, 2, 2048], BF16)) for i in range(2)]
    inv4 = eB(nc.sbuf_tensor("inv4", [128, 8], F32))
    g1s = eB(nc.sbuf_tensor("g1s", [128, 128], F32))

    ev_ybc_free = None
    ev_ptB_free = None
    ev_wg_free = [None, None]
    ev_wp_free = [None, None]
    ev_pg_free = [None, None, None]
    ev_slu_rd = [None, None]
    ev_yT = None
    ev_h_done = None
    ev_inv = None

    def phaseB_batch(bb):
        nonlocal ev_ybc_free, ev_ptB_free, ev_yT, ev_h_done, ev_inv, ev_zsb_free
        with (nc.psum_tensor(f"ptB{bb}", [128, 512], BF16) as ptB,
              nc.psum_tensor(f"pg0{bb}", [128, 512], F32) as pg0,
              nc.psum_tensor(f"pg1{bb}", [128, 512], F32) as pg1,
              nc.psum_tensor(f"pg2{bb}", [128, 512], F32) as pg2,
              nc.psum_tensor(f"pu0{bb}", [128, 512], F32) as pu0,
              nc.psum_tensor(f"pu1{bb}", [128, 512], F32) as pu1,
              nc.psum_tensor(f"pu2{bb}", [128, 512], F32) as pu2,
              nc.psum_tensor(f"pG{bb}", [128, 128], F32) as pG):
            pg = [pg0, pg1, pg2]
            pu = [pu0, pu1, pu2]
            for m in range(4):
                i = 4 * bb + m
                w8 = [ev_ybc_free] if ev_ybc_free else []
                ev_cast = op("c", lambda e_, i=i: e_.activation(
                    ybc[:], y_sb.ap()[:, i, :], AF.Copy), inc=True, prio_waits=w8)
                for kq in range(4):
                    lastt = None
                    for kj in range(4):
                        kd = 4 * kq + kj
                        w8t = []
                        if kj == 0:
                            w8t = [ev_cast]
                            if ev_ptB_free:
                                w8t.append(ev_ptB_free)
                        lastt = op("t", lambda e_, kd=kd, kj=kj, ptB=ptB:
                                   e_.transpose(
                                       ptB[:, 128 * kj:128 * (kj + 1)],
                                       ybc.ap()[:, 128 * kd:128 * (kd + 1)],
                                       idn[:]),
                                   inc=(kj == 3), prio_waits=w8t)
                    ev_ptB_free = op("v", lambda e_, kq=kq, m=m, ptB=ptB:
                                     e_.tensor_copy(
                        yTh.ap()[:, 4 * kq:4 * (kq + 1), 128 * m:128 * (m + 1)],
                        ptB.ap().rearrange("p (k c) -> p k c", k=4)),
                        inc=True, prio_waits=[lastt])
                ev_ybc_free = ev_ptB_free
            ev_yT = ev_ptB_free

            ev_wg = [None, None]
            w8 = [ev_ag_gu] if bb == 0 else []
            if ev_wg_free[0]:
                w8.append(ev_wg_free[0])
            ev_wg[0] = dma(ST_WG, wg_buf[0].ap().rearrange("p k c -> p (k c)"),
                           ag_out_gu.ap()[0:128, :], prio_waits=w8)
            for jp in range(64):
                par = jp % 3
                wb = jp % 2
                if jp + 1 < 64:
                    r = (jp + 1) // 8
                    q = (jp + 1) % 8
                    ev_wg[1 - wb] = dma(
                        ST_WG, wg_buf[1 - wb].ap().rearrange("p k c -> p (k c)"),
                        ag_out_gu.ap()[1024 * r + 128 * q:1024 * r + 128 * (q + 1), :],
                        prio_waits=[ev_wg_free[1 - wb]]
                        if ev_wg_free[1 - wb] else ())
                lastg = None
                for kd in range(NKD):
                    w8 = []
                    if kd == 0:
                        w8 = [ev_wg[wb], ev_yT]
                        if ev_pg_free[par]:
                            w8.append(ev_pg_free[par])
                    op("t", lambda e_, kd=kd, par=par, wb=wb: e_.matmul(
                        pg[par][:], wg_buf[wb].ap()[:, kd, 0:128],
                        yTh.ap()[:, kd, :],
                        start=(kd == 0), stop=(kd == NKD - 1)), prio_waits=w8)
                    lastg = op("t", lambda e_, kd=kd, par=par, wb=wb: e_.matmul(
                        pu[par][:], wg_buf[wb].ap()[:, kd, 128:256],
                        yTh.ap()[:, kd, :],
                        start=(kd == 0), stop=(kd == NKD - 1)),
                        inc=(kd == NKD - 1))
                ev_wg_free[wb] = lastg
                ev_sl = op("c", lambda e_, par=par, wb=wb: e_.activation(
                    slu[wb][:], pg[par][:], AF.Silu), inc=True,
                    prio_waits=[lastg] +
                    ([ev_pg_free[par]] if ev_pg_free[par] else []) +
                    ([ev_slu_rd[wb]] if ev_slu_rd[wb] else []))
                ev_h = op("v", lambda e_, jp=jp, par=par, wb=wb: e_.tensor_tensor(
                    h_sb.ap()[:, jp, :], slu[wb][:], pu[par][:], ALU.mult),
                    inc=True, prio_waits=[ev_sl])
                ev_pg_free[par] = ev_h
                ev_slu_rd[wb] = ev_h
                ev_h_done = ev_h
            wait("t", ev_pg_free[0])
            wait("t", ev_pg_free[1])
            wait("t", ev_pg_free[2])

            ev_gr = None
            for m in range(4):
                lastG = None
                for k in range(64):
                    w8 = [ev_h_done] if k == 0 else []
                    lastG = op("t", lambda e_, m=m, k=k: e_.matmul(
                        pG[:], h_sb.ap()[:, k, 128 * m:128 * (m + 1)],
                        h_sb.ap()[:, k, 128 * m:128 * (m + 1)],
                        start=(k == 0), stop=(k == 63)),
                        inc=(k == 63), prio_waits=w8)
                op("v", lambda e_: e_.tensor_tensor(g1s[:], pG.ap(), idnf[:],
                                                    ALU.mult),
                   prio_waits=[lastG, ev_idnf])
                ev_gr = op("v", lambda e_, m=m, bb=bb: e_.tensor_reduce(
                    inv4.ap()[:, 4 * bb + m:4 * bb + m + 1], g1s.ap(), AXX,
                    ALU.add), inc=True)
                wait("t", ev_gr)
                op("v", lambda e_, m=m, bb=bb: e_.tensor_scalar(
                    inv4[:, 4 * bb + m:4 * bb + m + 1],
                    inv4[:, 4 * bb + m:4 * bb + m + 1],
                    1.0 / 8192.0, EPS_RMS, ALU.mult, ALU.add),
                    prio_waits=[ev_gr])
                ev_rcp = op("v", lambda e_, m=m, bb=bb: e_.reciprocal(
                    inv4[:, 4 * bb + m:4 * bb + m + 1],
                    inv4[:, 4 * bb + m:4 * bb + m + 1]), inc=True)
                op("c", lambda e_, m=m, bb=bb: e_.activation(
                    inv4.ap()[:, 4 * bb + m:4 * bb + m + 1],
                    inv4.ap()[:, 4 * bb + m:4 * bb + m + 1], AF.Sqrt),
                    prio_waits=[ev_rcp])
            ev_inv = op("c", lambda e_: e_.nop(), inc=True)
            ev_b1v = op("v", lambda e_: e_.nop(), inc=True)
        return ev_b1v

    def phaseB_proj(bb, ev_b1v):
        nonlocal ev_zsb_free, ev_ptB_free
        with (nc.psum_tensor(f"po0{bb}", [128, 2048], F32) as po0,
              nc.psum_tensor(f"po1{bb}", [128, 2048], F32) as po1):
            po = [po0, po1]
            for mp in range(2):
                def wp_load(kk, par, extra=()):
                    e1 = dma(ST_WP,
                             wp_buf[par].ap()[:, 0, :],
                             ag_out_p.ap()[256 * kk:256 * kk + 128, :]
                             .rearrange("(k p) c -> p (k c)", p=128),
                             prio_waits=list(extra))
                    e2 = dma(ST_WP2,
                             wp_buf[par].ap()[:, 1, :],
                             ag_out_p.ap()[256 * kk + 128:256 * (kk + 1), :]
                             .rearrange("(k p) c -> p (k c)", p=128),
                             prio_waits=list(extra))
                    return (e1, e2)

                ev_wp = [None, None]
                w8 = [ev_ag_p] if (bb == 0 and mp == 0) else []
                if ev_wp_free[0]:
                    w8.append(ev_wp_free[0])
                ev_wp[0] = wp_load(0, 0, extra=w8)
                lastp = None
                for kk in range(32):
                    par = kk % 2
                    if kk + 1 < 32:
                        ev_wp[1 - par] = wp_load(
                            kk + 1, 1 - par,
                            extra=[ev_wp_free[1 - par]]
                            if ev_wp_free[1 - par] else ())
                    for k2 in range(2):
                        k = 2 * kk + k2
                        for mm in range(2):
                            m = 2 * mp + mm
                            for nb in range(4):
                                w8 = []
                                if k2 == 0 and mm == 0 and nb == 0:
                                    w8 = [ev_wp[par][0], ev_wp[par][1],
                                          ev_h_done]
                                    if k == 0:
                                        w8 += [ev_b1v, ev_inv]
                                evmm = op("t", lambda e_, k=k, k2=k2, m=m,
                                          nb=nb, mm=mm, par=par: e_.matmul(
                                    po[mm][:, 512 * nb:512 * (nb + 1)],
                                    h_sb.ap()[:, k, 128 * m:128 * (m + 1)],
                                    wp_buf[par].ap()[:, k2,
                                                     512 * nb:512 * (nb + 1)],
                                    start=(k == 0), stop=(k == 63)),
                                    inc=(k2 == 1 and mm == 1 and nb == 3),
                                    prio_waits=w8)
                                if evmm is not None:
                                    lastp = evmm
                    ev_wp_free[par] = lastp
                for mm in range(2):
                    m = 2 * mp + mm
                    i = 4 * bb + m
                    op("v", lambda e_, mm=mm, m=m, i=i, bb=bb:
                       e_.scalar_tensor_tensor(
                           zsb[:], po[mm].ap(),
                           inv4.ap()[:, 4 * bb + m:4 * bb + m + 1],
                           y_sb.ap()[:, i, :], ALU.mult, ALU.add),
                       prio_waits=[lastp, ev_inv] +
                       ([ev_zsb_free] if ev_zsb_free else []))
                    ev_z = op("v", lambda e_: e_.nop(), inc=True)
                    ev_sum = op("v", lambda e_: e_.tensor_reduce(
                        red4.ap()[:, 0:1], zsb.ap(), AXX, ALU.add), inc=True)
                    op("c", lambda e_: e_.activation(
                        sqs[:], zsb[:, 0:1024], AF.Square,
                        accum_out=red4.ap()[:, 1:2]), prio_waits=[ev_z])
                    ev_sq2 = op("c", lambda e_: e_.activation(
                        sqs[:], zsb[:, 1024:2048], AF.Square,
                        accum_out=red4.ap()[:, 2:3]), inc=True)
                    op("v", lambda e_: e_.tensor_scalar_mul(
                        red4[:, 0:1], red4[:, 0:1], -1.0 / 2048.0),
                        prio_waits=[ev_sum])
                    op("v", lambda e_: e_.tensor_add(
                        red4[:, 1:2], red4[:, 1:2], red4[:, 2:3]),
                        prio_waits=[ev_sq2])
                    op("v", lambda e_: e_.tensor_scalar_mul(
                        red4[:, 1:2], red4[:, 1:2], 1.0 / 2048.0))
                    op("v", lambda e_: e_.tensor_tensor(
                        red4[:, 2:3], red4[:, 0:1], red4[:, 0:1], ALU.mult))
                    op("v", lambda e_: e_.tensor_sub(
                        red4[:, 1:2], red4[:, 1:2], red4[:, 2:3]))
                    op("v", lambda e_: e_.tensor_scalar_add(
                        red4[:, 1:2], red4[:, 1:2], EPS_LN))
                    ev_rc = op("v", lambda e_: e_.reciprocal(
                        red4[:, 2:3], red4[:, 1:2]), inc=True)
                    ev_st2 = op("c", lambda e_: e_.activation(
                        red4.ap()[:, 3:4], red4.ap()[:, 2:3], AF.Sqrt),
                        inc=True, prio_waits=[ev_rc])
                    ev_o = op("v", lambda e_: e_.tensor_scalar(
                        zsb[:], zsb[:], red4.ap()[:, 0:1], red4.ap()[:, 3:4],
                        ALU.add, ALU.mult), inc=True, prio_waits=[ev_st2])
                    ev_zsb_free = ev_o
                    ev_store = dma(ST_O, o_rows[128 * i:128 * (i + 1), :],
                                   zsb[:], prio_waits=[ev_o])
                    wait("v", ev_store)
                wait("t", ev_zsb_free)
        ev_ptB_free = ev_zsb_free

    for bb in range(2):
        ev_b1v = phaseB_batch(bb)
        phaseB_proj(bb, ev_b1v)

    drain_all()
    flush("phaseB")
    sB.close()
    st.close()
    return nc


def _host_inputs(L0_attn_x, L0_attn_Wq, L0_attn_Wk, L0_attn_Wv, L0_attn_Wo,
                 L0_attn_betas, L0_ffn_w_gate_up, L0_ffn_w_proj):
    x = np.asarray(L0_attn_x)
    Wq = np.asarray(L0_attn_Wq); Wk = np.asarray(L0_attn_Wk)
    Wv = np.asarray(L0_attn_Wv); Wo = np.asarray(L0_attn_Wo)
    betas = np.asarray(L0_attn_betas).reshape(16, 128)
    Wgu = np.asarray(L0_ffn_w_gate_up); Wp = np.asarray(L0_ffn_w_proj)

    u = np.arange(SEG)
    s_q = 16 * (u % 32) + u // 32
    t = np.arange(SEG)[:, None]
    maskT = (t <= s_q[None, :]).astype(bf16)
    idn = np.eye(128, dtype=np.float32)

    in_maps = []
    for c in range(NCORES):
        b, g = c // 4, c % 4
        xb = np.ascontiguousarray(x[b])
        xres = np.ascontiguousarray(
            xb.reshape(NSEG, 4, 128, D)[:, g].reshape(1024, D))
        wgu_sl = np.ascontiguousarray(
            np.concatenate([Wgu[1024 * c:1024 * (c + 1)],
                            Wgu[HFF + 1024 * c:HFF + 1024 * (c + 1)]], 0))
        wp_sl = np.ascontiguousarray(Wp[:, 1024 * c:1024 * (c + 1)])
        in_maps.append(dict(
            x_b=xb,
            x_res=xres,
            wq_d=np.ascontiguousarray(Wq[:, 512 * g:512 * (g + 1)]),
            wk_d=np.ascontiguousarray(Wk[:, 512 * g:512 * (g + 1)]),
            wv_d=np.ascontiguousarray(Wv[:, 512 * g:512 * (g + 1)]),
            wo_d=Wo,
            betas_d=np.ascontiguousarray(betas[4 * g:4 * (g + 1)].reshape(1, 512)),
            wgu_d=wgu_sl,
            wp_d=wp_sl,
            mask_d=maskT,
            idn_d=idn.astype(bf16),
            idnf_d=idn,
        ))
    return in_maps


def kernel(L0_attn_x, L0_attn_Wq, L0_attn_Wk, L0_attn_Wv, L0_attn_Wo,
           L0_attn_betas, L0_ffn_w_gate_up, L0_ffn_w_proj):
    in_maps = _host_inputs(L0_attn_x, L0_attn_Wq, L0_attn_Wk, L0_attn_Wv,
                           L0_attn_Wo, L0_attn_betas, L0_ffn_w_gate_up,
                           L0_ffn_w_proj)
    nc = _build(debug=0)
    res = run_bass_kernel_spmd(nc, in_maps, core_ids=list(range(NCORES)))
    out = np.empty((2, SEQ, D), dtype=np.float32)
    for c in range(NCORES):
        b, g = c // 4, c % 4
        rows = np.asarray(res.results[c]["o_rows"]).reshape(NSEG, 128, D)
        out.reshape(2, NSEG, 4, 128, D)[b, :, g] = rows
    return out


# version 08 (final)
# speedup vs baseline: 1.2500x; 1.0106x over previous
import math
import numpy as np
import ml_dtypes
from contextlib import ExitStack
import concourse.bass as bass
import concourse.mybir as mybir
from concourse.bass_utils import run_bass_kernel_spmd

F32 = mybir.dt.float32
BF16 = mybir.dt.bfloat16
I16 = mybir.dt.int16
I32 = mybir.dt.int32
AF = mybir.ActivationFunctionType
ALU = mybir.AluOpType
AXX = mybir.AxisListType.X
bf16 = ml_dtypes.bfloat16

D = 2048
SEQ = 4096
SEG = 512
NSEG = 8
NH = 4
HFF = 8192
NKD = D // 128
EPS_LN = 1e-5
EPS_RMS = 1.1920929e-07
THR = 0.5009765625
C1E8 = float(np.float32(bf16(1e-8)))
INV_SCALE = float(np.float32(1.0) / np.float32(math.sqrt(128.0)))
Z0 = 1.0 / 128.0
NCORES = 8


class Ctr:
    def __init__(self, name):
        self.name = name
        self.h = None
        self.n = 0

    def bump(self, k=1):
        self.n += k
        return (self, self.n)


def _build(debug=0, phases=9, cut=None, noscr=False, noexp=False):
    nc = bass.Bass("TRN2", target_bir_lowering=False)

    x_b = nc.dram_tensor("x_b", [SEQ, D], F32, kind="ExternalInput").ap()
    x_res = nc.dram_tensor("x_res", [1024, D], F32, kind="ExternalInput").ap()
    wq_d = nc.dram_tensor("wq_d", [D, 512], F32, kind="ExternalInput").ap()
    wk_d = nc.dram_tensor("wk_d", [D, 512], F32, kind="ExternalInput").ap()
    wv_d = nc.dram_tensor("wv_d", [D, 512], F32, kind="ExternalInput").ap()
    wo_d = nc.dram_tensor("wo_d", [D, D], F32, kind="ExternalInput").ap()
    betas_d = nc.dram_tensor("betas_d", [1, 512], F32, kind="ExternalInput").ap()
    wgu_d = nc.dram_tensor("wgu_d", [2048, D], F32, kind="ExternalInput").ap()
    wp_d = nc.dram_tensor("wp_d", [D, 1024], F32, kind="ExternalInput").ap()
    mask_d = nc.dram_tensor("mask_d", [SEG, SEG], BF16, kind="ExternalInput").ap()
    idn_d = nc.dram_tensor("idn_d", [128, 128], BF16, kind="ExternalInput").ap()
    idnf_d = nc.dram_tensor("idnf_d", [128, 128], F32, kind="ExternalInput").ap()
    o_rows = nc.dram_tensor("o_rows", [1024, D], F32, kind="ExternalOutput").ap()

    ag_in_gu = nc.dram_tensor("ag_in_gu", [1024, 4096], BF16, kind="Internal")
    ag_out_gu = nc.dram_tensor("ag_out_gu", [8192, 4096], BF16, kind="Internal")
    ag_in_p = nc.dram_tensor("ag_in_p", [1024, 2048], BF16, kind="Internal")
    ag_out_p = nc.dram_tensor("ag_out_p", [8192, 2048], BF16, kind="Internal")

    dbg = {}
    if debug >= 1:
        dbg["d_aggu"] = nc.dram_tensor("d_aggu", [16384, 2048], BF16, kind="ExternalOutput").ap()
        dbg["d_agp"] = nc.dram_tensor("d_agp", [8192, 2048], BF16, kind="ExternalOutput").ap()
    if debug >= 2:
        dbg["d_y"] = nc.dram_tensor("d_y", [1024, D], F32, kind="ExternalOutput").ap()
        dbg["d_mt"] = nc.dram_tensor("d_mt", [2048, 1024], BF16, kind="ExternalOutput").ap()

    EQ = {k: [] for k in ("t", "v", "c", "g", "y")}
    T = Ctr("T"); V = Ctr("V"); C = Ctr("C"); GC = Ctr("GC")
    CC = Ctr("CC"); CC2 = Ctr("CC2"); BAR = Ctr("BAR")
    CTRS = [T, V, C, GC, CC, CC2, BAR]

    def emit(eng, fn):
        EQ[eng].append(fn)

    def wait(eng, evt):
        if evt is None:
            return
        ctr, val = evt
        emit(eng, lambda e, c_=ctr, v_=val: e.wait_ge(c_.h, v_))

    def waits(eng, evts):
        for evt in evts:
            wait(eng, evt)

    def barrier():
        tgt = BAR.n + 5
        BAR.n = tgt
        for eng in ("t", "v", "c", "g", "y"):
            emit(eng, lambda e: e.nop().then_inc(BAR.h, 1))
            emit(eng, lambda e, v_=tgt: e.wait_ge(BAR.h, v_))

    class Stream:
        def __init__(self, name, q):
            self.q = q
            self.ctr = Ctr(name)

    def dma(stream, out, in_, prio_waits=()):
        q = stream.q
        waits(q, prio_waits)
        if stream.ctr.n:
            wait(q, (stream.ctr, stream.ctr.n))
        evt = stream.ctr.bump(16)
        emit(q, lambda e, o=out, i=in_, c_=stream.ctr:
             e.dma_start(out=o, in_=i).then_inc(c_.h, 16))
        return evt

    def op(eng, fn, inc=False, prio_waits=()):
        ctr = {"t": T, "v": V, "c": C, "g": GC}[eng]
        waits(eng, prio_waits)
        if eng in ("v", "c", "g"):
            if ctr.n > 0:
                wait(eng, (ctr, ctr.n))
            evt = ctr.bump(1)
            emit(eng, lambda e, f=fn, c_=ctr: f(e).then_inc(c_.h, 1))
            return evt
        if inc:
            evt = ctr.bump(1)
            emit(eng, lambda e, f=fn, c_=ctr: f(e).then_inc(c_.h, 1))
            return evt
        emit(eng, lambda e, f=fn: f(e))
        return None

    def t_event():
        wait("t", (T, T.n))
        evt = T.bump(1)
        emit("t", lambda e: e.nop().then_inc(T.h, 1))
        return evt

    def flush(name):
        with nc.Block(name) as blk:
            if EQ["t"]:
                blk.tensor(lambda t, L=EQ["t"][:]: [fn(t) for fn in L])
            if EQ["v"]:
                blk.vector(lambda v, L=EQ["v"][:]: [fn(v) for fn in L])
            if EQ["c"]:
                blk.scalar(lambda s, L=EQ["c"][:]: [fn(s) for fn in L])
            if EQ["g"]:
                blk.gpsimd(lambda g, L=EQ["g"][:]: [fn(g) for fn in L])
            if EQ["y"]:
                blk.sync(lambda s, L=EQ["y"][:]: [fn(s) for fn in L])
        for k in EQ:
            EQ[k].clear()

    st = ExitStack()
    en = st.enter_context

    ST_C = Stream("sC", "y")
    ST_SLAB = Stream("sS", "y")
    ST_WG = Stream("sWG", "y")
    ST_WP = Stream("sWP", "y")
    ST_XR = Stream("sXR", "y")
    ST_O = Stream("sO", "y")
    ST_X = Stream("sX", "g")
    ST_QL = Stream("sQL", "g")
    ST_W = Stream("sW", "g")
    ST_WP2 = Stream("sWP2", "g")
    STREAMS = [ST_C, ST_SLAB, ST_WG, ST_WP, ST_XR, ST_O, ST_X, ST_QL, ST_W,
               ST_WP2]

    for ctr in CTRS + [x.ctr for x in STREAMS]:
        ctr.h = en(nc.semaphore(ctr.name))

    idn = en(nc.sbuf_tensor("idn", [128, 128], BF16))
    idnf = en(nc.sbuf_tensor("idnf", [128, 128], F32))
    maskT = en(nc.sbuf_tensor("maskT", [128, 4, SEG], BF16))
    beta_t = en(nc.sbuf_tensor("beta_t", [128, NH, 128], F32))
    onesf = en(nc.sbuf_tensor("onesf", [1, 128], F32))
    bsb = en(nc.sbuf_tensor("bsb", [1, 512], F32))
    zsb = en(nc.sbuf_tensor("zsb", [128, 2048], F32))
    sqs = en(nc.sbuf_tensor("sqs", [128, 1024], F32))
    red4 = en(nc.sbuf_tensor("red4", [128, 6], F32))

    ev_idn = dma(ST_C, idn[:], idn_d)
    ev_idnf = dma(ST_C, idnf[:], idnf_d)
    ev_mask = dma(ST_C, maskT.ap(), mask_d.rearrange("(m p) u -> p m u", p=128))
    ev_bsb = dma(ST_C, bsb[:], betas_d)
    op("v", lambda e_: e_.memset(onesf[:], 1.0))
    ev_ones = op("v", lambda e_: e_.nop(), inc=True)

    s0 = ExitStack()
    e0 = s0.enter_context
    qw_c = [e0(nc.sbuf_tensor(f"qwc{i}", [128, 4096], BF16)) for i in range(3)]
    m1b = [e0(nc.sbuf_tensor(f"m1b{i}", [128, 4096], BF16)) for i in range(2)]
    m2b = [e0(nc.sbuf_tensor(f"m2b{i}", [128, 4096], BF16)) for i in range(2)]
    qsum = e0(nc.sbuf_tensor("qsum", [128, 64], F32))
    qs = e0(nc.sbuf_tensor("qs", [128, 64], BF16))
    qsf2 = [e0(nc.sbuf_tensor(f"qsf{i}", [128, 64], F32)) for i in range(2)]
    qt2 = [e0(nc.sbuf_tensor(f"qt{i}", [128, 64], F32)) for i in range(2)]
    qtn2 = [e0(nc.sbuf_tensor(f"qtn{i}", [128, 64], F32)) for i in range(2)]
    wfin_gu = e0(nc.sbuf_tensor("wfin_gu", [128, 16, 2048], BF16))
    wfin_p = e0(nc.sbuf_tensor("wfin_p", [128, 16, 1024], BF16))
    asm = [e0(nc.sbuf_tensor(f"asm{i}", [128, 2048], BF16)) for i in range(2)]
    asm2 = [e0(nc.sbuf_tensor(f"asm2{i}", [128, 4096], BF16)) for i in range(2)]
    ptq = e0(nc.psum_tensor("ptq", [128, 512], BF16))

    qfree = [None, None, None]

    def quant_rows(src_dram, n_rows, n_cols, wfin_all, kblk):
        nchunk = n_rows // (128 * kblk)
        ngrp = (kblk * n_cols) // 64
        ev_free = qfree
        ev_done = None

        def grp(a):
            return a.rearrange("p (g z) -> p g z", z=64)

        def bcast(a):
            return a.rearrange("p (g o) -> p g o", o=1).broadcast_to(
                [128, ngrp, 64])

        def head(ch):
            par = ch % 3
            pm = ch % 2
            w8 = [ev_free[par]] if ev_free[par] else []
            qc = qw_c[par].ap().rearrange("p (k d) -> p k d", k=kblk)[:, :, 0:n_cols]
            ev_load = dma(
                ST_QL, qc,
                src_dram[ch * 128 * kblk:(ch + 1) * 128 * kblk, :]
                .rearrange("(k p) d -> p k d", p=128),
                prio_waits=w8)
            flat = qw_c[par].ap()
            op("v", lambda e_, a=flat: e_.tensor_copy(a.bitcast(I16),
                                                      a.bitcast(I16)),
               prio_waits=[ev_load])
            ev_red = op("v", lambda e_, a=flat: e_.tensor_reduce(
                qsum.ap()[:, 0:ngrp].rearrange("p (g o) -> p g o", o=1),
                grp(a),
                AXX, ALU.add, apply_absolute_value=True), inc=True)
            op("v", lambda e_: e_.tensor_copy(qsum.ap().bitcast(I32),
                                              qsum.ap().bitcast(I32)),
               prio_waits=[ev_red])
            op("v", lambda e_: e_.tensor_scalar_mul(qs[:], qsum[:], 1.0 / 64.0))
            op("v", lambda e_: e_.tensor_scalar_max(qs[:], qs[:], C1E8))
            op("v", lambda e_: e_.tensor_copy(qs.ap().bitcast(I16),
                                              qs.ap().bitcast(I16)))
            op("v", lambda e_, pm=pm: e_.tensor_copy(qsf2[pm][:], qs[:]))
            op("v", lambda e_, pm=pm: e_.tensor_scalar_mul(
                qt2[pm][:], qsf2[pm][:], THR))
            return op("v", lambda e_, pm=pm: e_.tensor_scalar_mul(
                qtn2[pm][:], qt2[pm][:], -1.0), inc=True)

        def gcomp(ch, ev_head):
            par = ch % 3
            pm = ch % 2
            flat = qw_c[par].ap()
            op("v", lambda e_, pm=pm, a=flat: e_.tensor_tensor(
                grp(m1b[pm].ap()), grp(a), bcast(qt2[pm].ap()[:, 0:ngrp]),
                ALU.is_gt), prio_waits=[ev_head])
            return op("v", lambda e_, pm=pm, a=flat: e_.tensor_tensor(
                grp(m2b[pm].ap()), grp(a), bcast(qtn2[pm].ap()[:, 0:ngrp]),
                ALU.is_lt), inc=True)

        def tail(ch, ev_g):
            par = ch % 3
            pm = ch % 2
            flat = qw_c[par].ap()
            op("v", lambda e_, pm=pm: e_.tensor_sub(
                m1b[pm][:], m1b[pm][:], m2b[pm][:]), prio_waits=[ev_g])
            op("v", lambda e_, pm=pm: e_.tensor_tensor(
                grp(m2b[pm].ap()), grp(m1b[pm].ap()),
                bcast(qsf2[pm].ap()[:, 0:ngrp]), ALU.mult))
            ws = wfin_all.ap()[:, ch * kblk:(ch + 1) * kblk, 0:n_cols] \
                .rearrange("p k d -> p (k d)")
            op("v", lambda e_, ws=ws, pm=pm, a=flat: e_.tensor_sub(
                ws, m2b[pm].ap(), a))
            op("v", lambda e_, ws=ws, a=flat: e_.tensor_add(ws, a, ws))
            op("v", lambda e_, ws=ws: e_.tensor_copy(ws.bitcast(I16),
                                                     ws.bitcast(I16)))
            ev_free[ch % 3] = op("v", lambda e_: e_.nop(), inc=True)
            return ev_free[ch % 3]

        ev_h = {0: head(0)}
        if nchunk > 1:
            ev_h[1] = head(1)
        for ch in range(nchunk):
            ev_g = gcomp(ch, ev_h[ch])
            ev_done = tail(ch, ev_g)
            if ch + 2 < nchunk:
                ev_h[ch + 2] = head(ch + 2)
        return ev_done

    def drain_all():
        for s_ in STREAMS:
            if s_.ctr.n:
                emit(s_.q, lambda e_, c_=s_.ctr: e_.wait_ge(c_.h, c_.n))

    def cutout(tag):
        drain_all()
        flush("cut_" + tag)
        s0.close()
        st.close()
        return nc

    if cut == "none":
        return cutout("none")
    ev_qgu = quant_rows(wgu_d, 2048, 2048, wfin_gu, kblk=2)
    if cut == "qgu":
        if debug >= 1:
            d_wfin = nc.dram_tensor("d_wfin", [2048, 2048], BF16,
                                    kind="ExternalOutput").ap()
            dma(ST_C, d_wfin.rearrange("(k p) d -> p k d", p=128),
                wfin_gu.ap(), prio_waits=[ev_qgu])
        return cutout("qgu")
    ev_qp = quant_rows(wp_d, D, 1024, wfin_p, kblk=4)
    if cut == "qp":
        return cutout("qp")

    ev_asm_free = [None, None]
    ev_slab = []
    ev_ptq_free = None

    def slab(src, n_blocks, n_out_rows, colmap, ag_in, ev_dep):
        nonlocal ev_ptq_free
        for kd in range(n_out_rows):
            par = kd % 2
            for Hq in range((n_blocks + 3) // 4):
                lastt = None
                for hj in range(4):
                    H = Hq * 4 + hj
                    if H >= n_blocks:
                        break
                    w8 = []
                    if hj == 0:
                        if Hq == 0:
                            w8 = [ev_dep, ev_idn]
                        if ev_ptq_free:
                            w8.append(ev_ptq_free)
                    lastt = op("t", lambda e_, H=H, kd=kd, hj=hj: e_.transpose(
                        ptq[:, 128 * hj:128 * (hj + 1)],
                        src(H, kd), idn[:]),
                        inc=(hj == 3 or H == n_blocks - 1), prio_waits=w8)
                for hj in range(4):
                    H = Hq * 4 + hj
                    if H >= n_blocks:
                        break
                    cb = colmap(H)
                    w8 = [lastt] if hj == 0 else []
                    if hj == 0 and ev_asm_free[par] and Hq == 0:
                        w8.append(ev_asm_free[par])
                    ev_ptq_free = op("v", lambda e_, hj=hj, cb=cb, par=par:
                                     e_.tensor_copy(asm[par][:, cb:cb + 128],
                                                    ptq[:, 128 * hj:128 * (hj + 1)]),
                                     inc=(hj == 3 or H == n_blocks - 1),
                                     prio_waits=w8)
            ev_ev = ev_ptq_free
            ev_st = dma(ST_SLAB, ag_in.ap()[128 * kd:128 * (kd + 1), :],
                        asm[par][:], prio_waits=[ev_ev])
            ev_asm_free[par] = ev_st
            ev_slab.append(ev_st)

    for q in range(8):
        par = q % 2
        for grp in range(8):
            lastt = None
            for j4 in range(4):
                idx = grp * 4 + j4
                kd = idx // 2
                is_up = idx % 2
                H = q + 8 * is_up
                w8 = []
                if j4 == 0:
                    w8 = [ev_qgu, ev_idn]
                    if ev_ptq_free:
                        w8.append(ev_ptq_free)
                lastt = op("t", lambda e_, H=H, kd=kd, j4=j4: e_.transpose(
                    ptq[:, 128 * j4:128 * (j4 + 1)],
                    wfin_gu.ap()[:, H, 128 * kd:128 * (kd + 1)], idn[:]),
                    inc=(j4 == 3), prio_waits=w8)
            for j4 in range(4):
                idx = grp * 4 + j4
                kd = idx // 2
                is_up = idx % 2
                cb = 256 * kd + 128 * is_up
                w8 = [lastt] if j4 == 0 else []
                if j4 == 0 and grp == 0 and ev_asm_free[par]:
                    w8.append(ev_asm_free[par])
                ev_ptq_free = op("v", lambda e_, j4=j4, cb=cb, par=par:
                                 e_.tensor_copy(asm2[par][:, cb:cb + 128],
                                                ptq[:, 128 * j4:128 * (j4 + 1)]),
                                 inc=(j4 == 3), prio_waits=w8)
        ev_st = dma(ST_SLAB, ag_in_gu.ap()[128 * q:128 * (q + 1), :],
                    asm2[par][:], prio_waits=[ev_ptq_free])
        ev_asm_free[par] = ev_st
        ev_slab.append(ev_st)
    if cut == "sgu":
        return cutout("sgu")
    slab(lambda H, kd: wfin_p.ap()[:, H, 128 * kd:128 * (kd + 1)],
         16, 8, lambda H: 128 * H, ag_in_p, ev_qp)
    if cut == "sp":
        return cutout("sp")

    if phases <= 0:
        drain_all()
        flush("phase0cut")
        s0.close()
        st.close()
        return nc

    waits("g", ev_slab)
    ev_ag_gu = CC.bump(1)
    emit("g", lambda e_: e_.collective_compute(
        "AllGather", ALU.bypass, replica_groups=[list(range(NCORES))],
        ins=[ag_in_gu.ap().opt()], outs=[ag_out_gu.ap().opt()]).then_inc(CC.h, 1))
    ev_ag_p = CC2.bump(1)
    emit("g", lambda e_: e_.collective_compute(
        "AllGather", ALU.bypass, replica_groups=[list(range(NCORES))],
        ins=[ag_in_p.ap().opt()], outs=[ag_out_p.ap().opt()]).then_inc(CC2.h, 1))

    if debug >= 1:
        evp = None
        for blk in range(64):
            evl = dma(ST_C, asm2[0][:], ag_out_gu.ap()[128 * blk:128 * (blk + 1), :],
                      prio_waits=[ev_ag_gu] + ([evp] if evp else []))
            evp = dma(ST_C, dbg["d_aggu"].rearrange("(n p) c -> p n c", p=128)
                      [:, blk:blk + 1, :].rearrange("p n (a b) -> p (n a) b", a=2),
                      asm2[0].ap().rearrange("p (a b) -> p a b", a=2),
                      prio_waits=[evl])
        for blk in range(64):
            evl = dma(ST_C, asm[0][:], ag_out_p.ap()[128 * blk:128 * (blk + 1), :],
                      prio_waits=[ev_ag_p, evp])
            evp = dma(ST_C, dbg["d_agp"][128 * blk:128 * (blk + 1), :], asm[0][:],
                      prio_waits=[evl])

    barrier()
    if phases <= 1:
        drain_all()
    flush("phase0")
    s0.close()
    if phases <= 1:
        st.close()
        return nc

    sMT = ExitStack()
    mt_sb = sMT.enter_context(nc.sbuf_tensor("mt_sb", [128, NKD, 1024], BF16))

    sA = ExitStack()
    eA = sA.enter_context
    wq_sb = eA(nc.sbuf_tensor("wq_sb", [128, NKD, 512], BF16))
    wk_sb = eA(nc.sbuf_tensor("wk_sb", [128, NKD, 512], BF16))
    wv_sb = eA(nc.sbuf_tensor("wv_sb", [128, NKD, 512], BF16))
    memz = eA(nc.sbuf_tensor("memz", [128, NH, 129], BF16))
    mem_f = eA(nc.sbuf_tensor("mem_f", [128, NH, 128], F32))
    z_f = eA(nc.sbuf_tensor("z_f", [128, NH], F32))
    xT = eA(nc.sbuf_tensor("xT", [128, NKD, 512], BF16))
    xseg = [eA(nc.sbuf_tensor(f"xseg{i}", [128, 4, 2048], BF16)) for i in range(2)]
    NPAR = 3
    qT_sb = [eA(nc.sbuf_tensor(f"qT{i}", [128, 512], BF16)) for i in range(NPAR)]
    kT_sb = [eA(nc.sbuf_tensor(f"kT{i}", [128, 512], BF16)) for i in range(NPAR)]
    sqT = [eA(nc.sbuf_tensor(f"sqT{i}", [128, 512], BF16)) for i in range(NPAR)]
    skT = [eA(nc.sbuf_tensor(f"skT{i}", [128, 512], BF16)) for i in range(NPAR)]
    skn = [eA(nc.sbuf_tensor(f"skn{i}", [128, 4, 128], BF16)) for i in range(NPAR)]
    tmpA = [eA(nc.sbuf_tensor("tmpA0", [128, 512], F32))] * NPAR
    tmpB = [eA(nc.sbuf_tensor("tmpB0", [128, 512], F32))] * NPAR
    etmp = [eA(nc.sbuf_tensor(f"etmp{i}", [128, 512], F32)) for i in range(2)]
    v_sb = [eA(nc.sbuf_tensor(f"v_sb{i}", [128, 4, 4 * 129], BF16))
            for i in range(2)]
    E_sb = [eA(nc.sbuf_tensor(f"E{i}", [128, 4, 512], BF16)) for i in range(NPAR)]
    att_d = [eA(nc.sbuf_tensor("attd0", [128, 128], F32))] * NPAR
    diff = [eA(nc.sbuf_tensor("diff0", [128, 128], F32))] * NPAR
    att_bf = [eA(nc.sbuf_tensor(f"attbf{i}", [128, 4, 128], BF16)) for i in range(NPAR)]
    delta = [eA(nc.sbuf_tensor(f"delta{i}", [128, 4, 128], BF16)) for i in range(NPAR)]
    rcol = [eA(nc.sbuf_tensor("rcol0", [128, 4], F32))] * NPAR
    pq = [eA(nc.psum_tensor(f"pq{i}", [128, 512], F32)) for i in range(2)]
    pk = [eA(nc.psum_tensor(f"pk{i}", [128, 512], F32)) for i in range(2)]
    psc = eA(nc.psum_tensor("psc", [128, 1024], F32))
    pnd = eA(nc.psum_tensor("pnd", [128, 512], F32))
    ptp = eA(nc.psum_tensor("ptpA", [128, 512], BF16))

    ev_wq = dma(ST_W, wq_sb.ap(), wq_d.rearrange("(k p) c -> p k c", p=128))
    ev_wk = dma(ST_W, wk_sb.ap(), wk_d.rearrange("(k p) c -> p k c", p=128))
    ev_wv = dma(ST_W, wv_sb.ap(), wv_d.rearrange("(k p) c -> p k c", p=128))

    op("v", lambda e_: e_.memset(mem_f.ap().rearrange("p h d -> p (h d)"), 0.0))
    op("v", lambda e_: e_.memset(z_f[:], Z0))
    op("v", lambda e_: e_.memset(memz.ap().rearrange("p h d -> p (h d)"), 0.0))
    for hh in range(NH):
        op("v", lambda e_, hh=hh: e_.memset(memz.ap()[:, hh, 128:129], Z0))
    for vpar in range(2):
        op("v", lambda e_, vpar=vpar:
           e_.memset(v_sb[vpar].ap().rearrange("p m c -> p (m c)"), 0.0))
        for m in range(4):
            for hh in range(NH):
                op("v", lambda e_, m=m, hh=hh, vpar=vpar:
                   e_.memset(v_sb[vpar].ap()[:, m, 129 * hh + 128:129 * hh + 129], 1.0))
    ev_vinit = op("v", lambda e_: e_.nop(), inc=True)

    for hh in range(NH):
        evb = op("t", lambda e_, hh=hh: e_.matmul(
            pq[0][:, 0:128], onesf[0:1, :], bsb[0:1, 128 * hh:128 * (hh + 1)],
            start=True, stop=True), inc=True,
            prio_waits=[ev_bsb, ev_ones] if hh == 0 else ())
        evs = op("c", lambda e_, hh=hh: e_.activation(
            beta_t.ap()[:, hh, :], pq[0][:, 0:128], AF.Sigmoid),
            inc=True, prio_waits=[evb])
        wait("t", evs)

    if cut == "beta":
        drain_all()
        flush("cutbeta")
        sA.close()
        sMT.close()
        st.close()
        return nc

    if noscr:
        scr = lambda apx: apx
    else:
        scr = lambda apx: apx.rearrange("p (t r) -> p r t", r=16)

    ev_xld = [None, None]
    ev_xT_users = None
    ev_psc_free = [None, None]
    ev_etmp_rd = [None, None]
    ev_pq_free = [None, None]
    ev_pk_free = [None, None]
    ev_pnd_free = None
    ev_ptp_free = None
    ev_mt_done = []
    ev_state = [None] * NH
    ev_par_free = [None] * NPAR
    H_evq = {}
    H_evq1 = {}
    H_evk1 = {}
    H_evsk = {}
    H_evskn = {}
    H_evE = {}
    H_evatt = {}

    ev_xld[0] = dma(ST_X, xseg[0].ap(),
                    x_b[0:SEG, :].rearrange("(m p) d -> p m d", p=128))

    ev_xseg_rd = [None, None]

    def st_xtp(i, par):
        nonlocal ev_ptp_free
        last_ev = None
        lastt = None
        for kd in range(NKD):
            for m in range(4):
                w8 = []
                if m == 0:
                    if kd == 0:
                        w8 = [ev_xld[par], ev_idn]
                        if ev_xT_users:
                            w8.append(ev_xT_users)
                    if ev_ptp_free:
                        w8.append(ev_ptp_free)
                lastt = op("t", lambda e_, m=m, kd=kd, par=par: e_.transpose(
                    ptp[:, 128 * m:128 * (m + 1)],
                    xseg[par].ap()[:, m, 128 * kd:128 * (kd + 1)], idn[:]),
                    inc=(m == 3), prio_waits=w8)
            last_ev = op("c", lambda e_, kd=kd: e_.activation(
                xT.ap()[:, kd, :], ptp.ap(), AF.Copy),
                inc=True, prio_waits=[lastt])
            ev_ptp_free = last_ev
        ev_xseg_rd[par] = lastt
        return last_ev

    ev_vseg_free = [None, None]

    def st_v(i, ev_xT_ready):
        vpar = i % 2
        ev = None
        for m in range(4):
            half = m % 2
            lastv = None
            for kd in range(NKD):
                w8 = []
                if kd == 0:
                    w8 = [ev_xT_ready, ev_wv, ev_vinit]
                    if ev_psc_free[half]:
                        w8.append(ev_psc_free[half])
                lastv = op("t", lambda e_, m=m, kd=kd, half=half: e_.matmul(
                    psc[:, 512 * half:512 * (half + 1)],
                    xT.ap()[:, kd, 128 * m:128 * (m + 1)],
                    wv_sb.ap()[:, kd, :],
                    start=(kd == 0), stop=(kd == NKD - 1)),
                    inc=(kd == NKD - 1), prio_waits=w8)
            w8e = [lastv]
            if m == 0 and ev_vseg_free[vpar]:
                w8e += ev_vseg_free[vpar]
            ev = op("c", lambda e_, m=m, half=half, vpar=vpar: e_.activation(
                v_sb[vpar].ap()[:, m, :]
                .rearrange("p (h c) -> p h c", c=129)[:, :, 0:128],
                psc.ap()[:, 512 * half:512 * (half + 1)]
                .rearrange("p (h c) -> p h c", c=128), AF.Copy),
                inc=True, prio_waits=w8e)
            ev_psc_free[half] = ev
        return ev

    def st_qk(i, gh, ev_xT_ready):
        hh = gh % 4
        par = gh % 2
        p3 = gh % NPAR
        w8 = [ev_xT_ready, ev_wq]
        if ev_pq_free[par]:
            w8.append(ev_pq_free[par])
        lastq = None
        for kd in range(NKD):
            lastq = op("t", lambda e_, hh=hh, kd=kd, par=par: e_.matmul(
                pq[par][:], wq_sb.ap()[:, kd, 128 * hh:128 * (hh + 1)],
                xT.ap()[:, kd, :],
                start=(kd == 0), stop=(kd == NKD - 1)),
                inc=(kd == NKD - 1), prio_waits=w8 if kd == 0 else ())
        w8 = [ev_wk]
        if ev_pk_free[par]:
            w8.append(ev_pk_free[par])
        lastk = None
        for kd in range(NKD):
            lastk = op("t", lambda e_, hh=hh, kd=kd, par=par: e_.matmul(
                pk[par][:], wk_sb.ap()[:, kd, 128 * hh:128 * (hh + 1)],
                xT.ap()[:, kd, :],
                start=(kd == 0), stop=(kd == NKD - 1)),
                inc=(kd == NKD - 1), prio_waits=w8 if kd == 0 else ())
        tA, tB = tmpA[p3], tmpB[p3]
        wfree = [ev_par_free[p3]] if ev_par_free[p3] else []
        ev_q1 = op("c", lambda e_, p3=p3, par=par: e_.activation(
            qT_sb[p3][:], scr(pq[par].ap()), AF.Copy), inc=True,
            prio_waits=[lastq] + wfree)
        ev_q2 = op("c", lambda e_, par=par: e_.activation(
            tB[:], scr(pq[par].ap()), AF.Relu), inc=True,
            prio_waits=[H_evsk[gh - 1]] if (gh - 1) in H_evsk else ())
        ev_mnq = op("v", lambda e_, par=par: e_.tensor_scalar_min(
            tA[:], scr(pq[par].ap()), 0.0), prio_waits=[lastq, ev_q2] + wfree)
        ev_q3 = op("c", lambda e_: e_.activation(
            tA[:], tA[:], AF.Exp), inc=True, prio_waits=[ev_mnq])
        ev_pq_free[par] = ev_q3
        ev_sq = op("v", lambda e_, p3=p3: e_.tensor_add(
            sqT[p3][:], tA[:], tB[:]), inc=True, prio_waits=[ev_q2, ev_q3])
        H_evq[gh] = ev_sq
        H_evq1[gh] = ev_q1
        ev_k1 = op("c", lambda e_, p3=p3, par=par: e_.activation(
            kT_sb[p3][:], pk[par].ap(), AF.Copy), inc=True, prio_waits=[lastk])
        ev_k2 = op("c", lambda e_, par=par: e_.activation(
            tB[:], pk[par].ap(), AF.Relu), inc=True, prio_waits=[ev_sq])
        ev_mnk = op("v", lambda e_, par=par: e_.tensor_scalar_min(
            tA[:], pk[par].ap(), 0.0), prio_waits=[lastk, ev_k2])
        ev_k3 = op("c", lambda e_: e_.activation(
            tA[:], tA[:], AF.Exp), inc=True, prio_waits=[ev_mnk])
        ev_pk_free[par] = ev_k3
        ev_sk = op("v", lambda e_, p3=p3: e_.tensor_add(
            skT[p3][:], tA[:], tB[:]), inc=True, prio_waits=[ev_k2, ev_k3])
        H_evsk[gh] = ev_sk
        H_evk1[gh] = ev_k1

    def st_sktp_sc(i, gh):
        nonlocal ev_ptp_free
        p3 = gh % NPAR
        wfree = [ev_par_free[p3]] if ev_par_free[p3] else []
        lastt = None
        for mt in range(4):
            w8 = []
            if mt == 0:
                w8 = [H_evsk[gh]]
                if ev_ptp_free:
                    w8.append(ev_ptp_free)
            lastt = op("t", lambda e_, mt=mt, p3=p3: e_.transpose(
                ptp[:, 128 * mt:128 * (mt + 1)],
                skT[p3].ap()[:, 128 * mt:128 * (mt + 1)], idn[:]),
                inc=(mt == 3), prio_waits=w8)
        ev_ptp_free = op("c", lambda e_, p3=p3: e_.activation(
            skn[p3].ap().rearrange("p m c -> p (m c)"), ptp.ap(), AF.Copy),
            inc=True, prio_waits=[lastt] + wfree)
        H_evskn[gh] = ev_ptp_free
        evE = []
        lastsc = None
        for mt in range(4):
            half = mt % 2
            w8 = [H_evq1[gh], H_evk1[gh]] if mt == 0 else []
            if ev_psc_free[half]:
                w8.append(ev_psc_free[half])
            lastsc = op("t", lambda e_, mt=mt, p3=p3, half=half: e_.matmul(
                psc[:, 512 * half:512 * (half + 1)],
                kT_sb[p3][:, 128 * mt:128 * (mt + 1)], qT_sb[p3][:],
                start=True, stop=True), inc=True, prio_waits=w8)
            buf = etmp[half]
            w8b = [lastsc, ev_mask]
            if ev_etmp_rd[half]:
                w8b.append(ev_etmp_rd[half])
            evx = op("c", lambda e_, mt=mt, buf=buf, half=half: e_.activation(
                buf[:], psc.ap()[:, 512 * half:512 * (half + 1)],
                AF.Exp, scale=INV_SCALE), inc=True, prio_waits=w8b)
            ev_psc_free[half] = evx
            w8c = list(wfree) if mt == 0 else []
            evm = op("v", lambda e_, mt=mt, buf=buf, p3=p3: e_.tensor_mul(
                E_sb[p3].ap()[:, mt, :], buf[:], maskT.ap()[:, mt, :]),
                inc=True, prio_waits=[evx] + w8c)
            ev_etmp_rd[half] = evm
            evE.append(evm)
        H_evE[gh] = evE

    def st_nd(i, gh):
        nonlocal ev_pnd_free
        hh = gh % 4
        p3 = gh % NPAR
        rc = rcol[p3]
        ad = att_d[p3]
        df = diff[p3]
        ev_cmb_all = None
        for m in range(4):
            w8 = [H_evE[gh][-1], ev_v_ready[i]]
            if ev_pnd_free:
                w8.append(ev_pnd_free)
            for mt in range(4):
                op("t", lambda e_, m=m, mt=mt, hh=hh, p3=p3, vp=i % 2: e_.matmul(
                    pnd[:, 0:129],
                    E_sb[p3].ap()[:, mt, 128 * m:128 * (m + 1)],
                    v_sb[vp].ap()[:, mt, 129 * hh:129 * (hh + 1)],
                    start=(mt == 0), stop=(mt == 3)),
                    prio_waits=w8 if mt == 0 else ())
            w8n = [H_evq[gh]]
            if ev_state[hh]:
                w8n.append(ev_state[hh])
            ev_nd = op("t", lambda e_, m=m, hh=hh, p3=p3: e_.matmul(
                pnd[:, 256:385],
                sqT[p3][:, 128 * m:128 * (m + 1)], memz.ap()[:, hh, :],
                start=True, stop=True), inc=True, prio_waits=w8n)
            ev_r1 = op("v", lambda e_, rc=rc: e_.reciprocal(
                rc[:, 0:1], pnd[:, 128:129]), inc=True, prio_waits=[ev_nd])
            ev_r2 = op("v", lambda e_, rc=rc: e_.reciprocal(
                rc[:, 1:2], pnd[:, 384:385]), inc=True)
            op("v", lambda e_, rc=rc, ad=ad: e_.tensor_scalar_mul(
                ad[:], pnd[:, 0:128], rc[:, 0:1]), prio_waits=[ev_r1, ev_r2])
            op("v", lambda e_, rc=rc, ad=ad, df=df: e_.scalar_tensor_tensor(
                df[:], pnd[:, 256:384], rc[:, 1:2], ad[:],
                ALU.mult, ALU.subtract))
            op("v", lambda e_, hh=hh, df=df: e_.tensor_mul(
                df[:], df[:], beta_t.ap()[:, hh, :]))
            ev_cmb = op("v", lambda e_, m=m, p3=p3, ad=ad, df=df: e_.tensor_add(
                att_bf[p3].ap()[:, m, :], ad[:], df[:]), inc=True)
            ev_cmb_all = ev_cmb
            ev_ak = op("t", lambda e_, m=m, hh=hh, p3=p3: e_.matmul(
                pnd[:, 256:385],
                skT[p3][:, 128 * m:128 * (m + 1)], memz.ap()[:, hh, :],
                start=True, stop=True), inc=True,
                prio_waits=[H_evsk[gh], ev_cmb])
            ev_r3 = op("v", lambda e_, rc=rc: e_.reciprocal(
                rc[:, 2:3], pnd[:, 384:385]), inc=True, prio_waits=[ev_ak])
            ev_dl = op("v", lambda e_, m=m, hh=hh, p3=p3, rc=rc, vp=i % 2:
                       e_.scalar_tensor_tensor(
                delta[p3].ap()[:, m, :], pnd[:, 256:384], rc[:, 2:3],
                v_sb[vp].ap()[:, m, 129 * hh:129 * hh + 128],
                ALU.mult, ALU.subtract), inc=True, prio_waits=[ev_r3])
            ev_pnd_free = ev_dl
        H_evatt[gh] = ev_cmb_all
        lastdm = None
        for mt in range(4):
            w8 = [H_evskn[gh], ev_pnd_free] if mt == 0 else []
            lastdm = op("t", lambda e_, mt=mt, p3=p3: e_.matmul(
                pnd[:, 0:128], skn[p3].ap()[:, mt, :],
                delta[p3].ap()[:, mt, :],
                start=(mt == 0), stop=(mt == 3)),
                inc=(mt == 3), prio_waits=w8)
        op("v", lambda e_, hh=hh: e_.tensor_sub(
            mem_f.ap()[:, hh, :], mem_f.ap()[:, hh, :], pnd[:, 0:128]),
            prio_waits=[lastdm])
        ev_mcp = op("v", lambda e_, hh=hh: e_.tensor_copy(
            memz.ap()[:, hh, 0:128], mem_f.ap()[:, hh, :]), inc=True)
        ev_pnd_free = ev_mcp
        ev_zr = op("v", lambda e_, p3=p3, rc=rc: e_.tensor_reduce(
            rc.ap()[:, 3:4], skT[p3].ap(), AXX, ALU.add),
            inc=True, prio_waits=[H_evsk[gh]])
        op("v", lambda e_, hh=hh, rc=rc: e_.tensor_add(
            z_f.ap()[:, hh:hh + 1], z_f.ap()[:, hh:hh + 1], rc[:, 3:4]),
            prio_waits=[ev_zr])
        ev_state[hh] = op("v", lambda e_, hh=hh: e_.tensor_copy(
            memz.ap()[:, hh, 128:129], z_f.ap()[:, hh:hh + 1]), inc=True)
        return lastdm

    def st_atttp(i, gh, ev_dmdone):
        nonlocal ev_ptp_free
        hh = gh % 4
        p3 = gh % NPAR
        lastat = None
        for m in range(4):
            w8 = []
            if m == 0:
                w8 = [H_evatt[gh]]
                if ev_ptp_free:
                    w8.append(ev_ptp_free)
            lastat = op("t", lambda e_, m=m, p3=p3: e_.transpose(
                ptp[:, 128 * m:128 * (m + 1)], att_bf[p3].ap()[:, m, :],
                idn[:]), inc=(m == 3), prio_waits=w8)
        ev_par_free[p3] = lastat
        ev_mtc = op("c", lambda e_, hh=hh, i=i: e_.activation(
            mt_sb.ap()[:, :, 128 * i + 32 * hh:128 * i + 32 * hh + 32]
            .rearrange("p (m r) t -> p m r t", m=4),
            ptp.ap().rearrange("p (m r t) -> p m r t", m=4, r=4), AF.Copy),
            inc=True, prio_waits=[lastat])
        ev_ptp_free = ev_mtc
        return ev_mtc

    ev_v_ready = {}
    for i in range(NSEG):
        par = i % 2
        if i + 1 < NSEG:
            ev_xld[1 - par] = dma(
                ST_X, xseg[1 - par].ap(),
                x_b[SEG * (i + 1):SEG * (i + 2), :].rearrange("(m p) d -> p m d", p=128),
                prio_waits=[ev_xseg_rd[1 - par]] if ev_xseg_rd[1 - par] else ())
        if i == 0:
            ev_xT_ready = st_xtp(0, 0)
            ev_v_ready[0] = st_v(0, ev_xT_ready)
        g0 = 4 * i
        st_qk(i, g0 + 0, ev_xT_ready)
        st_qk(i, g0 + 1, ev_xT_ready)
        st_sktp_sc(i, g0 + 0)
        st_qk(i, g0 + 2, ev_xT_ready)
        dm0 = st_nd(i, g0 + 0)
        st_atttp(i, g0 + 0, dm0)
        st_sktp_sc(i, g0 + 1)
        st_qk(i, g0 + 3, ev_xT_ready)
        dm1 = st_nd(i, g0 + 1)
        st_atttp(i, g0 + 1, dm1)
        st_sktp_sc(i, g0 + 2)
        dm2 = st_nd(i, g0 + 2)
        st_atttp(i, g0 + 2, dm2)
        st_sktp_sc(i, g0 + 3)
        if i + 1 < NSEG:
            ev_xT_users = t_event()
            ev_xT_ready = st_xtp(i + 1, 1 - par)
            ev_v_ready[i + 1] = st_v(i + 1, ev_xT_ready)
        dm3 = st_nd(i, g0 + 3)
        ev_vseg_free[i % 2] = [dm3, ev_state[3]]
        ev_mt3 = st_atttp(i, g0 + 3, dm3)
        ev_mt_done.append(ev_mt3)
        if cut == f"seg{i}":
            drain_all()
            flush("cutseg")
            sA.close()
            sMT.close()
            st.close()
            return nc

    if debug >= 2:
        evp = None
        for r in range(NKD):
            evp = dma(ST_C, dbg["d_mt"][128 * r:128 * (r + 1), :],
                      mt_sb.ap()[:, r, :],
                      prio_waits=[ev_mt_done[-1]] + ([evp] if evp else []))
    barrier()
    if phases <= 2:
        drain_all()
    flush("phaseA")
    sA.close()
    if phases <= 2:
        sMT.close()
        st.close()
        return nc

    y_sb = en(nc.sbuf_tensor("y_sb", [128, NSEG, 2048], F32, side="right"))
    sA2 = ExitStack()
    e2 = sA2.enter_context
    wo_sb = e2(nc.sbuf_tensor("wo_sb", [128, NKD, 2048], BF16))
    xres_b = [e2(nc.sbuf_tensor(f"xres{i}", [128, 2048], F32)) for i in range(2)]
    pwo = e2(nc.psum_tensor("pwo", [128, 2048], F32))

    ev_wo_p = []
    for part in range(4):
        ev_wo_p.append(dma(ST_W, wo_sb.ap()[:, 4 * part:4 * (part + 1), :],
                           wo_d[512 * part:512 * (part + 1), :]
                           .rearrange("(k p) c -> p k c", p=128)))

    ev_xres = [None, None]
    ev_xres_free = [None, None]
    ev_pwo_free = None
    ev_zsb_free = None

    ev_xres[0] = dma(ST_XR, xres_b[0][:], x_res[0:128, :])
    for i in range(NSEG):
        par = i % 2
        if i + 1 < NSEG:
            ev_xres[1 - par] = dma(
                ST_XR, xres_b[1 - par][:], x_res[128 * (i + 1):128 * (i + 2), :],
                prio_waits=[ev_xres_free[1 - par]] if ev_xres_free[1 - par] else ())
        lastw = None
        for rch in range(NKD):
            for nb in range(4):
                w8 = []
                if rch == 0 and nb == 0:
                    w8 = [ev_mt_done[i], ev_wo_p[0 if i == 0 else 3]]
                    if ev_pwo_free:
                        w8.append(ev_pwo_free)
                elif i == 0 and nb == 0 and rch % 4 == 0:
                    w8 = [ev_wo_p[rch // 4]]
                lastw = op("t", lambda e_, i=i, rch=rch, nb=nb: e_.matmul(
                    pwo[:, 512 * nb:512 * (nb + 1)],
                    mt_sb.ap()[:, rch, 128 * i:128 * (i + 1)],
                    wo_sb.ap()[:, rch, 512 * nb:512 * (nb + 1)],
                    start=(rch == 0), stop=(rch == NKD - 1)),
                    inc=(rch == NKD - 1 and nb == 3), prio_waits=w8)
        w8 = [lastw, ev_xres[par]]
        if ev_zsb_free:
            w8.append(ev_zsb_free)
        op("v", lambda e_, par=par: e_.tensor_add(zsb[:], pwo.ap(), xres_b[par][:]),
           prio_waits=w8)
        ev_z = op("v", lambda e_: e_.nop(), inc=True)
        ev_pwo_free = ev_z
        ev_xres_free[par] = ev_z
        ev_sum = op("v", lambda e_: e_.tensor_reduce(
            red4.ap()[:, 0:1], zsb.ap(), AXX, ALU.add), inc=True, prio_waits=[ev_z])
        op("c", lambda e_: e_.activation(
            sqs[:], zsb[:, 0:1024], AF.Square, accum_out=red4.ap()[:, 1:2]),
            prio_waits=[ev_z])
        ev_sq2 = op("c", lambda e_: e_.activation(
            sqs[:], zsb[:, 1024:2048], AF.Square, accum_out=red4.ap()[:, 2:3]),
            inc=True)
        op("v", lambda e_: e_.tensor_scalar_mul(red4[:, 0:1], red4[:, 0:1],
                                                -1.0 / 2048.0), prio_waits=[ev_sum])
        op("v", lambda e_: e_.tensor_add(red4[:, 1:2], red4[:, 1:2],
                                         red4[:, 2:3]), prio_waits=[ev_sq2])
        op("v", lambda e_: e_.tensor_scalar_mul(red4[:, 1:2], red4[:, 1:2],
                                                1.0 / 2048.0))
        op("v", lambda e_: e_.tensor_tensor(red4[:, 2:3], red4[:, 0:1],
                                            red4[:, 0:1], ALU.mult))
        op("v", lambda e_: e_.tensor_sub(red4[:, 1:2], red4[:, 1:2], red4[:, 2:3]))
        op("v", lambda e_: e_.tensor_scalar_add(red4[:, 1:2], red4[:, 1:2], EPS_LN))
        ev_rc = op("v", lambda e_: e_.reciprocal(red4[:, 2:3], red4[:, 1:2]),
                   inc=True)
        ev_st2 = op("c", lambda e_: e_.activation(red4.ap()[:, 3:4],
                                                  red4.ap()[:, 2:3], AF.Sqrt),
                    inc=True, prio_waits=[ev_rc])
        ev_y = op("v", lambda e_, i=i: e_.tensor_scalar(
            y_sb.ap()[:, i, :], zsb[:], red4.ap()[:, 0:1], red4.ap()[:, 3:4],
            ALU.add, ALU.mult), inc=True, prio_waits=[ev_st2])
        ev_zsb_free = ev_y

    if debug >= 2:
        evp = None
        for i in range(NSEG):
            evp = dma(ST_C, dbg["d_y"][128 * i:128 * (i + 1), :],
                      y_sb.ap()[:, i, :],
                      prio_waits=[ev_zsb_free] + ([evp] if evp else []))

    barrier()
    if phases <= 3:
        drain_all()
    flush("phaseA2")
    sA2.close()
    sMT.close()
    if phases <= 3:
        st.close()
        return nc

    sB = ExitStack()
    eB = sB.enter_context
    yTh = eB(nc.sbuf_tensor("yTh", [128, NKD, 512], BF16))
    ybc = eB(nc.sbuf_tensor("ybc", [128, 2048], BF16))
    slu = [eB(nc.sbuf_tensor(f"slu{i}", [128, 512], BF16)) for i in range(2)]
    h_sb = eB(nc.sbuf_tensor("h_sb", [128, 64, 512], BF16))
    wg_buf = [eB(nc.sbuf_tensor(f"wg{i}", [128, NKD, 256], BF16)) for i in range(2)]
    wp_buf = [eB(nc.sbuf_tensor(f"wpb{i}", [128, 2, 2048], BF16)) for i in range(2)]
    inv4 = eB(nc.sbuf_tensor("inv4", [128, 8], F32))
    g1s = eB(nc.sbuf_tensor("g1s", [128, 128], F32))

    ev_ybc_free = None
    ev_ptB_free = None
    ev_wg_free = [None, None]
    ev_wp_free = [None, None]
    ev_pg_free = [None, None, None]
    ev_slu_rd = [None, None]
    ev_yT = None
    ev_h_done = None
    ev_inv = None

    def phaseB_batch(bb):
        nonlocal ev_ybc_free, ev_ptB_free, ev_yT, ev_h_done, ev_inv, ev_zsb_free
        with (nc.psum_tensor(f"ptB{bb}", [128, 512], BF16) as ptB,
              nc.psum_tensor(f"pg0{bb}", [128, 512], F32) as pg0,
              nc.psum_tensor(f"pg1{bb}", [128, 512], F32) as pg1,
              nc.psum_tensor(f"pg2{bb}", [128, 512], F32) as pg2,
              nc.psum_tensor(f"pu0{bb}", [128, 512], F32) as pu0,
              nc.psum_tensor(f"pu1{bb}", [128, 512], F32) as pu1,
              nc.psum_tensor(f"pu2{bb}", [128, 512], F32) as pu2,
              nc.psum_tensor(f"pG{bb}", [128, 128], F32) as pG):
            pg = [pg0, pg1, pg2]
            pu = [pu0, pu1, pu2]
            for m in range(4):
                i = 4 * bb + m
                w8 = [ev_ybc_free] if ev_ybc_free else []
                ev_cast = op("c", lambda e_, i=i: e_.activation(
                    ybc[:], y_sb.ap()[:, i, :], AF.Copy), inc=True, prio_waits=w8)
                for kq in range(4):
                    lastt = None
                    for kj in range(4):
                        kd = 4 * kq + kj
                        w8t = []
                        if kj == 0:
                            w8t = [ev_cast]
                            if ev_ptB_free:
                                w8t.append(ev_ptB_free)
                        lastt = op("t", lambda e_, kd=kd, kj=kj, ptB=ptB:
                                   e_.transpose(
                                       ptB[:, 128 * kj:128 * (kj + 1)],
                                       ybc.ap()[:, 128 * kd:128 * (kd + 1)],
                                       idn[:]),
                                   inc=(kj == 3), prio_waits=w8t)
                    ev_ptB_free = op("v", lambda e_, kq=kq, m=m, ptB=ptB:
                                     e_.tensor_copy(
                        yTh.ap()[:, 4 * kq:4 * (kq + 1), 128 * m:128 * (m + 1)],
                        ptB.ap().rearrange("p (k c) -> p k c", k=4)),
                        inc=True, prio_waits=[lastt])
                ev_ybc_free = ev_ptB_free
            ev_yT = ev_ptB_free

            ev_wg = [None, None]
            w8 = [ev_ag_gu] if bb == 0 else []
            if ev_wg_free[0]:
                w8.append(ev_wg_free[0])
            ev_wg[0] = dma(ST_WG, wg_buf[0].ap().rearrange("p k c -> p (k c)"),
                           ag_out_gu.ap()[0:128, :], prio_waits=w8)
            for jp in range(64):
                par = jp % 3
                wb = jp % 2
                if jp + 1 < 64:
                    r = (jp + 1) // 8
                    q = (jp + 1) % 8
                    ev_wg[1 - wb] = dma(
                        ST_WG, wg_buf[1 - wb].ap().rearrange("p k c -> p (k c)"),
                        ag_out_gu.ap()[1024 * r + 128 * q:1024 * r + 128 * (q + 1), :],
                        prio_waits=[ev_wg_free[1 - wb]]
                        if ev_wg_free[1 - wb] else ())
                lastg = None
                for kd in range(NKD):
                    w8 = []
                    if kd == 0:
                        w8 = [ev_wg[wb], ev_yT]
                        if ev_pg_free[par]:
                            w8.append(ev_pg_free[par])
                    op("t", lambda e_, kd=kd, par=par, wb=wb: e_.matmul(
                        pg[par][:], wg_buf[wb].ap()[:, kd, 0:128],
                        yTh.ap()[:, kd, :],
                        start=(kd == 0), stop=(kd == NKD - 1)), prio_waits=w8)
                    lastg = op("t", lambda e_, kd=kd, par=par, wb=wb: e_.matmul(
                        pu[par][:], wg_buf[wb].ap()[:, kd, 128:256],
                        yTh.ap()[:, kd, :],
                        start=(kd == 0), stop=(kd == NKD - 1)),
                        inc=(kd == NKD - 1))
                ev_wg_free[wb] = lastg
                ev_sl = op("c", lambda e_, par=par, wb=wb: e_.activation(
                    slu[wb][:], pg[par][:], AF.Silu), inc=True,
                    prio_waits=[lastg] +
                    ([ev_pg_free[par]] if ev_pg_free[par] else []) +
                    ([ev_slu_rd[wb]] if ev_slu_rd[wb] else []))
                ev_h = op("v", lambda e_, jp=jp, par=par, wb=wb: e_.tensor_tensor(
                    h_sb.ap()[:, jp, :], slu[wb][:], pu[par][:], ALU.mult),
                    inc=True, prio_waits=[ev_sl])
                ev_pg_free[par] = ev_h
                ev_slu_rd[wb] = ev_h
                ev_h_done = ev_h
            wait("t", ev_pg_free[0])
            wait("t", ev_pg_free[1])
            wait("t", ev_pg_free[2])

            ev_gr = None
            for m in range(4):
                lastG = None
                for k in range(64):
                    w8 = [ev_h_done] if k == 0 else []
                    lastG = op("t", lambda e_, m=m, k=k: e_.matmul(
                        pG[:], h_sb.ap()[:, k, 128 * m:128 * (m + 1)],
                        h_sb.ap()[:, k, 128 * m:128 * (m + 1)],
                        start=(k == 0), stop=(k == 63)),
                        inc=(k == 63), prio_waits=w8)
                op("v", lambda e_: e_.tensor_tensor(g1s[:], pG.ap(), idnf[:],
                                                    ALU.mult),
                   prio_waits=[lastG, ev_idnf])
                ev_gr = op("v", lambda e_, m=m, bb=bb: e_.tensor_reduce(
                    inv4.ap()[:, 4 * bb + m:4 * bb + m + 1], g1s.ap(), AXX,
                    ALU.add), inc=True)
                wait("t", ev_gr)
                op("v", lambda e_, m=m, bb=bb: e_.tensor_scalar(
                    inv4[:, 4 * bb + m:4 * bb + m + 1],
                    inv4[:, 4 * bb + m:4 * bb + m + 1],
                    1.0 / 8192.0, EPS_RMS, ALU.mult, ALU.add),
                    prio_waits=[ev_gr])
                ev_rcp = op("v", lambda e_, m=m, bb=bb: e_.reciprocal(
                    inv4[:, 4 * bb + m:4 * bb + m + 1],
                    inv4[:, 4 * bb + m:4 * bb + m + 1]), inc=True)
                op("c", lambda e_, m=m, bb=bb: e_.activation(
                    inv4.ap()[:, 4 * bb + m:4 * bb + m + 1],
                    inv4.ap()[:, 4 * bb + m:4 * bb + m + 1], AF.Sqrt),
                    prio_waits=[ev_rcp])
            ev_inv = op("c", lambda e_: e_.nop(), inc=True)
            ev_b1v = op("v", lambda e_: e_.nop(), inc=True)
        return ev_b1v

    def phaseB_proj(bb, ev_b1v):
        nonlocal ev_zsb_free, ev_ptB_free
        with (nc.psum_tensor(f"po0{bb}", [128, 2048], F32) as po0,
              nc.psum_tensor(f"po1{bb}", [128, 2048], F32) as po1):
            po = [po0, po1]
            for mp in range(2):
                def wp_load(kk, par, extra=()):
                    e1 = dma(ST_WP,
                             wp_buf[par].ap()[:, 0, :],
                             ag_out_p.ap()[256 * kk:256 * kk + 128, :]
                             .rearrange("(k p) c -> p (k c)", p=128),
                             prio_waits=list(extra))
                    e2 = dma(ST_WP2,
                             wp_buf[par].ap()[:, 1, :],
                             ag_out_p.ap()[256 * kk + 128:256 * (kk + 1), :]
                             .rearrange("(k p) c -> p (k c)", p=128),
                             prio_waits=list(extra))
                    return (e1, e2)

                ev_wp = [None, None]
                w8 = [ev_ag_p] if (bb == 0 and mp == 0) else []
                if ev_wp_free[0]:
                    w8.append(ev_wp_free[0])
                ev_wp[0] = wp_load(0, 0, extra=w8)
                lastp = None
                for kk in range(32):
                    par = kk % 2
                    if kk + 1 < 32:
                        ev_wp[1 - par] = wp_load(
                            kk + 1, 1 - par,
                            extra=[ev_wp_free[1 - par]]
                            if ev_wp_free[1 - par] else ())
                    for k2 in range(2):
                        k = 2 * kk + k2
                        for mm in range(2):
                            m = 2 * mp + mm
                            for nb in range(4):
                                w8 = []
                                if k2 == 0 and mm == 0 and nb == 0:
                                    w8 = [ev_wp[par][0], ev_wp[par][1],
                                          ev_h_done]
                                    if k == 0:
                                        w8 += [ev_b1v, ev_inv]
                                evmm = op("t", lambda e_, k=k, k2=k2, m=m,
                                          nb=nb, mm=mm, par=par: e_.matmul(
                                    po[mm][:, 512 * nb:512 * (nb + 1)],
                                    h_sb.ap()[:, k, 128 * m:128 * (m + 1)],
                                    wp_buf[par].ap()[:, k2,
                                                     512 * nb:512 * (nb + 1)],
                                    start=(k == 0), stop=(k == 63)),
                                    inc=(k2 == 1 and mm == 1 and nb == 3),
                                    prio_waits=w8)
                                if evmm is not None:
                                    lastp = evmm
                    ev_wp_free[par] = lastp
                for mm in range(2):
                    m = 2 * mp + mm
                    i = 4 * bb + m
                    op("v", lambda e_, mm=mm, m=m, i=i, bb=bb:
                       e_.scalar_tensor_tensor(
                           zsb[:], po[mm].ap(),
                           inv4.ap()[:, 4 * bb + m:4 * bb + m + 1],
                           y_sb.ap()[:, i, :], ALU.mult, ALU.add),
                       prio_waits=[lastp, ev_inv] +
                       ([ev_zsb_free] if ev_zsb_free else []))
                    ev_z = op("v", lambda e_: e_.nop(), inc=True)
                    ev_sum = op("v", lambda e_: e_.tensor_reduce(
                        red4.ap()[:, 0:1], zsb.ap(), AXX, ALU.add), inc=True)
                    op("c", lambda e_: e_.activation(
                        sqs[:], zsb[:, 0:1024], AF.Square,
                        accum_out=red4.ap()[:, 1:2]), prio_waits=[ev_z])
                    ev_sq2 = op("c", lambda e_: e_.activation(
                        sqs[:], zsb[:, 1024:2048], AF.Square,
                        accum_out=red4.ap()[:, 2:3]), inc=True)
                    op("v", lambda e_: e_.tensor_scalar_mul(
                        red4[:, 0:1], red4[:, 0:1], -1.0 / 2048.0),
                        prio_waits=[ev_sum])
                    op("v", lambda e_: e_.tensor_add(
                        red4[:, 1:2], red4[:, 1:2], red4[:, 2:3]),
                        prio_waits=[ev_sq2])
                    op("v", lambda e_: e_.tensor_scalar_mul(
                        red4[:, 1:2], red4[:, 1:2], 1.0 / 2048.0))
                    op("v", lambda e_: e_.tensor_tensor(
                        red4[:, 2:3], red4[:, 0:1], red4[:, 0:1], ALU.mult))
                    op("v", lambda e_: e_.tensor_sub(
                        red4[:, 1:2], red4[:, 1:2], red4[:, 2:3]))
                    op("v", lambda e_: e_.tensor_scalar_add(
                        red4[:, 1:2], red4[:, 1:2], EPS_LN))
                    ev_rc = op("v", lambda e_: e_.reciprocal(
                        red4[:, 2:3], red4[:, 1:2]), inc=True)
                    ev_st2 = op("c", lambda e_: e_.activation(
                        red4.ap()[:, 3:4], red4.ap()[:, 2:3], AF.Sqrt),
                        inc=True, prio_waits=[ev_rc])
                    ev_o = op("v", lambda e_: e_.tensor_scalar(
                        zsb[:], zsb[:], red4.ap()[:, 0:1], red4.ap()[:, 3:4],
                        ALU.add, ALU.mult), inc=True, prio_waits=[ev_st2])
                    ev_zsb_free = ev_o
                    ev_store = dma(ST_O, o_rows[128 * i:128 * (i + 1), :],
                                   zsb[:], prio_waits=[ev_o])
                    wait("v", ev_store)
                wait("t", ev_zsb_free)
        ev_ptB_free = ev_zsb_free

    for bb in range(2):
        ev_b1v = phaseB_batch(bb)
        phaseB_proj(bb, ev_b1v)

    drain_all()
    flush("phaseB")
    sB.close()
    st.close()
    return nc


def _host_inputs(L0_attn_x, L0_attn_Wq, L0_attn_Wk, L0_attn_Wv, L0_attn_Wo,
                 L0_attn_betas, L0_ffn_w_gate_up, L0_ffn_w_proj):
    x = np.asarray(L0_attn_x)
    Wq = np.asarray(L0_attn_Wq); Wk = np.asarray(L0_attn_Wk)
    Wv = np.asarray(L0_attn_Wv); Wo = np.asarray(L0_attn_Wo)
    betas = np.asarray(L0_attn_betas).reshape(16, 128)
    Wgu = np.asarray(L0_ffn_w_gate_up); Wp = np.asarray(L0_ffn_w_proj)

    u = np.arange(SEG)
    s_q = 16 * (u % 32) + u // 32
    t = np.arange(SEG)[:, None]
    maskT = (t <= s_q[None, :]).astype(bf16)
    idn = np.eye(128, dtype=np.float32)

    in_maps = []
    for c in range(NCORES):
        b, g = c // 4, c % 4
        xb = np.ascontiguousarray(x[b])
        xres = np.ascontiguousarray(
            xb.reshape(NSEG, 4, 128, D)[:, g].reshape(1024, D))
        wgu_sl = np.ascontiguousarray(
            np.concatenate([Wgu[1024 * c:1024 * (c + 1)],
                            Wgu[HFF + 1024 * c:HFF + 1024 * (c + 1)]], 0))
        wp_sl = np.ascontiguousarray(Wp[:, 1024 * c:1024 * (c + 1)])
        in_maps.append(dict(
            x_b=xb,
            x_res=xres,
            wq_d=np.ascontiguousarray(Wq[:, 512 * g:512 * (g + 1)]),
            wk_d=np.ascontiguousarray(Wk[:, 512 * g:512 * (g + 1)]),
            wv_d=np.ascontiguousarray(Wv[:, 512 * g:512 * (g + 1)]),
            wo_d=Wo,
            betas_d=np.ascontiguousarray(betas[4 * g:4 * (g + 1)].reshape(1, 512)),
            wgu_d=wgu_sl,
            wp_d=wp_sl,
            mask_d=maskT,
            idn_d=idn.astype(bf16),
            idnf_d=idn,
        ))
    return in_maps


def kernel(L0_attn_x, L0_attn_Wq, L0_attn_Wk, L0_attn_Wv, L0_attn_Wo,
           L0_attn_betas, L0_ffn_w_gate_up, L0_ffn_w_proj):
    in_maps = _host_inputs(L0_attn_x, L0_attn_Wq, L0_attn_Wk, L0_attn_Wv,
                           L0_attn_Wo, L0_attn_betas, L0_ffn_w_gate_up,
                           L0_ffn_w_proj)
    nc = _build(debug=0)
    res = run_bass_kernel_spmd(nc, in_maps, core_ids=list(range(NCORES)))
    out = np.empty((2, SEQ, D), dtype=np.float32)
    for c in range(NCORES):
        b, g = c // 4, c % 4
        rows = np.asarray(res.results[c]["o_rows"]).reshape(NSEG, 128, D)
        out.reshape(2, NSEG, 4, 128, D)[b, :, g] = rows
    return out
